# version 09
# baseline (speedup 1.0000x reference)
import numpy as np
from contextlib import ExitStack
import concourse.bass as bass
import concourse.mybir as mybir
from concourse.bass_utils import run_bass_kernel_spmd

F32 = mybir.dt.float32
BF16 = mybir.dt.bfloat16
F16 = mybir.dt.float16
ALU = mybir.AluOpType
AF = mybir.ActivationFunctionType
AX = mybir.AxisListType

NCORES = 8
D = 1024
NCH = 8
T = 4096
S = 2048
NSEQ = 2
NH = 16
HD = 64
I3 = 3072
ICH = 24
QL = 256
TT = 512
NTT = T // TT
NTC = S // TT
MAGIC = 12582912.0
C1V = float(2 ** 20 + 1)
LNEPS = 1e-5
RMSEPS = 1e-6
QMAX = 32767.0
SM_SLACK = 40.0
HALF_LN_D = 0.5 * float(np.log(1024.0))

ENGINES = ("sync", "vector", "scalar", "tensor", "gpsimd")
V, SC, TE, SY, GP = "vector", "scalar", "tensor", "sync", "gpsimd"


class Prog:
    def __init__(self, nc, ctx):
        self.nc = nc
        self.q = {e: [] for e in ENGINES}
        self.sem = {}
        self.cnt = {}
        self._ctx = ctx
        for e in ENGINES:
            self.new_sem("m_" + e)
        for s in ("ld", "ldb", "st", "st2", "cc", "s_prm", "s_cmn", "s_w0",
                  "s_w1", "s_xa", "s_att", "s_vst0", "s_vst1", "s_qn", "s_s0",
                  "s_xn"):
            self.new_sem(s)

    def new_sem(self, name):
        if name in self.sem:
            return name
        self.sem[name] = self._ctx.enter_context(self.nc.semaphore(name))
        self.cnt[name] = 0
        return name

    def megabarrier(self):
        for e in ENGINES:
            for sname in list(self.cnt):
                if not sname.startswith("m_") and self.cnt[sname] > 0:
                    self.wait(e, sname, self.cnt[sname])
        self.barrier()

    def wait(self, eng, sem, val):
        if val <= 0:
            return
        h = self.sem[sem]
        self.q[eng].append(lambda e, h=h, v=val: e.wait_ge(h, v))

    def op(self, eng, fn, inc=None, k=1):
        if inc is not None:
            h = self.sem[inc]
            self.cnt[inc] += k
            self.q[eng].append(lambda e, h=h, k=k, fn=fn: fn(e).then_inc(h, k))
            return self.cnt[inc]
        self.q[eng].append(lambda e, fn=fn: fn(e))
        return None

    def tok(self, eng):
        self.wait(eng, "m_" + eng, self.cnt["m_" + eng])
        if eng == V:
            return self.op(eng, lambda e: e.memset(self.scratch_v[0:1, 0:1], 0.0),
                           inc="m_" + eng)
        if eng == GP:
            return self.op(eng, lambda e: e.memset(self.scratch_g[0:1, 0:1], 0.0),
                           inc="m_" + eng)
        if eng == SC:
            return self.op(eng, lambda e: e.activation(
                self.scratch_s[0:1, 0:1], self.scratch_s[0:1, 1:2], AF.Copy,
                scale=0.0), inc="m_" + eng)
        if eng == TE and self.te_dummy is not None:
            td = self.te_dummy
            return self.op(eng, lambda e, td=td: e.matmul(
                td[0], td[1], td[1], start=True, stop=True), inc="m_" + eng)
        return self.op(eng, lambda e: e.nop(), inc="m_" + eng)

    def fence(self, eng):
        t = self.tok(eng)
        self.wait(eng, "m_" + eng, t)

    def barrier(self):
        toks = {e: self.tok(e) for e in ENGINES}
        for e in ENGINES:
            for o in ENGINES:
                if o != e:
                    self.wait(e, "m_" + o, toks[o])

    def replay(self, block):
        for e in ENGINES:
            def mk(e=e):
                def f(eng):
                    for fn in self.q[e]:
                        fn(eng)
                return f
            getattr(block, e)(mk())


class G:
    def __init__(self, name, fused=False, ar32=0, arbf=0, ar16=0):
        self.nc = bass.Bass("TRN2", target_bir_lowering=False)
        self.ctx = ExitStack()
        self.p = Prog(self.nc, self.ctx)
        self.p.scratch_v = self.ctx.enter_context(
            self.nc.sbuf_tensor("_scr_v", [1, 2], F32))
        self.p.scratch_g = self.ctx.enter_context(
            self.nc.sbuf_tensor("_scr_g", [1, 2], F32))
        self.p.scratch_s = self.ctx.enter_context(
            self.nc.sbuf_tensor("_scr_s", [1, 2], F32))
        self.p.te_dummy = None
        self.h = {}
        self.name = name
        self.fused = fused
        self.ar = None
        self.aroff = 0
        if fused:
            self.ar = self.ctx.enter_context(
                self.nc.sbuf_tensor("_arena", [128, ar32], F32))

    def pool_reset(self):
        self.aroff = 0

    def din(self, name, shape, dtype=F32):
        a = self.nc.dram_tensor(name, list(shape), dtype, kind="ExternalInput").ap()
        self.h[name] = a
        return a

    def dout(self, name, shape, dtype=F32):
        a = self.nc.dram_tensor(name, list(shape), dtype, kind="ExternalOutput").ap()
        self.h[name] = a
        return a

    def dint(self, name, shape, dtype=F32):
        a = self.nc.dram_tensor(name, list(shape), dtype, kind="Internal").ap()
        self.h[name] = a
        return a

    def sb(self, name, shape, dtype=F32, pers=False):
        if not self.fused or pers:
            t = self.ctx.enter_context(
                self.nc.sbuf_tensor(name + ("_p" if self.fused and pers else ""),
                                    list(shape), dtype))
            self.h[name] = t
            return t
        assert len(shape) == 2
        esz = {id(F32): 4, id(BF16): 2, id(F16): 2}[id(dtype)]
        n32 = (shape[1] * esz + 3) // 4
        off = self.aroff
        ncap = self.ar.shape[1]
        assert off + n32 <= ncap, (
            f"arena overflow: {name} needs {n32} f32 at {off}, cap {ncap}")
        if esz == 4:
            v = self.ar[0:shape[0], off:off + shape[1]]
        else:
            v = self.ar[:, off:off + n32].bitcast(dtype)[0:shape[0], 0:shape[1]]
            self.h[name + "_f32raw"] = self.ar[:, off:off + n32]
        self.aroff = (off + n32 + 7) // 8 * 8
        self.h[name] = v
        return v

    def din_once(self, name, shape, dtype=F32):
        if name in self.h:
            return self.h[name]
        return self.din(name, shape, dtype)

    def dio_out(self, name, shape, dtype=F32):
        return (self.dint if self.fused else self.dout)(name, shape, dtype)

    def dio_in(self, name, shape, dtype=F32):
        if self.fused:
            return self.h[name]
        return self.din(name, shape, dtype)

    def ps(self, name, shape, dtype=F32):
        t = self.ctx.enter_context(self.nc.psum_tensor(name, list(shape), dtype))
        self.h[name] = t
        return t

    def finish(self):
        p = self.p
        for sem in list(p.cnt):
            if not sem.startswith("m_") and p.cnt[sem] > 0:
                p.wait(GP, sem, p.cnt[sem])
        block = self.ctx.enter_context(self.nc.Block())
        p.replay(block)
        self.ctx.close()
        return self.nc



def load_wT_bf16(g, src, dst, jchunks, cols):
    p, h = g.p, g.h
    st = h["wstage"]
    half = st.shape[1] // 2
    assert cols <= half
    tok = 0
    toks = []
    prev = getattr(g, "_wstage_prev", {0: 0, 1: 0})
    for jc in range(jchunks):
        off = 0 if jc % 2 == 0 else half
        sem = "s_w0" if jc % 2 == 0 else "s_w1"
        if jc >= 2:
            p.wait(SY, "m_vector", toks[jc - 2])
        elif prev[jc % 2]:
            p.wait(SY, "m_vector", prev[jc % 2])
        ldt = p.op(SY, lambda e, jc=jc, off=off: e.dma_start(
            out=st[:, off:off + cols], in_=src[jc * 128:(jc + 1) * 128, :]),
            inc=sem, k=16)
        p.wait(V, sem, ldt)
        tok = p.op(V, lambda e, jc=jc, off=off: e.tensor_copy(
            dst[:, jc * cols:(jc + 1) * cols], st[:, off:off + cols]),
            inc="m_vector")
        toks.append(tok)
    g._wstage_prev = {0: toks[-2] if jchunks >= 2 else toks[-1],
                      1: toks[-1] if jchunks >= 2 else prev[1]}
    return tok


def layernorm_tile(g, xa, xn, xnb, psA, psB, psC=None):
    p, h = g.p, g.h
    sq = h["sq"]
    rows = h["rows"]
    r0 = rows[0:1, 0:TT]
    r1 = rows[0:1, TT:2 * TT]
    r2 = rows[0:1, 2 * TT:3 * TT]
    for c in range(NCH):
        p.op(SC, lambda e, c=c, sq=sq: e.activation(
            sq[:, c * TT:(c + 1) * TT], xa[:, c * TT:(c + 1) * TT], AF.Square))
    sqtok = p.tok(SC)
    p.wait(TE, "m_scalar", sqtok)
    for c in range(NCH):
        p.op(TE, lambda e, c=c: e.matmul(psA[0:1, 0:TT], h["ones_col"][:, 0:1],
                                         xa[:, c * TT:(c + 1) * TT],
                                         start=(c == 0), stop=(c == NCH - 1)))
    for c in range(NCH):
        p.op(TE, lambda e, c=c, sq=sq: e.matmul(
            psC[0:1, 0:TT], h["ones_col"][:, 0:1], sq[:, c * TT:(c + 1) * TT],
            start=(c == 0), stop=(c == NCH - 1)),
             inc=("m_tensor" if c == NCH - 1 else None))
    stat_tok = p.cnt["m_tensor"]
    p.wait(V, "m_tensor", stat_tok)
    p.op(V, lambda e: e.tensor_scalar(r0, psA[0:1, 0:TT], 1.0 / D, None, ALU.mult))
    p.op(V, lambda e: e.tensor_scalar(r1, psC[0:1, 0:TT], 1.0 / D, None, ALU.mult))
    p.fence(V)
    p.op(V, lambda e: e.tensor_mul(r2, r0, r0))
    p.fence(V)
    vtok = p.op(V, lambda e: e.tensor_sub(r1, r1, r2), inc="m_vector")
    p.wait(SC, "m_vector", vtok)
    p.op(SC, lambda e: e.activation(r2, r1, AF.Ln, bias=h["cst"][0:1, 0:1]))
    p.fence(SC)
    rtok = p.op(SC, lambda e: e.activation(r1, r2, AF.Exp, scale=-0.5),
                inc="m_scalar")
    p.wait(V, "m_scalar", rtok)
    mrtok = p.op(V, lambda e: e.tensor_mul(r2, r0, r1), inc="m_vector")
    p.wait(TE, "m_vector", mrtok)
    p.op(TE, lambda e: e.matmul(psB[:, 0:TT], h["ones_row"][0:1, :], r1,
                                start=True, stop=True))
    bctok = p.op(TE, lambda e: e.matmul(psB[:, TT:2 * TT], h["ones_row"][0:1, :],
                                        r2, start=True, stop=True),
                 inc="m_tensor")
    p.wait(V, "m_tensor", bctok)
    for c in range(NCH):
        p.op(V, lambda e, c=c: e.tensor_mul(xn[:, c * TT:(c + 1) * TT],
                                            xa[:, c * TT:(c + 1) * TT],
                                            psB[:, 0:TT]))
    p.fence(V)
    for c in range(NCH):
        p.op(V, lambda e, c=c: e.tensor_sub(xn[:, c * TT:(c + 1) * TT],
                                            xn[:, c * TT:(c + 1) * TT],
                                            psB[:, TT:2 * TT]))
    xn_tok = p.tok(V)
    p.wait(SC, "m_vector", xn_tok)
    for c in range(NCH):
        p.op(SC, lambda e, c=c: e.activation(xnb[:, c * TT:(c + 1) * TT],
                                             xn[:, c * TT:(c + 1) * TT], AF.Copy))
    xnb_tok = p.tok(SC)
    return xn_tok, xnb_tok


def bcast_scalar_col(g, table, row, col, dstcol, scale=1.0, psname="psBC"):
    p, h = g.p, g.h
    tk = p.op(TE, lambda e: e.matmul(h[psname][:, 0:1], h["ones_row"][0:1, :],
                                     table[row:row + 1, col:col + 1],
                                     start=True, stop=True), inc="m_tensor")
    p.wait(SC, "m_tensor", tk)
    return p.op(SC, lambda e: e.activation(dstcol[:, 0:1], h[psname][:, 0:1],
                                           AF.Copy, scale=scale), inc="m_scalar")



def emit_g1(g, bg_factory=None):
    p, h = g.p, g.h
    ps = common_setup(g)
    g.din_once("x", [D, T])
    for n in ("wqT", "wkT", "wvT", "woT"):
        g.din(n, [D, D])
    for n in ("bq", "bk", "bv", "bo", "gq", "gk"):
        g.din(n, [128, NCH])
    g.dio_out("xn0", [D, T])
    g.dio_out("s0", [D, T])
    q0 = g.dint("q0", [D, T], BF16)
    shdram = g.dint("shd", [2, 16])
    k0 = g.dint("k0", [D, T], BF16)
    v0 = g.dint("v0", [T, NH * 65], BF16)
    bvr = g.dint("bvr", [8, 128])

    g.sb("sq", [128, NCH * TT])
    for n in ("wqTb", "wkTb", "wvTb"):
        g.sb(n, [128, NCH * D], BF16)
    h["woTb"] = h["wvTb"]
    h["xa1"] = h["xa0"]
    g.sb("tmp8", [8, 128])
    g.sb("qf", [128, NCH * TT])
    g.sb("qnb", [128, NCH * TT], BF16)
    g.sb("ntabq", [2, 16]); g.sb("ntabk", [2, 16])
    g.sb("shift", [2, 16])
    g.sb("bvbc", [128, D])
    g.sb("vst0", [128, NH * 65], BF16); g.sb("vst1", [128, NH * 65], BF16)
    g.sb("qh", [64, S], BF16); g.sb("kh", [64, S], BF16)
    g.sb("vh", [128, 16 * 65], BF16)
    g.sb("et0", [128, TT], BF16); g.sb("et1", [128, TT], BF16)
    g.sb("et2", [128, TT], BF16)
    g.sb("ytm", [128, 64], BF16)
    g.sb("yfm", [128, NCH * S], BF16)
    g.sb("s0f", [128, TT])
    g.sb("shcol", [128, 1])
    g.sb("rcol", [128, 1])
    for n in ("bqc", "bkc", "bvc", "boc", "gqc", "gkc"):
        g.sb(n, [128, NCH])

    psQ0, psQ1, psA = ps[0], ps[1], ps[2]
    psS0, psS1, psO = ps[3], ps[4], ps[5]
    psB = ps["B"]
    psTb = psB[:, 0:64].bitcast(BF16)

    for n_in, n_c in (("bq", "bqc"), ("bk", "bkc"), ("bv", "bvc"), ("bo", "boc"),
                      ("gq", "gqc"), ("gk", "gkc")):
        p.op(SY, lambda e, a=h[n_in], b=h[n_c]: e.dma_start(out=b[:], in_=a),
             inc="s_prm", k=16)
    p.wait(V, "s_prm", p.cnt["s_prm"])
    p.op(V, lambda e: e.memset(h["ntabq"][:], 0.0))
    p.op(V, lambda e: e.memset(h["ntabk"][:], 0.0))
    p.op(V, lambda e: e.memset(h["mxc"][:], 0.0))
    p.op(V, lambda e: e.memset(h["vst0"][:], 0.0))
    p.op(V, lambda e: e.memset(h["vst1"][:], 0.0))
    p.fence(V)
    for b in ("vst0", "vst1"):
        p.op(V, lambda e, b=b: e.memset(
            h[b][:].rearrange("p (nh c) -> p nh c", c=65)[:, :, 0:1], 1.0))
    cst = p.tok(V)
    p.wait(TE, "m_vector", cst)
    p.wait(SC, "m_vector", cst)

    tk = p.op(TE, lambda e: e.transpose(psA[0:8, 0:128], h["bvc"][:],
                                        h["identf"][:]), inc="m_tensor")
    p.wait(V, "m_tensor", tk)
    vk = p.op(V, lambda e: e.tensor_copy(h["tmp8"][:], psA[0:8, 0:128]),
              inc="m_vector")
    p.wait(GP, "m_vector", vk)
    s1t = p.op(GP, lambda e: e.dma_start(out=bvr, in_=h["tmp8"][:]), inc="st2", k=16)
    p.wait(SY, "st2", s1t)
    bvt = p.op(SY, lambda e: e.dma_start(
        out=h["bvbc"][:],
        in_=bvr.rearrange("a b -> () (a b)").broadcast_to([128, D])), inc="ldb", k=16)
    p.wait(V, "ldb", bvt)

    for n in ("wqT", "wkT", "wvT"):
        load_wT_bf16(g, h[n], h[n + "b"], NCH, D)
    wtok = p.cnt["m_vector"]
    p.wait(TE, "m_vector", wtok)

    xn_toks = [0] * NTT
    sq_toks = [0] * NTT
    cons_toks = [0] * NTT
    qk_evtoks = []
    vst_last = {0: 0, 1: 0}
    prev_vev3 = 0
    prev_xn_store = 0
    prev_qn_tok = 0
    prev_qn_store = 0
    prev_sq_tok = 0

    for tt in range(NTT):
        xa = h["xa0"] if tt % 2 == 0 else h["xa1"]
        seq = tt // 4
        if tt >= 1:
            p.wait(SY, "m_vector", xn_toks[tt - 1])
            p.wait(SY, "m_scalar", sq_toks[tt - 1])
        for c in range(NCH):
            p.op(SY, lambda e, c=c, tt=tt, xa=xa: e.dma_start(
                out=xa[:, c * TT:(c + 1) * TT],
                in_=h["x"][c * 128:(c + 1) * 128, tt * TT:(tt + 1) * TT]),
                inc="s_xa", k=16)
        xld = p.cnt["s_xa"]
        p.wait(SC, "s_xa", xld)
        p.wait(V, "s_xa", xld)
        if tt >= 1:
            p.wait(V, "m_tensor", cons_toks[tt - 1])
            p.wait(SC, "m_tensor", cons_toks[tt - 1])
            p.wait(TE, "m_vector", prev_vev3)
            p.wait(V, "s_xn", prev_xn_store)
        xn_tok, xnb_tok = layernorm_tile(g, xa, h["xn"], h["xnb"], psA, psB,
                                         psC=psS0)
        xn_toks[tt] = xn_tok
        sq_toks[tt] = xnb_tok
        p.wait(GP, "m_vector", xn_tok)
        for c in range(NCH):
            p.op(GP, lambda e, c=c, tt=tt: e.dma_start(
                out=h["xn0"][c * 128:(c + 1) * 128, tt * TT:(tt + 1) * TT],
                in_=h["xn"][:, c * TT:(c + 1) * TT]), inc="s_xn", k=16)
        prev_xn_store = p.cnt["s_xn"]
        for c in range(NCH):
            p.op(V, lambda e, c=c: e.tensor_reduce(
                h["tmpcol"][:, 0:1], h["xn"][:, c * TT:(c + 1) * TT], AX.X,
                ALU.max, apply_absolute_value=True))
            p.fence(V)
            p.op(V, lambda e: e.tensor_max(h["mxc"][:, 0:1], h["mxc"][:, 0:1],
                                           h["tmpcol"][:, 0:1]))
            p.fence(V)

        p.wait(TE, "m_scalar", xnb_tok)
        evtoks = qk_evtoks
        for (wb, bc, gc, ntab, dram) in (("wqTb", "bqc", "gqc", "ntabq", q0),
                                         ("wkTb", "bkc", "gkc", "ntabk", k0)):
            for ic in range(NCH):
                psb = psQ0 if ic % 2 == 0 else psQ1
                if len(evtoks) >= 2:
                    p.wait(TE, "m_scalar", evtoks[-2])
                for jc in range(NCH):
                    p.op(TE, lambda e, ic=ic, jc=jc, wb=wb, psb=psb: e.matmul(
                        psb[:, 0:TT],
                        h[wb][:, jc * D + ic * 128: jc * D + (ic + 1) * 128],
                        h["xnb"][:, jc * TT:(jc + 1) * TT],
                        start=(jc == 0), stop=(jc == NCH - 1)),
                        inc=("m_tensor" if jc == NCH - 1 else None))
                mt = p.cnt["m_tensor"]
                p.wait(SC, "m_tensor", mt)
                if prev_qn_tok:
                    p.wait(SC, "m_vector", prev_qn_tok)
                ev = p.op(SC, lambda e, ic=ic, psb=psb, bc=bc: e.activation(
                    h["qf"][:, ic * TT:(ic + 1) * TT], psb[:, 0:TT], AF.Identity,
                    bias=h[bc][:, ic:ic + 1]), inc="m_scalar")
                evtoks.append(ev)
            qf_tok = evtoks[-1]
            p.wait(V, "m_scalar", qf_tok)
            for ic in range(NCH):
                p.op(V, lambda e, ic=ic, sq=h["sq"]: e.tensor_mul(
                    sq[:, ic * TT:(ic + 1) * TT],
                    h["qf"][:, ic * TT:(ic + 1) * TT],
                    h["qf"][:, ic * TT:(ic + 1) * TT]))
            sq_all = p.tok(V)
            p.wait(TE, "m_vector", sq_all)
            for ic in range(NCH):
                p.op(TE, lambda e, ic=ic, sq=h["sq"]: e.matmul(
                    psA[0:1, 0:TT], h["ones_col"][:, 0:1],
                    sq[:, ic * TT:(ic + 1) * TT],
                    start=(ic == 0), stop=(ic == NCH - 1)),
                    inc=("m_tensor" if ic == NCH - 1 else None))
            ssum = p.cnt["m_tensor"]
            p.wait(SC, "m_tensor", ssum)
            p.op(SC, lambda e: e.activation(h["rows"][0:1, 2 * TT:3 * TT],
                                            psA[0:1, 0:TT],
                                            AF.Ln, bias=h["cst"][0:1, 1:2]))
            p.fence(SC)
            rt = p.op(SC, lambda e: e.activation(h["rows"][0:1, TT:2 * TT],
                                                 h["rows"][0:1, 2 * TT:3 * TT],
                                                 AF.Exp,
                                                 scale=-0.5, bias=h["cst"][0:1, 2:3]),
                      inc="m_scalar")
            p.wait(TE, "m_scalar", rt)
            if prev_qn_tok:
                p.wait(TE, "m_vector", prev_qn_tok)
            bt = p.op(TE, lambda e: e.matmul(psB[:, 0:TT], h["ones_row"][0:1, :],
                                             h["rows"][0:1, TT:2 * TT], start=True,
                                             stop=True), inc="m_tensor")
            p.wait(V, "m_tensor", bt)
            if prev_qn_store:
                p.wait(V, "s_qn", prev_qn_store)
            if prev_sq_tok:
                p.wait(V, "m_scalar", prev_sq_tok)
            for ic in range(NCH):
                p.op(V, lambda e, ic=ic, gc=gc: e.scalar_tensor_tensor(
                    h["qnb"][:, ic * TT:(ic + 1) * TT],
                    h["qf"][:, ic * TT:(ic + 1) * TT], h[gc][:, ic:ic + 1],
                    psB[:, 0:TT], ALU.mult, ALU.mult))
            qn_tok = p.tok(V)
            prev_qn_tok = qn_tok
            p.wait(GP, "m_vector", qn_tok)
            for ic in range(NCH):
                p.op(GP, lambda e, ic=ic, tt=tt, dram=dram: e.dma_start(
                    out=dram[ic * 128:(ic + 1) * 128, tt * TT:(tt + 1) * TT],
                    in_=h["qnb"][:, ic * TT:(ic + 1) * TT]), inc="s_qn", k=16)
            prev_qn_store = p.cnt["s_qn"]
            p.wait(SC, "m_vector", qn_tok)
            for ic in range(NCH):
                sqt = p.op(SC, lambda e, ic=ic, sq=h["sq"]: e.activation(
                    sq[:, ic * TT:(ic + 1) * TT],
                    h["qnb"][:, ic * TT:(ic + 1) * TT], AF.Square),
                    inc="m_scalar")
                p.wait(TE, "m_scalar", sqt)
                pss = psS0 if ic % 2 == 0 else psS1
                hm = p.op(TE, lambda e, ic=ic, pss=pss, sq=h["sq"]: e.matmul(
                    pss[0:2, 0:TT], h["hmask"][:, 0:2],
                    sq[:, ic * TT:(ic + 1) * TT], start=True, stop=True),
                    inc="m_tensor")
                p.wait(V, "m_tensor", hm)
                p.op(V, lambda e, pss=pss: e.tensor_reduce(
                    h["tmp2"][0:2, 0:1], pss[0:2, 0:TT], AX.X, ALU.max))
                p.fence(V)
                vred = p.op(V, lambda e, ic=ic, ntab=ntab, seq=seq: e.tensor_max(
                    h[ntab][0:2, (seq * NCH + ic):(seq * NCH + ic) + 1],
                    h[ntab][0:2, (seq * NCH + ic):(seq * NCH + ic) + 1],
                    h["tmp2"][0:2, 0:1]), inc="m_vector")
                p.fence(V)
                p.wait(TE, "m_vector", vred)
            prev_sq_tok = p.cnt["m_scalar"]

        vstoks = [0] * 4
        vev_toks = [0] * 4
        for sub in range(4):
            vst = h["vst0"] if sub % 2 == 0 else h["vst1"]
            for ihalf in range(2):
                psb = psS0 if ihalf == 0 else psS1
                if sub >= 1:
                    p.wait(TE, "m_vector", vev_toks[sub - 1])
                for jc in range(NCH):
                    p.op(TE, lambda e, sub=sub, ihalf=ihalf, jc=jc, psb=psb:
                         e.matmul(psb[:, 0:TT],
                                  h["xnb"][:, jc * TT + sub * 128:
                                           jc * TT + (sub + 1) * 128],
                                  h["wvTb"][:, jc * D + ihalf * TT:
                                            jc * D + (ihalf + 1) * TT],
                                  start=(jc == 0), stop=(jc == NCH - 1)),
                         inc=("m_tensor" if jc == NCH - 1 else None))
            vmt = p.cnt["m_tensor"]
            p.wait(V, "m_tensor", vmt)
            if vst_last[sub % 2]:
                p.wait(V, "s_vst0" if sub % 2 == 0 else "s_vst1",
                       vst_last[sub % 2])
            for ihalf in range(2):
                psb = psS0 if ihalf == 0 else psS1
                p.op(V, lambda e, ihalf=ihalf, psb=psb, vst=vst:
                     e.scalar_tensor_tensor(
                         vst[:].rearrange("p (nh c) -> p nh c", c=65)
                         [:, ihalf * 8:(ihalf + 1) * 8, 1:65],
                         psb[:, 0:TT].rearrange("p (nh c) -> p nh c", c=64),
                         1.0,
                         h["bvbc"][:, ihalf * TT:(ihalf + 1) * TT].rearrange(
                             "p (nh c) -> p nh c", c=64),
                         ALU.mult, ALU.add))
            vtok = p.tok(V)
            vev_toks[sub] = vtok
            p.wait(GP, "m_vector", vtok)
            vstoks[sub] = p.op(GP, lambda e, tt=tt, sub=sub, vst=vst: e.dma_start(
                out=v0[tt * TT + sub * 128: tt * TT + (sub + 1) * 128, :],
                in_=vst[:]), inc="s_vst0" if sub % 2 == 0 else "s_vst1", k=16)
            vst_last[sub % 2] = vstoks[sub]
        cons_toks[tt] = p.cnt["m_tensor"]
        prev_vev3 = vev_toks[3]

    p.op(V, lambda e: e.tensor_mul(h["shift"][0:2, 0:16], h["ntabq"][0:2, 0:16],
                                   h["ntabk"][0:2, 0:16]))
    sht = p.tok(V)
    p.wait(SC, "m_vector", sht)
    p.op(SC, lambda e: e.activation(h["shift"][0:2, 0:16], h["shift"][0:2, 0:16],
                                    AF.Ln, bias=h["cst"][0:2, 3:4]))
    p.fence(SC)
    sh2 = p.op(SC, lambda e: e.activation(h["shift"][0:2, 0:16],
                                          h["shift"][0:2, 0:16], AF.Exp,
                                          scale=0.5), inc="m_scalar")
    p.wait(V, "m_scalar", sh2)
    p.op(V, lambda e: e.tensor_scalar(h["shift"][0:2, 0:16],
                                      h["shift"][0:2, 0:16],
                                      0.125, -SM_SLACK, ALU.mult, ALU.add))
    p.fence(V)
    p.op(V, lambda e: e.tensor_scalar(h["shift"][0:2, 0:16],
                                      h["shift"][0:2, 0:16], 0.0, None, ALU.max))
    p.fence(V)
    shneg = p.op(V, lambda e: e.tensor_scalar(h["shift"][0:2, 0:16],
                                              h["shift"][0:2, 0:16], -1.0, None,
                                              ALU.mult), inc="m_vector")
    p.wait(GP, "m_vector", shneg)
    p.op(GP, lambda e: e.dma_start(out=shdram, in_=h["shift"][0:2, 0:16]),
         inc="st2", k=16)
    p.barrier()
    for sem in ("st", "st2", "s_qn", "s_vst0", "s_vst1", "s_xn"):
        p.wait(SY, sem, p.cnt[sem])
    wo_tok = load_wT_bf16(g, h["woT"], h["woTb"], NCH, D)
    p.wait(TE, "m_vector", wo_tok)

    bg_pairs = list(bg_factory(g)) if bg_factory else []
    bg_loaded = []
    for _ in range(2):
        if bg_pairs:
            pr = bg_pairs.pop(0)
            pr[0]()
            bg_loaded.append(pr)

    out_banks = [psO, psA, psQ0, psQ1]
    psS2 = psB[:, 512:1024]
    sc_last = {0: 0, 1: 0, 2: 0}
    et_last = {0: 0, 1: 0, 2: 0}
    wo_last_ev = 0
    wo_evs = []
    wo_sts = 0
    for s in range(NSEQ):
        for hd in range(NH):
            ic, halfp = hd // 2, hd % 2
            if s > 0 and hd == 0:
                p.wait(TE, "m_scalar", wo_last_ev)
            if s + hd > 0:
                p.wait(SY, "m_tensor", p.cnt["m_tensor"])
            p.op(SY, lambda e, s=s, hd=hd: e.dma_start(
                out=h["qh"][:], in_=q0[hd * 64:(hd + 1) * 64, s * S:(s + 1) * S]),
                inc="s_att", k=16)
            p.op(SY, lambda e, s=s, hd=hd: e.dma_start(
                out=h["kh"][:], in_=k0[hd * 64:(hd + 1) * 64, s * S:(s + 1) * S]),
                inc="s_att", k=16)
            p.op(SY, lambda e, s=s, hd=hd: e.dma_start(
                out=h["vh"][:].rearrange("p (kt c) -> p kt c", c=65),
                in_=v0[s * S:(s + 1) * S, hd * 65:(hd + 1) * 65]
                .rearrange("(kt p) c -> p kt c", p=128)), inc="s_att", k=16)
            p.op(SY, lambda e, halfp=halfp, s=s, ic=ic: e.dma_start(
                out=h["shcol"][:, 0:1],
                in_=shdram[halfp:halfp + 1, s * NCH + ic: s * NCH + ic + 1]
                .broadcast_to([128, 1])), inc="s_att", k=16)
            hld = p.cnt["s_att"]
            p.wait(TE, "s_att", hld)
            p.wait(SC, "s_att", hld)
            for tc in range(NTC):
                ext_toks = {}
                out_toks = {}
                sc_banks = [psS0, psS1, psS2]
                et_bufs = [h["et0"], h["et1"], h["et2"]]

                def emit_outs(kt):
                    et = et_bufs[kt % 3]
                    p.wait(TE, "m_scalar", ext_toks[kt])
                    for qs in range(4):
                        ot = p.op(TE, lambda e, kt=kt, qs=qs, et=et,
                                  ob=out_banks[qs]: e.matmul(
                            ob[:, 0:65],
                            et[:, qs * 128:(qs + 1) * 128],
                            h["vh"][:, kt * 65:(kt + 1) * 65],
                            start=(kt == 0), stop=(kt == 15)),
                            inc=("m_tensor" if qs == 3 else None))
                    out_toks[kt] = ot

                for kt in range(16):
                    pss = sc_banks[kt % 3]
                    et = et_bufs[kt % 3]
                    if sc_last[kt % 3]:
                        p.wait(TE, "m_scalar", sc_last[kt % 3])
                    smt = p.op(TE, lambda e, kt=kt, tc=tc, pss=pss: e.matmul(
                        pss[:, 0:TT], h["kh"][:, kt * 128:(kt + 1) * 128],
                        h["qh"][:, tc * TT:(tc + 1) * TT], start=True, stop=True),
                        inc="m_tensor")
                    p.wait(SC, "m_tensor", smt)
                    if et_last[kt % 3]:
                        p.wait(SC, "m_tensor", et_last[kt % 3])
                    ext = p.op(SC, lambda e, pss=pss, et=et: e.activation(
                        et[:], pss[:, 0:TT], AF.Exp, bias=h["shcol"][:, 0:1],
                        scale=0.125), inc="m_scalar")
                    ext_toks[kt] = ext
                    sc_last[kt % 3] = ext
                    if kt >= 1:
                        emit_outs(kt - 1)
                        et_last[(kt - 1) % 3] = out_toks[kt - 1]
                emit_outs(15)
                et_last[15 % 3] = out_toks[15]
                p.wait(V, "m_tensor", out_toks[15])
                yts = []
                for qs in range(4):
                    p.op(V, lambda e, qs=qs, ob=out_banks[qs]: e.reciprocal(
                        h["rcol"][:, 0:1], ob[:, 0:1]))
                    p.fence(V)
                    yt = p.op(V, lambda e, qs=qs, ob=out_banks[qs]: e.tensor_scalar(
                        h["ytm"][:, 0:64], ob[:, 1:65],
                        h["rcol"][:, 0:1], None, ALU.mult), inc="m_vector")
                    yts.append(yt)
                    p.wait(TE, "m_vector", yt)
                    tp = p.op(TE, lambda e: e.transpose(
                        psTb[0:64, 0:128], h["ytm"][:, 0:64], h["identb"][:]),
                        inc="m_tensor")
                    p.wait(SC, "m_tensor", tp)
                    cp = p.op(SC, lambda e, tc=tc, qs=qs, ic=ic, halfp=halfp:
                              e.activation(
                                  h["yfm"][halfp * 64:(halfp + 1) * 64,
                                           ic * S + tc * TT + qs * 128:
                                           ic * S + tc * TT + (qs + 1) * 128],
                                  psTb[0:64, 0:128], AF.Copy), inc="m_scalar")
                    p.wait(TE, "m_scalar", cp)
                    p.wait(V, "m_scalar", cp)
                    p.wait(V, "m_vector", yt)
            prev_ld = bg_loaded
            bg_loaded = []
            for pr in prev_ld:
                pr[1]()
            for _ in range(2):
                if bg_pairs:
                    pr = bg_pairs.pop(0)
                    pr[0]()
                    bg_loaded.append(pr)
        yfm_tok = p.cnt["m_scalar"]
        p.wait(TE, "m_scalar", yfm_tok)
        for oc in range(NCH):
            for tc in range(NTC):
                it = oc * NTC + tc
                psb = psQ0 if it % 2 == 0 else psQ1
                if len(wo_evs) >= 2:
                    p.wait(TE, "m_scalar", wo_evs[-2])
                for jc in range(NCH):
                    p.op(TE, lambda e, oc=oc, tc=tc, jc=jc, psb=psb: e.matmul(
                        psb[:, 0:TT],
                        h["woTb"][:, jc * D + oc * 128: jc * D + (oc + 1) * 128],
                        h["yfm"][:, jc * S + tc * TT: jc * S + (tc + 1) * TT],
                        start=(jc == 0), stop=(jc == NCH - 1)),
                        inc=("m_tensor" if jc == NCH - 1 else None))
                wt_ = p.cnt["m_tensor"]
                p.wait(SC, "m_tensor", wt_)
                if wo_evs:
                    p.wait(SC, "s_s0", wo_sts)
                ev = p.op(SC, lambda e, oc=oc, psb=psb: e.activation(
                    h["s0f"][:, 0:TT], psb[:, 0:TT], AF.Identity,
                    bias=h["boc"][:, oc:oc + 1]), inc="m_scalar")
                wo_evs.append(ev)
                wo_last_ev = ev
                p.wait(GP, "m_scalar", ev)
                wo_sts = p.op(GP, lambda e, oc=oc, s=s, tc=tc: e.dma_start(
                    out=h["s0"][oc * 128:(oc + 1) * 128,
                                s * S + tc * TT: s * S + (tc + 1) * TT],
                    in_=h["s0f"][:, 0:TT]), inc="s_s0", k=16)
    for pr in bg_loaded:
        pr[1]()
    while bg_pairs:
        pr = bg_pairs.pop(0)
        pr[0]()
        pr[1]()


def build_g1():
    g = G("g1")
    emit_g1(g)
    return g



def batch_load(g, sem, pairs, waiters=()):
    p = g.p
    for dst, src in pairs:
        p.op(SY, lambda e, dst=dst, src=src: e.dma_start(out=dst, in_=src),
             inc=sem, k=16)
    tok = p.cnt[sem]
    for e in waiters:
        p.wait(e, sem, tok)
    return tok


def load_cast_bf16(g, src, dst, jchunks, cols):
    return load_wT_bf16(g, src, dst, jchunks, cols)


def colmax_to_scal(g, col, idx, psT):
    p, h = g.p, g.h
    t = p.tok(V)
    p.wait(TE, "m_vector", t)
    tk = p.op(TE, lambda e: e.transpose(psT[0:1, 0:128], col[:, 0:1],
                                        h["identf"][:]), inc="m_tensor")
    p.wait(V, "m_tensor", tk)
    p.op(V, lambda e, idx=idx: e.tensor_reduce(
        h["scal"][0:1, idx:idx + 1], psT[0:1, 0:128], AX.X, ALU.max))
    p.fence(V)


def scal_to_cols(g, idxs):
    p, h = g.p, g.h
    t = p.tok(V)
    p.wait(GP, "m_vector", t)
    st = p.op(GP, lambda e: e.dma_start(out=h["scald"], in_=h["scal"][0:1, 0:64]),
              inc="st2", k=16)
    p.wait(SY, "st2", st)
    for si, ci, sc in idxs:
        p.op(SY, lambda e, si=si, ci=ci: e.dma_start(
            out=h["colsr"][:, ci:ci + 1],
            in_=h["scald"][0:1, si:si + 1].broadcast_to([128, 1])),
            inc="s_col", k=16)
    tok = p.cnt["s_col"]
    p.wait(V, "s_col", tok)
    toks = []
    for si, ci, sc in idxs:
        if sc != 1.0:
            p.op(V, lambda e, ci=ci, sc=sc: e.tensor_scalar(
                h["cols"][:, ci:ci + 1], h["colsr"][:, ci:ci + 1], sc, None,
                ALU.mult))
        else:
            p.op(V, lambda e, ci=ci: e.tensor_copy(h["cols"][:, ci:ci + 1],
                                                   h["colsr"][:, ci:ci + 1]))
    return p.tok(V)


def exchange_max(g, n, rg):
    p, h = g.p, g.h
    t = p.tok(V)
    p.wait(GP, "m_vector", t)
    p.op(GP, lambda e, n=n: e.dma_start(out=h[f"cci{n}"], in_=h["ccst"][:]),
         inc="st2", k=16)
    p.wait(GP, "st2", p.cnt["st2"])
    p.op(GP, lambda e, n=n: e.collective_compute(
        "AllReduce", ALU.max, rg, [h[f"cci{n}"].opt()], [h[f"cco{n}"].opt()]),
        inc="cc", k=1)
    p.wait(GP, "cc", p.cnt["cc"])
    ld = p.op(GP, lambda e, n=n: e.dma_start(out=h["ccst"][:], in_=h[f"cco{n}"]),
              inc="s_cc", k=16)
    p.wait(V, "s_cc", ld)
    tok = p.op(V, lambda e: e.tensor_copy(h["scal"][0:1, 32:48],
                                          h["ccst"][0:1, 0:16]), inc="m_vector")
    p.wait(V, "m_vector", tok)
    return tok


def common_setup(g):
    if "xa0" in g.h:
        return g.h["_ps"]
    for n, shp, dt_ in (("xa0", [128, NCH * TT], F32), ("xn", [128, NCH * TT], F32),
                        ("xnb", [128, NCH * TT], BF16),
                        ("rows", [1, 3 * TT], F32), ("ones_col", [128, 1], F32),
                        ("ones_row", [1, 128], F32), ("cst", [128, 4], F32),
                        ("tmpcol", [128, 1], F32), ("tmp2", [2, 2], F32),
                        ("wstage", [128, 2048], F32), ("identf", [128, 128], F32),
                        ("identb", [128, 128], BF16), ("scal", [1, 64], F32),
                        ("cols", [128, 16], F32), ("colsr", [128, 16], F32),
                        ("hmask", [128, 2], F32), ("ccst", [16, 128], F32),
                        ("mxc", [128, 1], F32)):
        g.sb(n, shp, dt_, pers=True)
    psB = g.ctx.enter_context(g.nc.psum_tensor("PSB", [128, 1024], F32))
    PS = [g.ctx.enter_context(g.nc.psum_tensor(f"PS{i}", [128, 512], F32))
          for i in range(6)]
    g.h["PSB"] = psB
    for i, t in enumerate(PS):
        g.h[f"PS{i}"] = t
    ps = {"B": psB, **{i: PS[i] for i in range(6)}}
    g.h["_ps"] = ps
    g.din_once("identf32", [128, 128])
    p = g.p
    ln_consts(g)
    p.op(V, lambda e: e.memset(g.h["scal"][:], 0.0))
    p.op(V, lambda e: e.memset(g.h["cols"][:], 0.0))
    p.op(V, lambda e: e.memset(g.h["colsr"][:], 0.0))
    p.op(V, lambda e: e.memset(g.h["ccst"][:], 0.0))
    p.op(V, lambda e: e.memset(g.h["hmask"][:], 0.0))
    p.fence(V)
    p.op(V, lambda e: e.memset(g.h["hmask"][0:64, 0:1], 1.0))
    p.op(V, lambda e: e.memset(g.h["hmask"][64:128, 1:2], 1.0))
    c0 = p.tok(V)
    p.wait(SY, "m_vector", c0)
    batch_load(g, "s_cmn", [(g.h["identf"][:], g.h["identf32"])], (V,))
    p.op(V, lambda e: e.tensor_copy(g.h["identb"][:], g.h["identf"][:]))
    c1 = p.tok(V)
    p.wait(TE, "m_vector", c1)
    p.wait(SC, "m_vector", c1)
    p.wait(GP, "m_vector", c1)
    p.te_dummy = (PS[5][0:1, 500:501], g.h["ones_row"][0:1, 0:1])
    return ps


def ln_consts(g):
    p, h = g.p, g.h
    p.op(V, lambda e: e.memset(h["cst"][:, 0:1], LNEPS))
    p.op(V, lambda e: e.memset(h["cst"][:, 1:2], RMSEPS * D))
    p.op(V, lambda e: e.memset(h["cst"][:, 2:3], HALF_LN_D))
    p.op(V, lambda e: e.memset(h["cst"][:, 3:4], 1e-30))
    p.op(V, lambda e: e.memset(h["ones_col"][:], 1.0))
    p.op(V, lambda e: e.memset(h["ones_row"][:], 1.0))



def emit_wquant_chunks(g):
    p, h = g.p, g.h
    for sname in ("s_wq0", "s_ws0"):
        p.new_sem(sname)
    g.din("wg", [I3, D]); g.din("wu", [I3, D]); g.din("wd", [D, I3])
    g.din("wgT", [D, I3]); g.din("wuT", [D, I3]); g.din("wdT", [I3, D])
    g.dint("STu", [32, I3])
    g.dint("STg", [32, I3])
    g.dint("STd", [96, D])
    wbigs = [h["sq"][:, 0:I3], h["wqTb_f32raw"][:, 0:I3],
             h["wkTb_f32raw"][:, 0:I3]]
    stgS = h["sq"][:, I3:I3 + 128]
    am = h["qf"][:, I3:I3 + 96]
    sS = h["qf"][:, I3 + 96:I3 + 192]
    state = {"first": True, "v": {0: 0, 1: 0, 2: 0}}

    def mk(wn, stn, ti, cols_, nb, k):
        par = k % 3
        st_ = {}

        def a():
            if state["first"]:
                state["first"] = False
                for s_ in ("m_tensor", "m_scalar", "m_vector"):
                    p.wait(SY, s_, p.cnt[s_])
                p.wait(SY, "s_xn", p.cnt["s_xn"])
                p.wait(SY, "s_qn", p.cnt["s_qn"])
            elif state["v"][par]:
                p.wait(SY, "m_vector", state["v"][par])
            st_["ld"] = p.op(SY, lambda e: e.dma_start(
                out=wbigs[par][:, 0:cols_],
                in_=h[wn][ti * 128:(ti + 1) * 128, :]), inc="s_wq0", k=16)

        def b():
            wbig = wbigs[par]
            p.wait(V, "s_wq0", st_["ld"])
            p.wait(V, "s_ws0", p.cnt["s_ws0"])
            for c0 in range(0, cols_, 1024):
                p.op(V, lambda e, c0=c0: e.tensor_reduce(
                    am[:, c0 // 32:(c0 + 1024) // 32],
                    wbig[:, c0:c0 + 1024].rearrange("p (b c) -> p b c", c=32),
                    AX.X, ALU.max, apply_absolute_value=True))
            p.fence(V)
            p.op(V, lambda e: e.tensor_scalar(sS[:, 0:nb], am[:, 0:nb],
                                              1.0 / 7.0, 1e-12,
                                              ALU.mult, ALU.max))
            p.fence(V)
            for bi in range(4):
                for bj in range(nb // 32):
                    p.op(V, lambda e, bi=bi, bj=bj: e.transpose(
                        stgS[bj * 32:(bj + 1) * 32, bi * 32:(bi + 1) * 32],
                        sS[bi * 32:(bi + 1) * 32, bj * 32:(bj + 1) * 32]))
            vtok = p.tok(V)
            state["v"][par] = vtok
            p.wait(GP, "m_vector", vtok)
            p.op(GP, lambda e: e.dma_start(
                out=stn[0:nb, ti * 128:(ti + 1) * 128], in_=stgS[0:nb, 0:128]),
                inc="s_ws0", k=16)

        return a, b

    pairs = []
    k = 0
    for (wn, stn, ntile, cols_, nb) in (
            ("wu", h["STu"], ICH, D, 32), ("wg", h["STg"], ICH, D, 32),
            ("wd", h["STd"], NCH, I3, 96)):
        for ti in range(ntile):
            pairs.append(mk(wn, stn, ti, cols_, nb, k))
            k += 1
    return pairs


def quant_wload(g, wtn, stn, jchunks, cols_):
    p, h = g.p, g.h
    big0, big2 = h["big0"], h["big2"]
    prev = g.h.setdefault("_qwl_prev", {0: {"v": 0, "g": 0},
                                        1: {"v": 0, "g": 0}})
    mm_snap = p.cnt["m_tensor"]
    for jc in range(jchunks):
        par = jc % 2
        buf = big0 if par == 0 else big2
        wbig = buf[:, 0:cols_]
        sexp = buf[:, I3:I3 + cols_]
        if prev[par]["v"]:
            p.wait(SY, "m_vector", prev[par]["v"])
        if prev[par]["g"]:
            p.wait(SY, "m_gpsimd", prev[par]["g"])
        p.op(SY, lambda e, jc=jc, wbig=wbig: e.dma_start(
            out=wbig, in_=h[wtn][jc * 128:(jc + 1) * 128, :]),
            inc="s_wq0", k=16)
        for bb in range(4):
            p.op(SY, lambda e, jc=jc, bb=bb, sexp=sexp: e.dma_start(
                out=sexp[bb * 32:(bb + 1) * 32, :],
                in_=stn[jc * 4 + bb:jc * 4 + bb + 1, :]
                .broadcast_to([32, cols_])), inc="s_wq0", k=16)
        ld = p.cnt["s_wq0"]
        p.wait(V, "s_wq0", ld)
        p.op(V, lambda e, sexp=sexp: e.reciprocal(sexp, sexp))
        rtok = p.tok(V)
        p.wait(GP, "s_wq0", ld)
        p.wait(GP, "m_vector", rtok)
        p.op(GP, lambda e, wbig=wbig, sexp=sexp: e.tensor_mul(wbig, wbig, sexp))
        g1tok = p.tok(GP)
        p.wait(V, "m_gpsimd", g1tok)
        p.op(V, lambda e, wbig=wbig: e.tensor_scalar(
            wbig, wbig, MAGIC, MAGIC, ALU.add, ALU.subtract))
        p.op(V, lambda e, sexp=sexp: e.reciprocal(sexp, sexp))
        vtok = p.tok(V)
        prev[par]["v"] = vtok
        p.wait(GP, "m_vector", vtok)
        p.wait(GP, "m_tensor", mm_snap)
        p.op(GP, lambda e, jc=jc, wbig=wbig, sexp=sexp: e.tensor_mul(
            h["wffn"][:, jc * cols_:(jc + 1) * cols_], wbig, sexp))
        gtok = p.tok(GP)
        prev[par]["g"] = gtok
    return p.cnt["m_gpsimd"]


def emit_g2(g, rg=None):
    if rg is None:
        rg = [list(range(NCORES))]
    p, h = g.p, g.h
    ps = common_setup(g)
    for sname in ("s_col", "s_cc", "s_wq0", "s_wq1", "s_up0", "s_up1",
                  "s_st0", "s_st1", "s_x1", "s_x1b", "s_r0", "s_r1", "s_ws0", "s_ws1", "s_wf"):
        p.new_sem(sname)
    g.din_once("x", [D, T])
    g.dio_in("xn0", [D, T])
    g.dio_in("s0", [D, T])
    assert "STu" in h, "emit_wquant_chunks must run (in G1) before emit_g2"
    g.dio_out("x1", [D, T])
    up_raw = g.dint("up_raw", [I3, T])
    gate_raw = g.dint("gate_raw", [I3, T])
    t4d = g.dint("t4d", [I3, T])
    upi32 = g.dint("upi32", [I3, T])
    t5d = g.dint("t5d", [I3, T])
    g.dint("scald", [1, 64])
    for n in range(1, 5):
        g.dint(f"cci{n}", [16, 128]); g.dint(f"cco{n}", [16, 128])

    g.sb("tmpg", [128, 1])
    big0 = g.sb("big0", [128, 2 * I3])
    big1 = g.sb("big1", [128, 2 * I3])
    big2 = g.sb("big2", [128, 2 * I3])
    g.sb("wffn", [128, NCH * I3], BF16)
    g.sb("xib", [128, NCH * TT], BF16)
    g.sb("upf0", [128, TT]); g.sb("upf1", [128, TT])
    for n in ("mupc", "mgc", "m4c", "m5c"):
        g.sb(n, [128, 1])

    psT, psU0, psU1 = ps[0], ps[1], ps[2]

    SA = {"up": big1[:, 0:1024], "gt": big1[:, 1024:2048],
          "ui": big1[:, 2048:3072], "gi": big1[:, 3072:4096],
          "e": big1[:, 4096:5120], "d": big1[:, 5120:6144],
          "si": big2[:, 0:1024], "t4": big2[:, 1024:2048],
          "xr": big2[:, 2048:3072], "sr": big2[:, 3072:4096],
          "gr": big2[:, 4096:5120], "df": big2[:, 5120:6144]}
    oib = big0[:].bitcast(BF16)
    h["efb"] = SA["e"]
    h["gfb"] = SA["d"]

    for n in ("mupc", "mgc", "m4c", "m5c"):
        p.op(V, lambda e, n=n: e.memset(h[n][:], 0.0))
    cst0 = p.tok(V)
    p.wait(TE, "m_vector", cst0)
    p.wait(SC, "m_vector", cst0)

    colmax_to_scal(g, h["mxc"], 0, psT)
    p.op(V, lambda e: e.tensor_copy(h["ccst"][0:1, 0:1], h["scal"][0:1, 0:1]))
    exchange_max(g, 1, rg)
    p.op(V, lambda e: e.tensor_scalar(h["scal"][0:1, 1:2], h["scal"][0:1, 32:33],
                                      1.0 / QMAX, 1e-12, ALU.mult, ALU.max))
    p.fence(V)
    p.op(V, lambda e: e.reciprocal(h["scal"][0:1, 2:3], h["scal"][0:1, 1:2]))
    p.fence(V)
    rx_tok = scal_to_cols(g, [(2, 0, 1.0)])

    prev_xi = 0
    evs = []
    sts = {0: 0, 1: 0}
    p.wait(SY, "s_ws0", p.cnt["s_ws0"])
    for (wtn, stn, dstd, mcol) in (("wuT", "STu", up_raw, "mupc"),
                                   ("wgT", "STg", gate_raw, "mgc")):
        rdy = quant_wload(g, wtn, h[stn], NCH, I3)
        p.wait(TE, "m_gpsimd", rdy)
        for tt in range(NTT):
            if prev_xi:
                p.wait(SY, "m_vector", prev_xi)
            batch_load(g, "s_xa",
                       [(h["xa0"][:, c * TT:(c + 1) * TT],
                         h["xn0"][c * 128:(c + 1) * 128, tt * TT:(tt + 1) * TT])
                        for c in range(NCH)], (V,))
            p.wait(V, "m_vector", rx_tok)
            p.wait(V, "m_tensor", p.cnt["m_tensor"])
            for c in range(NCH):
                p.op(V, lambda e, c=c: e.tensor_scalar(
                    h["rows"][0:1, 0:1] if False else h["xa0"][:, c * TT:(c + 1) * TT],
                    h["xa0"][:, c * TT:(c + 1) * TT], h["cols"][:, 0:1], MAGIC,
                    ALU.mult, ALU.add))
            p.fence(V)
            for c in range(NCH):
                p.op(V, lambda e, c=c: e.tensor_scalar(
                    h["xib"][:, c * TT:(c + 1) * TT],
                    h["xa0"][:, c * TT:(c + 1) * TT], MAGIC, None, ALU.subtract))
            xi_tok = p.tok(V)
            prev_xi = xi_tok
            p.wait(TE, "m_vector", xi_tok)
            for uc in range(ICH):
                par = uc % 2
                psb = psU0 if par == 0 else psU1
                upf = h["upf0"] if par == 0 else h["upf1"]
                if len(evs) >= 2:
                    p.wait(TE, "m_scalar", evs[-2])
                for jc in range(NCH):
                    p.op(TE, lambda e, uc=uc, jc=jc, psb=psb: e.matmul(
                        psb[:, 0:TT],
                        h["wffn"][:, jc * I3 + uc * 128: jc * I3 + (uc + 1) * 128],
                        h["xib"][:, jc * TT:(jc + 1) * TT],
                        start=(jc == 0), stop=(jc == NCH - 1)),
                        inc=("m_tensor" if jc == NCH - 1 else None))
                mt = p.cnt["m_tensor"]
                p.wait(SC, "m_tensor", mt)
                if sts[par]:
                    p.wait(SC, "s_st0" if par == 0 else "s_st1", sts[par])
                ev = p.op(SC, lambda e, psb=psb, upf=upf: e.activation(
                    upf[:, 0:TT], psb[:, 0:TT], AF.Copy), inc="m_scalar")
                evs.append(ev)
                p.wait(V, "m_scalar", ev)
                p.op(V, lambda e, upf=upf: e.tensor_reduce(
                    h["tmpcol"][:, 0:1], upf[:, 0:TT], AX.X, ALU.max,
                    apply_absolute_value=True))
                p.fence(V)
                p.op(V, lambda e, mcol=mcol: e.tensor_max(
                    h[mcol][:, 0:1], h[mcol][:, 0:1], h["tmpcol"][:, 0:1]))
                rv = p.tok(V)
                p.wait(V, "m_vector", rv)
                p.wait(GP, "m_vector", rv)
                p.op(GP, lambda e, uc=uc, tt=tt, dstd=dstd, upf=upf: e.dma_start(
                    out=dstd[uc * 128:(uc + 1) * 128, tt * TT:(tt + 1) * TT],
                    in_=upf[:, 0:TT]),
                    inc="s_st0" if par == 0 else "s_st1", k=16)
                sts[par] = p.cnt["s_st0" if par == 0 else "s_st1"]
    colmax_to_scal(g, h["mupc"], 3, psT)
    colmax_to_scal(g, h["mgc"], 4, psT)
    p.op(V, lambda e: e.tensor_copy(h["ccst"][0:1, 0:2], h["scal"][0:1, 3:5]))
    exchange_max(g, 2, rg)
    p.op(V, lambda e: e.tensor_scalar(h["scal"][0:1, 5:6], h["scal"][0:1, 32:33],
                                      1.0 / QMAX, 1e-12, ALU.mult, ALU.max))
    p.op(V, lambda e: e.tensor_scalar(h["scal"][0:1, 7:8], h["scal"][0:1, 33:34],
                                      1.0 / QMAX, 1e-12, ALU.mult, ALU.max))
    p.fence(V)
    p.op(V, lambda e: e.reciprocal(h["scal"][0:1, 6:7], h["scal"][0:1, 5:6]))
    p.op(V, lambda e: e.reciprocal(h["scal"][0:1, 8:9], h["scal"][0:1, 7:8]))
    p.op(V, lambda e: e.tensor_mul(h["scal"][0:1, 9:10], h["scal"][0:1, 7:8],
                                   h["scal"][0:1, 1:2]))
    p.fence(V)
    sc2 = scal_to_cols(g, [(6, 1, 1.0), (8, 2, 1.0), (9, 3, 1.0)])
    p.wait(SC, "m_vector", sc2)
    p.wait(V, "m_vector", sc2)
    p.wait(GP, "m_vector", sc2)

    p.op(V, lambda e: e.memset(h["cols"][:, 14:15], MAGIC))
    p.op(V, lambda e: e.memset(h["cols"][:, 15:16], -MAGIC))
    mcst = p.tok(V)
    p.wait(SC, "m_vector", mcst)
    lastv = {0: 0, 1: 0}
    lastsc = {0: 0, 1: 0}
    lastst = {0: 0, 1: 0}
    lastg = {0: 0, 1: 0}
    for it in range(NTT * ICH):
        tt, uc = divmod(it, ICH)
        par = it % 2
        off = 0 if par == 0 else TT
        sem = "s_up0" if par == 0 else "s_up1"
        stsem = "s_st0" if par == 0 else "s_st1"
        if lastv[par]:
            p.wait(SY, "m_vector", lastv[par])
            p.wait(SY, "m_scalar", lastsc[par])
            p.wait(SY, "m_gpsimd", lastg[par])
        p.op(SY, lambda e, uc=uc, tt=tt, off=off: e.dma_start(
            out=SA["up"][:, off:off + TT],
            in_=up_raw[uc * 128:(uc + 1) * 128, tt * TT:(tt + 1) * TT]),
            inc=sem, k=16)
        p.op(SY, lambda e, uc=uc, tt=tt, off=off: e.dma_start(
            out=SA["gt"][:, off:off + TT],
            in_=gate_raw[uc * 128:(uc + 1) * 128, tt * TT:(tt + 1) * TT]),
            inc=sem, k=16)
        ldt = p.cnt[sem]
        p.wait(V, sem, ldt)
        if lastg[par]:
            p.wait(V, "m_gpsimd", lastg[par])
        if lastsc[par]:
            p.wait(V, "m_scalar", lastsc[par])
        p.op(V, lambda e, off=off: e.tensor_scalar(
            SA["sr"][:, off:off + TT], SA["up"][:, off:off + TT],
            h["cols"][:, 1:2], MAGIC, ALU.mult, ALU.add))
        p.op(V, lambda e, off=off: e.tensor_scalar(
            SA["gi"][:, off:off + TT], SA["gt"][:, off:off + TT],
            h["cols"][:, 2:3], MAGIC, ALU.mult, ALU.add))
        v1_tok = p.tok(V)
        p.wait(SC, "m_vector", v1_tok)
        if lastst[par]:
            p.wait(SC, stsem, lastst[par])
        p.op(SC, lambda e, off=off: e.activation(
            SA["ui"][:, off:off + TT], SA["sr"][:, off:off + TT], AF.Identity,
            scale=1.0, bias=h["cols"][:, 15:16]))
        p.op(SC, lambda e, off=off: e.activation(
            SA["gi"][:, off:off + TT], SA["gi"][:, off:off + TT], AF.Identity,
            scale=1.0, bias=h["cols"][:, 15:16]))
        gi_sc = p.tok(SC)
        p.wait(GP, "m_scalar", gi_sc)
        if lastst[par]:
            p.wait(GP, stsem, lastst[par])
        p.op(GP, lambda e, uc=uc, tt=tt, off=off: e.dma_start(
            out=upi32[uc * 128:(uc + 1) * 128, tt * TT:(tt + 1) * TT],
            in_=SA["ui"][:, off:off + TT]), inc=stsem, k=16)
        p.fence(SC)
        p.op(SC, lambda e, off=off: e.activation(
            SA["e"][:, off:off + TT], SA["gi"][:, off:off + TT], AF.Sigmoid,
            scale=h["cols"][:, 3:4]))
        si1_sc = p.tok(SC)
        p.wait(V, "m_scalar", si1_sc)
        p.op(V, lambda e, off=off: e.tensor_scalar(
            SA["si"][:, off:off + TT], SA["e"][:, off:off + TT], 65536.0,
            MAGIC, ALU.mult, ALU.add))
        p.fence(V)
        p.op(V, lambda e, off=off: e.tensor_scalar(
            SA["si"][:, off:off + TT], SA["si"][:, off:off + TT], MAGIC,
            65535.0, ALU.subtract, ALU.min))
        si_tok = p.tok(V)
        p.wait(GP, "m_vector", si_tok)
        p.op(GP, lambda e, off=off: e.tensor_mul(
            SA["t4"][:, off:off + TT], SA["gi"][:, off:off + TT],
            SA["si"][:, off:off + TT]))
        g2t = p.tok(GP)
        p.wait(GP, "m_gpsimd", g2t)
        p.op(GP, lambda e, uc=uc, tt=tt, off=off: e.dma_start(
            out=t4d[uc * 128:(uc + 1) * 128, tt * TT:(tt + 1) * TT],
            in_=SA["t4"][:, off:off + TT]), inc=stsem, k=16)
        g4t = p.tok(GP)
        p.wait(V, "m_gpsimd", g4t)
        p.op(V, lambda e, off=off: e.tensor_reduce(
            h["tmpg"][:, 0:1], SA["t4"][:, off:off + TT], AX.X, ALU.max,
            apply_absolute_value=True))
        p.fence(V)
        p.op(V, lambda e: e.tensor_max(h["m4c"][:, 0:1], h["m4c"][:, 0:1],
                                       h["tmpg"][:, 0:1]))
        m4tok = p.tok(V)
        lastst[par] = p.cnt[stsem]
        lastv[par] = m4tok
        lastsc[par] = si1_sc
        lastg[par] = g4t
    p.wait(V, "m_gpsimd", p.cnt["m_gpsimd"])
    colmax_to_scal(g, h["m4c"], 10, psT)
    p.op(V, lambda e: e.tensor_mul(h["scal"][0:1, 11:12], h["scal"][0:1, 10:11],
                                   h["scal"][0:1, 7:8]))
    p.fence(V)
    p.op(V, lambda e: e.tensor_scalar(h["scal"][0:1, 11:12],
                                      h["scal"][0:1, 11:12], 1.0 / 65536.0,
                                      None, ALU.mult))
    p.fence(V)
    p.op(V, lambda e: e.tensor_mul(h["scal"][0:1, 11:12], h["scal"][0:1, 11:12],
                                   h["scal"][0:1, 1:2]))
    p.fence(V)
    p.op(V, lambda e: e.tensor_copy(h["ccst"][0:1, 0:1], h["scal"][0:1, 11:12]))
    exchange_max(g, 3, rg)
    p.op(V, lambda e: e.tensor_scalar(h["scal"][0:1, 12:13],
                                      h["scal"][0:1, 32:33], 1.0 / QMAX, 1e-12,
                                      ALU.mult, ALU.max))
    p.fence(V)
    p.op(V, lambda e: e.reciprocal(h["scal"][0:1, 13:14], h["scal"][0:1, 12:13]))
    p.fence(V)
    p.op(V, lambda e: e.tensor_mul(h["scal"][0:1, 13:14], h["scal"][0:1, 13:14],
                                   h["scal"][0:1, 9:10]))
    p.fence(V)
    p.op(V, lambda e: e.tensor_scalar(h["scal"][0:1, 13:14],
                                      h["scal"][0:1, 13:14], 1.0 / 65536.0,
                                      None, ALU.mult))
    p.fence(V)
    c4tok = scal_to_cols(g, [(13, 4, 1.0)])
    p.wait(V, "m_vector", c4tok)

    p.wait(SY, "s_st0", p.cnt["s_st0"])
    p.wait(SY, "s_st1", p.cnt["s_st1"])
    lastv = {0: 0, 1: 0}
    lastsc = {0: 0, 1: 0}
    lastst = {0: 0, 1: 0}
    lastg = {0: 0, 1: 0}
    for it in range(NTT * ICH):
        tt, uc = divmod(it, ICH)
        par = it % 2
        off = 0 if par == 0 else TT
        sem = "s_up0" if par == 0 else "s_up1"
        stsem = "s_st0" if par == 0 else "s_st1"
        if lastsc[par]:
            p.wait(SY, "m_scalar", lastsc[par])
            p.wait(SY, "m_gpsimd", lastg[par])
            p.wait(SY, "m_vector", lastv[par])
        p.op(SY, lambda e, uc=uc, tt=tt, off=off: e.dma_start(
            out=SA["up"][:, off:off + TT],
            in_=t4d[uc * 128:(uc + 1) * 128, tt * TT:(tt + 1) * TT]),
            inc=sem, k=16)
        p.op(SY, lambda e, uc=uc, tt=tt, off=off: e.dma_start(
            out=SA["gt"][:, off:off + TT],
            in_=upi32[uc * 128:(uc + 1) * 128, tt * TT:(tt + 1) * TT]),
            inc=sem, k=16)
        ldt = p.cnt[sem]
        p.wait(V, sem, ldt)
        if lastg[par]:
            p.wait(V, "m_gpsimd", lastg[par])
        if lastsc[par]:
            p.wait(V, "m_scalar", lastsc[par])
        p.op(V, lambda e, off=off: e.tensor_scalar(
            SA["sr"][:, off:off + TT], SA["up"][:, off:off + TT],
            h["cols"][:, 4:5], MAGIC, ALU.mult, ALU.add))
        v1_tok = p.tok(V)
        p.wait(SC, "m_vector", v1_tok)
        if lastg[par]:
            p.wait(SC, "m_gpsimd", lastg[par])
        p.op(SC, lambda e, off=off: e.activation(
            SA["gi"][:, off:off + TT], SA["sr"][:, off:off + TT], AF.Identity,
            scale=1.0, bias=h["cols"][:, 15:16]))
        g2sc = p.tok(SC)
        p.wait(GP, "m_scalar", g2sc)
        p.wait(GP, sem, ldt)
        if lastv[par]:
            p.wait(GP, "m_vector", lastv[par])
        if lastst[par]:
            p.wait(GP, stsem, lastst[par])
        p.op(GP, lambda e, off=off: e.tensor_mul(
            SA["t4"][:, off:off + TT], SA["gi"][:, off:off + TT],
            SA["gt"][:, off:off + TT]))
        g5t = p.tok(GP)
        p.wait(GP, "m_gpsimd", g5t)
        p.op(GP, lambda e, uc=uc, tt=tt, off=off: e.dma_start(
            out=t5d[uc * 128:(uc + 1) * 128, tt * TT:(tt + 1) * TT],
            in_=SA["t4"][:, off:off + TT]), inc=stsem, k=16)
        p.wait(V, "m_gpsimd", g5t)
        p.op(V, lambda e, off=off: e.tensor_reduce(
            h["tmpcol"][:, 0:1], SA["t4"][:, off:off + TT], AX.X, ALU.max,
            apply_absolute_value=True))
        p.fence(V)
        p.op(V, lambda e: e.tensor_max(h["m5c"][:, 0:1], h["m5c"][:, 0:1],
                                       h["tmpcol"][:, 0:1]))
        t5tok = p.tok(V)
        lastst[par] = p.cnt[stsem]
        lastv[par] = t5tok
        lastsc[par] = g2sc
        lastg[par] = g5t
    colmax_to_scal(g, h["m5c"], 14, psT)
    p.op(V, lambda e: e.tensor_mul(h["scal"][0:1, 18:19], h["scal"][0:1, 5:6],
                                   h["scal"][0:1, 1:2]))
    p.fence(V)
    p.op(V, lambda e: e.tensor_mul(h["scal"][0:1, 15:16], h["scal"][0:1, 14:15],
                                   h["scal"][0:1, 12:13]))
    p.fence(V)
    p.op(V, lambda e: e.tensor_mul(h["scal"][0:1, 15:16], h["scal"][0:1, 15:16],
                                   h["scal"][0:1, 18:19]))
    p.fence(V)
    p.op(V, lambda e: e.tensor_copy(h["ccst"][0:1, 0:1], h["scal"][0:1, 15:16]))
    exchange_max(g, 4, rg)
    p.op(V, lambda e: e.tensor_scalar(h["scal"][0:1, 16:17],
                                      h["scal"][0:1, 32:33], 1.0 / QMAX, 1e-12,
                                      ALU.mult, ALU.max))
    p.fence(V)
    p.op(V, lambda e: e.reciprocal(h["scal"][0:1, 17:18], h["scal"][0:1, 16:17]))
    p.fence(V)
    p.op(V, lambda e: e.tensor_mul(h["scal"][0:1, 17:18], h["scal"][0:1, 17:18],
                                   h["scal"][0:1, 12:13]))
    p.fence(V)
    p.op(V, lambda e: e.tensor_mul(h["scal"][0:1, 17:18], h["scal"][0:1, 17:18],
                                   h["scal"][0:1, 18:19]))
    p.fence(V)
    c5tok = scal_to_cols(g, [(17, 5, 1.0), (16, 6, 1.0)])
    p.wait(V, "m_vector", c5tok)
    p.wait(SC, "m_vector", c5tok)

    p.wait(SY, "s_st0", p.cnt["s_st0"])
    p.wait(SY, "s_st1", p.cnt["s_st1"])
    p.wait(SY, "m_tensor", p.cnt["m_tensor"])
    for s_ in ("m_vector", "m_scalar", "m_gpsimd"):
        p.wait(SY, s_, p.cnt[s_])
    rdy4 = quant_wload(g, "wdT", h["STd"], ICH, D)
    p.wait(TE, "m_gpsimd", rdy4)
    p.wait(SY, "m_gpsimd", rdy4)
    lastv = {0: 0, 1: 0}
    lastsc4 = {0: 0, 1: 0}
    lastx1 = {0: 0, 1: 0}
    lastres = {0: 0, 1: 0}
    evs4 = []
    for tt in range(NTT):
        p.wait(SC, "m_tensor", p.cnt["m_tensor"])
        for uc in range(ICH):
            par = uc % 2
            off = 0 if par == 0 else TT
            sem = "s_up0" if par == 0 else "s_up1"
            if lastsc4[par]:
                p.wait(SY, "m_scalar", lastsc4[par])
                p.wait(SY, "m_vector", lastv[par])
            p.op(SY, lambda e, uc=uc, tt=tt, off=off: e.dma_start(
                out=SA["up"][:, off:off + TT],
                in_=t5d[uc * 128:(uc + 1) * 128, tt * TT:(tt + 1) * TT]),
                inc=sem, k=16)
            ldt = p.cnt[sem]
            p.wait(V, sem, ldt)
            if lastsc4[par]:
                p.wait(V, "m_scalar", lastsc4[par])
            p.op(V, lambda e, off=off: e.tensor_scalar(
                SA["ui"][:, off:off + TT], SA["up"][:, off:off + TT],
                h["cols"][:, 5:6], MAGIC, ALU.mult, ALU.add))
            vt = p.tok(V)
            lastv[par] = vt
            p.wait(SC, "m_vector", vt)
            p.op(SC, lambda e, uc=uc, off=off: e.activation(
                oib[:, uc * TT:(uc + 1) * TT], SA["ui"][:, off:off + TT],
                AF.Identity, scale=1.0, bias=h["cols"][:, 15:16]))
            o2sc = p.tok(SC)
            lastsc4[par] = o2sc
        oi_tok = p.cnt["m_scalar"]
        p.wait(TE, "m_scalar", oi_tok)
        for hc in range(NCH):
            par = hc % 2
            off = 0 if par == 0 else TT
            psb = psU0 if par == 0 else psU1
            if len(evs4) >= 2:
                p.wait(TE, "m_scalar", evs4[-2])
            for ic2 in range(ICH):
                p.op(TE, lambda e, hc=hc, ic2=ic2, psb=psb: e.matmul(
                    psb[:, 0:TT],
                    h["wffn"][:, ic2 * D + hc * 128: ic2 * D + (hc + 1) * 128],
                    oib[:, ic2 * TT:(ic2 + 1) * TT],
                    start=(ic2 == 0), stop=(ic2 == ICH - 1)),
                    inc=("m_tensor" if ic2 == ICH - 1 else None))
            mt = p.cnt["m_tensor"]
            p.wait(SC, "m_tensor", mt)
            if lastx1[par]:
                p.wait(SC, "s_x1" if par == 0 else "s_x1b", lastx1[par])
            ev2 = p.op(SC, lambda e, psb=psb, off=off: e.activation(
                h["gfb"][:, off:off + TT], psb[:, 0:TT], AF.Silu,
                scale=h["cols"][:, 6:7]), inc="m_scalar")
            evs4.append(ev2)
            rsem = "s_r0" if par == 0 else "s_r1"
            if lastres[par]:
                p.wait(SY, "m_vector", lastres[par])
            batch_load(g, rsem,
                       [(SA["xr"][:, off:off + TT],
                         h["x"][hc * 128:(hc + 1) * 128, tt * TT:(tt + 1) * TT]),
                        (SA["sr"][:, off:off + TT],
                         h["s0"][hc * 128:(hc + 1) * 128, tt * TT:(tt + 1) * TT])],
                       ())
            rld = p.cnt[rsem]
            p.wait(V, "m_scalar", ev2)
            p.wait(V, rsem, rld)
            p.op(V, lambda e, off=off: e.tensor_mul(
                SA["gr"][:, off:off + TT], SA["sr"][:, off:off + TT],
                h["gfb"][:, off:off + TT]))
            p.fence(V)
            p.op(V, lambda e, off=off: e.tensor_add(
                SA["gr"][:, off:off + TT], SA["xr"][:, off:off + TT],
                SA["gr"][:, off:off + TT]))
            x1tok = p.tok(V)
            lastres[par] = x1tok
            p.wait(GP, "m_vector", x1tok)
            p.op(GP, lambda e, hc=hc, tt=tt, off=off: e.dma_start(
                out=h["x1"][hc * 128:(hc + 1) * 128, tt * TT:(tt + 1) * TT],
                in_=SA["gr"][:, off:off + TT]),
                inc="s_x1" if par == 0 else "s_x1b", k=16)
            lastx1[par] = p.cnt["s_x1" if par == 0 else "s_x1b"]


def build_g2(rg=None):
    g = G("g2")
    emit_g2(g, rg)
    return g



def emit_g3(g):
    p, h = g.p, g.h
    ps = common_setup(g)
    for sname in ("s_q1a", "s_q1b", "s_s1a", "s_s1b", "s_sh"):
        p.new_sem(sname)
    g.dio_in("x1", [D, T])
    g.din("wdqT", [D, QL])
    g.din("wuqT", [QL, D])
    g.din("wdq_nat", [QL, D])
    g.din("wuk_nat", [D, QL])
    g.din("wuv_nat", [D, QL])
    g.din("woT1", [D, D])
    g.din("wdkvT", [D, QL])
    g.din("masks", [4 * 128, TT], BF16)
    g.dio_out("xn1", [D, T])
    g.dio_out("s1", [D, T])
    q1 = g.dint("q1", [D, T], BF16)
    shd3 = g.dint("shd3", [2, 16])

    g.sb("sq", [128, NCH * TT])
    g.sb("wuqTb", [128, 2 * D], BF16)
    g.sb("wdq_b", [128, 2 * D], BF16)
    g.sb("wdqT_b", [128, NCH * QL], BF16)
    g.sb("wdkvT_b", [128, NCH * QL], BF16)
    g.sb("wuk_b", [128, NCH * QL], BF16)
    g.sb("wuv_b", [128, NCH * QL], BF16)
    g.sb("wostg", [128, 2 * D], BF16)
    g.sb("ATb", [128, NCH * D], BF16)
    g.sb("keT", [128, 2 * D], BF16)
    g.sb("Mb", [128, 2 * D], BF16)
    g.sb("ntabq3", [2, 16])
    g.sb("shift3", [2, 16])
    g.sb("cs2", [2, 1])
    g.sb("qdqb", [128, 2 * TT], BF16)
    g.sb("qnb3", [128, NCH * TT], BF16)
    g.sb("ckvb", [128, 2 * T], BF16)
    g.sb("ckvtm", [128, 32 * 257], BF16)
    g.sb("sqk", [128, TT])
    g.sb("q1h", [64, S], BF16)
    g.sb("KCb", [64, S], BF16)
    g.sb("msk", [128, TT])
    g.sb("maskt", [128, 4 * TT], BF16)
    g.sb("e3t0", [128, TT], BF16); g.sb("e3t1", [128, TT], BF16)
    g.sb("e3t2", [128, TT], BF16)
    g.sb("ctxb", [128, 2 * TT], BF16)
    g.sb("rrow", [1, TT])
    g.sb("rbcs", [64, TT])
    g.sb("s1t", [64, 2 * TT])
    g.sb("shcol3", [128, 1])

    psP0, psP1, psA = ps[0], ps[1], ps[2]
    psC, psRS, psW = ps[3], ps[4], ps[5]
    psB = ps["B"]
    psTb = psRS[:, 64:128].bitcast(BF16)

    p.op(V, lambda e: e.memset(h["ntabq3"][:], 0.0))
    p.op(V, lambda e: e.memset(h["scal"][:], 0.0))
    p.op(V, lambda e: e.memset(
        h["ckvtm"][:].rearrange("p (t c) -> p t c", c=257)[:, :, 256:257], 1.0))
    cst0 = p.tok(V)
    p.wait(TE, "m_vector", cst0)
    p.wait(SC, "m_vector", cst0)
    p.wait(SY, "m_vector", cst0)
    batch_load(g, "s_prm",
               [(h["maskt"][:, mi * TT:(mi + 1) * TT],
                 h["masks"][mi * 128:(mi + 1) * 128, :]) for mi in range(4)],
               (V,))
    idb_tok = p.tok(V)
    p.wait(TE, "m_vector", idb_tok)

    load_wT_bf16(g, h["wuqT"], h["wuqTb"], 2, D)
    load_wT_bf16(g, h["wdq_nat"], h["wdq_b"], 2, D)
    load_wT_bf16(g, h["wdqT"], h["wdqT_b"], NCH, QL)
    load_wT_bf16(g, h["wdkvT"], h["wdkvT_b"], NCH, QL)
    load_wT_bf16(g, h["wuk_nat"], h["wuk_b"], NCH, QL)
    load_wT_bf16(g, h["wuv_nat"], h["wuv_b"], NCH, QL)
    wtok = p.cnt["m_vector"]
    p.wait(TE, "m_vector", wtok)

    evs = []
    for jc in range(NCH):
        for ih in range(2):
            psb = psP0 if (jc * 2 + ih) % 2 == 0 else psP1
            if len(evs) >= 2:
                p.wait(TE, "m_scalar", evs[-2])
            for lc in range(2):
                p.op(TE, lambda e, jc=jc, ih=ih, lc=lc, psb=psb: e.matmul(
                    psb[:, 0:TT],
                    h["wuqTb"][:, lc * D + jc * 128: lc * D + (jc + 1) * 128],
                    h["wdq_b"][:, lc * D + ih * TT: lc * D + (ih + 1) * TT],
                    start=(lc == 0), stop=(lc == 1)),
                    inc=("m_tensor" if lc == 1 else None))
            mt = p.cnt["m_tensor"]
            p.wait(SC, "m_tensor", mt)
            ev = p.op(SC, lambda e, jc=jc, ih=ih, psb=psb: e.activation(
                h["ATb"][:, jc * D + ih * TT: jc * D + (ih + 1) * TT],
                psb[:, 0:TT], AF.Copy), inc="m_scalar")
            evs.append(ev)
    p.wait(TE, "m_scalar", evs[-1])
    for lh in range(2):
        for ih in range(2):
            psb = psP0 if (lh * 2 + ih) % 2 == 0 else psP1
            if len(evs) >= 2:
                p.wait(TE, "m_scalar", evs[-2])
            for jc in range(NCH):
                p.op(TE, lambda e, lh=lh, ih=ih, jc=jc, psb=psb: e.matmul(
                    psb[:, 0:TT],
                    h["wuk_b"][:, jc * QL + lh * 128: jc * QL + (lh + 1) * 128],
                    h["ATb"][:, jc * D + ih * TT: jc * D + (ih + 1) * TT],
                    start=(jc == 0), stop=(jc == NCH - 1)),
                    inc=("m_tensor" if jc == NCH - 1 else None))
            mt = p.cnt["m_tensor"]
            p.wait(SC, "m_tensor", mt)
            ev = p.op(SC, lambda e, lh=lh, ih=ih, psb=psb: e.activation(
                h["keT"][:, lh * D + ih * TT: lh * D + (ih + 1) * TT],
                psb[:, 0:TT], AF.Copy), inc="m_scalar")
            evs.append(ev)
    p.wait(TE, "m_scalar", evs[-1])
    mbanks = [psP0, psP1, psA, psC]
    wo_toks = []
    p.wait(SY, "m_vector", p.cnt["m_vector"])
    for jc in range(NCH):
        par = jc % 2
        off = 0 if par == 0 else D
        sem = "s_w0" if par == 0 else "s_w1"
        if jc >= 2:
            p.wait(SY, "m_tensor", wo_toks[jc - 2])
        ldt = p.op(SY, lambda e, jc=jc, off=off: e.dma_start(
            out=h["wstage"][:, 0:D] if False else h["wstage"][:, off:off + D],
            in_=h["woT1"][jc * 128:(jc + 1) * 128, :]), inc=sem, k=16)
        p.wait(V, sem, ldt)
        wct = p.op(V, lambda e, off=off: e.tensor_copy(
            h["wostg"][:, off:off + D], h["wstage"][:, off:off + D]),
            inc="m_vector")
        p.wait(TE, "m_vector", wct)
        for q in range(4):
            lh, ih = divmod(q, 2)
            tk = p.op(TE, lambda e, jc=jc, lh=lh, ih=ih, off=off, q=q: e.matmul(
                mbanks[q][:, 0:TT],
                h["wuv_b"][:, jc * QL + lh * 128: jc * QL + (lh + 1) * 128],
                h["wostg"][:, off + ih * TT: off + (ih + 1) * TT],
                start=(jc == 0), stop=(jc == NCH - 1)),
                inc=("m_tensor" if q == 3 else None))
        wo_toks.append(p.cnt["m_tensor"])
    p.wait(SC, "m_tensor", wo_toks[-1])
    for q in range(4):
        lh, ih = divmod(q, 2)
        ev = p.op(SC, lambda e, lh=lh, ih=ih, q=q: e.activation(
            h["Mb"][:, lh * D + ih * TT: lh * D + (ih + 1) * TT],
            mbanks[q][:, 0:TT], AF.Copy), inc="m_scalar")
        evs.append(ev)
    ke_tok = evs[-1]
    p.wait(V, "m_scalar", ke_tok)
    for it in range(4):
        lh, ih = divmod(it, 2)
        p.op(V, lambda e, lh=lh, ih=ih, sq=h["sq"]: e.tensor_mul(
            sq[:, 0:TT],
            h["keT"][:, lh * D + ih * TT: lh * D + (ih + 1) * TT],
            h["keT"][:, lh * D + ih * TT: lh * D + (ih + 1) * TT]))
        sqt = p.tok(V)
        p.wait(TE, "m_vector", sqt)
        tk = p.op(TE, lambda e, it=it, sq=h["sq"]: e.matmul(
            psA[0:1, 0:TT], h["ones_col"][:, 0:1], sq[:, 0:TT],
            start=(it == 0), stop=(it == 3)), inc="m_tensor")
        p.wait(V, "m_tensor", tk)
    p.op(V, lambda e: e.tensor_reduce(h["scal"][0:1, 0:1], psA[0:1, 0:TT],
                                      AX.X, ALU.add))
    p.fence(V)

    p.wait(SC, "m_vector", p.cnt["m_vector"])
    xn_toks = [0] * NTT
    sq_toks = [0] * NTT
    cons_toks = [0] * NTT
    prev_xn_store = 0
    prev_qn_store = {0: 0, 1: 0}
    evq = []
    prev_qn_tok = 0
    prev_sq_tok = 0
    for tt in range(NTT):
        xa = h["xa0"]
        seq = tt // 4
        if tt >= 1:
            p.wait(SY, "m_vector", xn_toks[tt - 1])
            p.wait(SY, "m_scalar", sq_toks[tt - 1])
        batch_load(g, "s_xa",
                   [(xa[:, c * TT:(c + 1) * TT],
                     h["x1"][c * 128:(c + 1) * 128, tt * TT:(tt + 1) * TT])
                    for c in range(NCH)], (V, SC))
        if tt >= 1:
            p.wait(V, "m_tensor", cons_toks[tt - 1])
            p.wait(SC, "m_tensor", cons_toks[tt - 1])
            p.wait(V, "s_xn", prev_xn_store)
            p.wait(TE, "m_vector", p.cnt["m_vector"])
        xn_tok, xnb_tok = layernorm_tile(g, xa, h["xn"], h["xnb"], psA, psB,
                                         psC=psC)
        xn_toks[tt] = xn_tok
        sq_toks[tt] = xnb_tok
        p.wait(GP, "m_vector", xn_tok)
        for c in range(NCH):
            p.op(GP, lambda e, c=c, tt=tt: e.dma_start(
                out=h["xn1"][c * 128:(c + 1) * 128, tt * TT:(tt + 1) * TT],
                in_=h["xn"][:, c * TT:(c + 1) * TT]), inc="s_xn", k=16)
        prev_xn_store = p.cnt["s_xn"]
        p.wait(TE, "m_scalar", xnb_tok)
        for qlc in range(2):
            psb = psP0 if qlc % 2 == 0 else psP1
            if len(evq) >= 2:
                p.wait(TE, "m_scalar", evq[-2])
            for jc in range(NCH):
                p.op(TE, lambda e, qlc=qlc, jc=jc, psb=psb: e.matmul(
                    psb[:, 0:TT],
                    h["wdqT_b"][:, jc * QL + qlc * 128: jc * QL + (qlc + 1) * 128],
                    h["xnb"][:, jc * TT:(jc + 1) * TT],
                    start=(jc == 0), stop=(jc == NCH - 1)),
                    inc=("m_tensor" if jc == NCH - 1 else None))
            mt = p.cnt["m_tensor"]
            p.wait(SC, "m_tensor", mt)
            ev = p.op(SC, lambda e, qlc=qlc, psb=psb: e.activation(
                h["qdqb"][:, qlc * TT:(qlc + 1) * TT], psb[:, 0:TT], AF.Copy),
                inc="m_scalar")
            evq.append(ev)
        qdq_tok = evq[-1]
        p.wait(TE, "m_scalar", qdq_tok)
        if prev_qn_tok:
            p.wait(V, "m_tensor", 0)
        for ic in range(NCH):
            psb = psP0 if ic % 2 == 0 else psP1
            if len(evq) >= 2:
                p.wait(TE, "m_scalar", evq[-2])
            for qlc in range(2):
                p.op(TE, lambda e, ic=ic, qlc=qlc, psb=psb: e.matmul(
                    psb[:, 0:TT],
                    h["wuqTb"][:, qlc * D + ic * 128: qlc * D + (ic + 1) * 128],
                    h["qdqb"][:, qlc * TT:(qlc + 1) * TT],
                    start=(qlc == 0), stop=(qlc == 1)),
                    inc=("m_tensor" if qlc == 1 else None))
            mt = p.cnt["m_tensor"]
            p.wait(SC, "m_tensor", mt)
            par = ic % 2
            if prev_qn_store[par]:
                p.wait(SC, "s_q1a" if par == 0 else "s_q1b",
                       prev_qn_store[par])
            ev = p.op(SC, lambda e, ic=ic, psb=psb: e.activation(
                h["qnb3"][:, ic * TT:(ic + 1) * TT], psb[:, 0:TT], AF.Copy),
                inc="m_scalar")
            evq.append(ev)
            p.wait(GP, "m_scalar", ev)
            p.op(GP, lambda e, ic=ic, tt=tt: e.dma_start(
                out=q1[ic * 128:(ic + 1) * 128, tt * TT:(tt + 1) * TT],
                in_=h["qnb3"][:, ic * TT:(ic + 1) * TT]),
                inc="s_q1a" if par == 0 else "s_q1b", k=16)
            prev_qn_store[par] = p.cnt["s_q1a" if par == 0 else "s_q1b"]
            sqt = p.op(SC, lambda e, ic=ic, sq=h["sq"]: e.activation(
                sq[:, 0:TT], h["qnb3"][:, ic * TT:(ic + 1) * TT],
                AF.Square), inc="m_scalar")
            evq.append(sqt)
            p.wait(TE, "m_scalar", sqt)
            hm = p.op(TE, lambda e, sq=h["sq"]: e.matmul(
                psRS[0:2, 0:TT], h["hmask"][:, 0:2], sq[:, 0:TT],
                start=True, stop=True), inc="m_tensor")
            p.wait(V, "m_tensor", hm)
            p.op(V, lambda e: e.tensor_reduce(h["tmp2"][0:2, 0:1],
                                              psRS[0:2, 0:TT], AX.X, ALU.max))
            p.fence(V)
            p.op(V, lambda e, ic=ic, seq=seq: e.tensor_max(
                h["ntabq3"][0:2, (seq * NCH + ic):(seq * NCH + ic) + 1],
                h["ntabq3"][0:2, (seq * NCH + ic):(seq * NCH + ic) + 1],
                h["tmp2"][0:2, 0:1]))
            vred = p.tok(V)
            p.wait(V, "m_vector", vred)
            p.wait(TE, "m_vector", vred)
        for kc in range(2):
            psb = psP0 if kc % 2 == 0 else psP1
            if len(evq) >= 2:
                p.wait(TE, "m_scalar", evq[-2])
            for jc in range(NCH):
                p.op(TE, lambda e, kc=kc, jc=jc, psb=psb: e.matmul(
                    psb[:, 0:TT],
                    h["wdkvT_b"][:, jc * QL + kc * 128: jc * QL + (kc + 1) * 128],
                    h["xnb"][:, jc * TT:(jc + 1) * TT],
                    start=(jc == 0), stop=(jc == NCH - 1)),
                    inc=("m_tensor" if jc == NCH - 1 else None))
            mt = p.cnt["m_tensor"]
            p.wait(SC, "m_tensor", mt)
            ev = p.op(SC, lambda e, kc=kc, tt=tt, psb=psb: e.activation(
                h["ckvb"][:, kc * T + tt * TT: kc * T + (tt + 1) * TT],
                psb[:, 0:TT], AF.Copy), inc="m_scalar")
            evq.append(ev)
            p.wait(V, "m_scalar", ev)
            sqv = p.op(V, lambda e, kc=kc, tt=tt: e.tensor_mul(
                h["sqk"][:, 0:TT],
                h["ckvb"][:, kc * T + tt * TT: kc * T + (tt + 1) * TT],
                h["ckvb"][:, kc * T + tt * TT: kc * T + (tt + 1) * TT]),
                inc="m_vector")
            p.wait(TE, "m_vector", sqv)
            ck = p.op(TE, lambda e, kc=kc: e.matmul(
                psC[0:1, 0:TT], h["ones_col"][:, 0:1], h["sqk"][:, 0:TT],
                start=(kc == 0), stop=(kc == 1)),
                inc=("m_tensor" if kc == 1 else None))
        cknorm = p.cnt["m_tensor"]
        p.wait(V, "m_tensor", cknorm)
        p.op(V, lambda e: e.tensor_reduce(h["tmp2"][0:1, 1:2], psC[0:1, 0:TT],
                                          AX.X, ALU.max))
        p.fence(V)
        p.op(V, lambda e, seq=seq: e.tensor_max(
            h["scal"][0:1, 20 + seq:21 + seq], h["scal"][0:1, 20 + seq:21 + seq],
            h["tmp2"][0:1, 1:2]))
        p.fence(V)
        ck_ev = evq[-1]
        p.wait(TE, "m_scalar", ck_ev)
        for sub in range(4):
            ti = tt * 4 + sub
            for kc in range(2):
                tp = p.op(TE, lambda e, kc=kc, tt=tt, sub=sub: e.transpose(
                    psTb[:, 0:128],
                    h["ckvb"][:, kc * T + tt * TT + sub * 128:
                              kc * T + tt * TT + (sub + 1) * 128],
                    h["identb"][:]), inc="m_tensor")
                p.wait(SC, "m_tensor", tp)
                cp = p.op(SC, lambda e, ti=ti, kc=kc: e.activation(
                    h["ckvtm"][:, ti * 257 + kc * 128: ti * 257 + (kc + 1) * 128],
                    psTb[:, 0:128], AF.Copy), inc="m_scalar")
                p.wait(TE, "m_scalar", cp)
        cons_toks[tt] = p.cnt["m_tensor"]
        prev_qn_tok = 1

    for seq in range(NSEQ):
        p.op(V, lambda e, seq=seq: e.tensor_mul(
            h["scal"][0:1, 24 + seq:25 + seq], h["scal"][0:1, 0:1],
            h["scal"][0:1, 20 + seq:21 + seq]))
    sct0 = p.tok(V)
    p.wait(TE, "m_vector", sct0)
    for seq in range(NSEQ):
        if seq > 0:
            p.wait(TE, "m_vector", p.cnt["m_vector"])
        tk = p.op(TE, lambda e, seq=seq: e.matmul(
            psW[0:2, 0:1], h["ones_row"][0:1, 0:2],
            h["scal"][0:1, 24 + seq:25 + seq], start=True, stop=True),
            inc="m_tensor")
        p.wait(V, "m_tensor", tk)
        p.op(V, lambda e: e.tensor_copy(h["cs2"][0:2, 0:1], psW[0:2, 0:1]))
        p.fence(V)
        p.op(V, lambda e, seq=seq: e.tensor_scalar(
            h["shift3"][0:2, seq * NCH:(seq + 1) * NCH],
            h["ntabq3"][0:2, seq * NCH:(seq + 1) * NCH],
            h["cs2"][0:2, 0:1], None, ALU.mult))
        p.fence(V)
    sht = p.tok(V)
    p.wait(SC, "m_vector", sht)
    p.op(SC, lambda e: e.activation(h["shift3"][0:2, 0:16], h["shift3"][0:2, 0:16],
                                    AF.Ln, bias=h["cst"][0:2, 3:4]))
    p.fence(SC)
    sh2 = p.op(SC, lambda e: e.activation(h["shift3"][0:2, 0:16],
                                          h["shift3"][0:2, 0:16], AF.Exp,
                                          scale=0.5), inc="m_scalar")
    p.wait(V, "m_scalar", sh2)
    p.op(V, lambda e: e.tensor_scalar(h["shift3"][0:2, 0:16],
                                      h["shift3"][0:2, 0:16],
                                      0.125, -SM_SLACK, ALU.mult, ALU.add))
    p.fence(V)
    p.op(V, lambda e: e.tensor_scalar(h["shift3"][0:2, 0:16],
                                      h["shift3"][0:2, 0:16], 0.0, None, ALU.max))
    p.fence(V)
    shneg = p.op(V, lambda e: e.tensor_scalar(h["shift3"][0:2, 0:16],
                                              h["shift3"][0:2, 0:16], -1.0, None,
                                              ALU.mult), inc="m_vector")
    p.wait(GP, "m_vector", shneg)
    p.op(GP, lambda e: e.dma_start(out=shd3, in_=h["shift3"][0:2, 0:16]),
         inc="st2", k=16)
    p.barrier()
    for sem in ("st2", "s_q1a", "s_q1b", "s_xn"):
        p.wait(SY, sem, p.cnt[sem])

    lasts1 = {0: 0, 1: 0}
    sc3_last = {0: 0, 1: 0, 2: 0}
    sc3_lastv = {0: 0, 1: 0, 2: 0}
    et3_last = {0: 0, 1: 0, 2: 0}
    mk_last = [0]
    rec_prev = [0]
    s1v_prev = [0]
    for s in range(NSEQ):
        for hd in range(NH):
            ic, halfp = hd // 2, hd % 2
            if s + hd > 0:
                p.wait(SY, "m_tensor", p.cnt["m_tensor"])
            p.op(SY, lambda e, s=s, hd=hd: e.dma_start(
                out=h["q1h"][:], in_=q1[hd * 64:(hd + 1) * 64, s * S:(s + 1) * S]),
                inc="s_att", k=16)
            p.op(SY, lambda e, halfp=halfp, s=s, ic=ic: e.dma_start(
                out=h["shcol3"][:, 0:1],
                in_=shd3[halfp:halfp + 1, s * NCH + ic: s * NCH + ic + 1]
                .broadcast_to([128, 1])), inc="s_att", k=16)
            hld = p.cnt["s_att"]
            p.wait(TE, "s_att", hld)
            if sc3_last[2]:
                p.wait(TE, "m_scalar", sc3_last[2])
                p.wait(TE, "m_vector", sc3_lastv[2])
            kevs = []
            for sc4 in range(NTC):
                for lc in range(2):
                    p.op(TE, lambda e, sc4=sc4, lc=lc, hd=hd, s=s: e.matmul(
                        psW[0:64, 0:TT],
                        h["keT"][:, lc * D + hd * 64: lc * D + (hd + 1) * 64],
                        h["ckvb"][:, lc * T + s * S + sc4 * TT:
                                  lc * T + s * S + (sc4 + 1) * TT],
                        start=(lc == 0), stop=(lc == 1)),
                        inc=("m_tensor" if lc == 1 else None))
                mt = p.cnt["m_tensor"]
                p.wait(SC, "m_tensor", mt)
                kev = p.op(SC, lambda e, sc4=sc4: e.activation(
                    h["KCb"][0:64, sc4 * TT:(sc4 + 1) * TT], psW[0:64, 0:TT],
                    AF.Copy), inc="m_scalar")
                kevs.append(kev)
                p.wait(TE, "m_scalar", kev)
            p.wait(SC, "s_att", hld)
            for tc in range(NTC):
                n_st = tc * 4 + 4
                ext_toks = {}
                ctx_toks = {}
                sc3 = [psP0, psP1, psW]
                et3 = [h["e3t0"], h["e3t1"], h["e3t2"]]

                def emit_ctx(st):
                    et = et3[st % 3]
                    ti = s * 16 + st
                    p.wait(TE, "m_scalar", ext_toks[st])
                    if st == 0 and rec_prev[0]:
                        p.wait(TE, "m_vector", rec_prev[0])
                    for lc in range(2):
                        p.op(TE, lambda e, st=st, lc=lc, et=et, ti=ti,
                             n_st=n_st: e.matmul(
                            psA[:, 0:TT] if lc == 0 else psC[:, 0:TT],
                            h["ckvtm"][:, ti * 257 + lc * 128:
                                       ti * 257 + (lc + 1) * 128],
                            et[:], start=(st == 0), stop=(st == n_st - 1)),
                            inc=None)
                    ct = p.op(TE, lambda e, st=st, et=et, ti=ti, n_st=n_st:
                              e.matmul(
                        psRS[0:1, 0:TT],
                        h["ckvtm"][:, ti * 257 + 256: ti * 257 + 257],
                        et[:], start=(st == 0), stop=(st == n_st - 1)),
                        inc="m_tensor")
                    ctx_toks[st] = ct

                for st in range(n_st):
                    pss = sc3[st % 3]
                    et = et3[st % 3]
                    if sc3_last[st % 3]:
                        p.wait(TE, "m_scalar", sc3_last[st % 3])
                        p.wait(TE, "m_vector", sc3_lastv[st % 3])
                    smt = p.op(TE, lambda e, st=st, tc=tc, pss=pss: e.matmul(
                        pss[:, 0:TT], h["KCb"][0:64, st * 128:(st + 1) * 128],
                        h["q1h"][:, tc * TT:(tc + 1) * TT],
                        start=True, stop=True), inc="m_tensor")
                    partial = st >= tc * 4
                    if partial:
                        mi = st - tc * 4
                        p.wait(V, "m_tensor", smt)
                        if mk_last[0]:
                            p.wait(V, "m_scalar", mk_last[0])
                        mk = p.op(V, lambda e, pss=pss, mi=mi: e.tensor_add(
                            h["msk"][:, 0:TT], pss[:, 0:TT],
                            h["maskt"][:, mi * TT:(mi + 1) * TT]),
                            inc="m_vector")
                        sc3_lastv[st % 3] = mk
                        p.wait(SC, "m_vector", mk)
                        if et3_last[st % 3]:
                            p.wait(SC, "m_tensor", et3_last[st % 3])
                        ext = p.op(SC, lambda e, et=et: e.activation(
                            et[:], h["msk"][:, 0:TT], AF.Exp,
                            bias=h["shcol3"][:, 0:1], scale=0.125),
                            inc="m_scalar")
                        mk_last[0] = ext
                    else:
                        p.wait(SC, "m_tensor", smt)
                        if et3_last[st % 3]:
                            p.wait(SC, "m_tensor", et3_last[st % 3])
                        ext = p.op(SC, lambda e, pss=pss, et=et: e.activation(
                            et[:], pss[:, 0:TT], AF.Exp,
                            bias=h["shcol3"][:, 0:1], scale=0.125),
                            inc="m_scalar")
                    ext_toks[st] = ext
                    sc3_last[st % 3] = ext
                    if st >= 1:
                        emit_ctx(st - 1)
                        et3_last[(st - 1) % 3] = ctx_toks[st - 1]
                emit_ctx(n_st - 1)
                et3_last[(n_st - 1) % 3] = ctx_toks[n_st - 1]
                ctx_done = ctx_toks[n_st - 1]
                p.wait(SC, "m_tensor", ctx_done)
                cv0 = p.op(SC, lambda e: e.activation(
                    h["ctxb"][:, 0:TT], psA[:, 0:TT], AF.Copy), inc="m_scalar")
                cv1 = p.op(SC, lambda e: e.activation(
                    h["ctxb"][:, TT:2 * TT], psC[:, 0:TT], AF.Copy),
                    inc="m_scalar")
                p.wait(V, "m_tensor", ctx_done)
                p.op(V, lambda e: e.reciprocal(h["rrow"][0:1, 0:TT],
                                               psRS[0:1, 0:TT]))
                rec_tok = p.tok(V)
                rec_prev[0] = rec_tok
                p.wait(TE, "m_vector", rec_tok)
                p.wait(TE, "m_scalar", cv1)
                bt = p.op(TE, lambda e: e.matmul(
                    psB[0:64, 0:TT], h["ones_row"][0:1, 0:64],
                    h["rrow"][0:1, 0:TT], start=True, stop=True),
                    inc="m_tensor")
                p.wait(SC, "m_tensor", bt)
                if s1v_prev[0]:
                    p.wait(SC, "m_vector", s1v_prev[0])
                rb = p.op(SC, lambda e: e.activation(
                    h["rbcs"][0:64, 0:TT], psB[0:64, 0:TT], AF.Copy),
                    inc="m_scalar")
                p.wait(TE, "m_scalar", rb)
                if s1v_prev[0]:
                    p.wait(TE, "m_vector", s1v_prev[0])
                for lc in range(2):
                    p.op(TE, lambda e, lc=lc, hd=hd: e.matmul(
                        psB[0:64, TT:2 * TT],
                        h["Mb"][:, lc * D + hd * 64: lc * D + (hd + 1) * 64],
                        h["ctxb"][:, lc * TT:(lc + 1) * TT],
                        start=(lc == 0), stop=(lc == 1)),
                        inc=("m_tensor" if lc == 1 else None))
                yt = p.cnt["m_tensor"]
                p.wait(V, "m_tensor", yt)
                par = tc % 2
                off = 0 if par == 0 else TT
                if lasts1[par]:
                    p.wait(V, "s_s1a" if par == 0 else "s_s1b", lasts1[par])
                s1v = p.op(V, lambda e, off=off: e.tensor_mul(
                    h["s1t"][0:64, off:off + TT], psB[0:64, TT:2 * TT],
                    h["rbcs"][0:64, 0:TT]), inc="m_vector")
                s1v_prev[0] = s1v
                p.wait(GP, "m_vector", s1v)
                p.op(GP, lambda e, hd=hd, s=s, tc=tc, off=off: e.dma_start(
                    out=h["s1"][hd * 64:(hd + 1) * 64,
                                s * S + tc * TT: s * S + (tc + 1) * TT],
                    in_=h["s1t"][0:64, off:off + TT]),
                    inc="s_s1a" if par == 0 else "s_s1b", k=16)
                lasts1[par] = p.cnt["s_s1a" if par == 0 else "s_s1b"]


def build_g3():
    g = G("g3")
    emit_g3(g)
    return g



def emit_g4(g):
    p, h = g.p, g.h
    ps = common_setup(g)
    for sname in ("s_col", "s_r0", "s_r1", "s_o0", "s_o1", "s_wf"):
        p.new_sem(sname)
    g.dio_in("x1", [D, T])
    g.dio_in("xn1", [D, T])
    g.dio_in("s1", [D, T])
    g.din("w8T", [D, D])
    g.din("b1", [128, NCH])
    g.din("in_s", [1, 1])
    g.din("w_s", [1, 1])
    g.dout("out", [D, T])
    if "scald" not in g.h:
        g.dint("scald", [1, 64])

    g.sb("w8b", [128, NCH * D], BF16)
    g.sb("b1c", [128, NCH])
    g.sb("x8b", [128, NCH * TT], BF16)
    g.sb("w1t", [128, TT]); g.sb("d_t", [128, TT]); g.sb("hi_t", [128, TT])
    g.sb("ts_t", [128, TT]); g.sb("aw_t", [128, TT]); g.sb("mk_t", [128, TT])
    g.sb("df8", [128, TT])
    g.sb("dfb", [128, 2 * TT]); g.sb("efb4", [128, 2 * TT]); g.sb("gfb4", [128, 2 * TT])
    g.sb("xr", [128, 2 * TT]); g.sb("sr", [128, 2 * TT]); g.sb("orr", [128, 2 * TT])

    psP0, psP1, psW = ps[0], ps[1], ps[5]

    p.op(V, lambda e: e.memset(h["scal"][:], 0.0))
    p.op(V, lambda e: e.memset(h["cols"][:], 0.0))
    p.op(V, lambda e: e.memset(h["colsr"][:], 0.0))
    cst0 = p.tok(V)
    p.wait(TE, "m_vector", cst0)
    p.wait(SC, "m_vector", cst0)
    p.wait(SY, "m_vector", cst0)
    batch_load(g, "s_prm", [(h["b1c"][:], h["b1"]),
                            (h["scal"][0:1, 0:1], h["in_s"]),
                            (h["scal"][0:1, 1:2], h["w_s"])], (V,))
    p.op(V, lambda e: e.reciprocal(h["scal"][0:1, 2:3], h["scal"][0:1, 0:1]))
    p.fence(V)
    p.op(V, lambda e: e.tensor_mul(h["scal"][0:1, 3:4], h["scal"][0:1, 0:1],
                                   h["scal"][0:1, 1:2]))
    p.fence(V)
    ctok = scal_to_cols(g, [(2, 0, 1.0), (3, 1, 1.0)])
    p.wait(SC, "m_vector", ctok)
    load_wT_bf16(g, h["w8T"], h["w8b"], NCH, D)
    wtok = p.cnt["m_vector"]
    p.wait(TE, "m_vector", wtok)

    C2_9 = float(2.0 ** -9)
    lastq = 0
    evs = []
    lastres = {0: 0, 1: 0}
    lastst = {0: 0, 1: 0}
    for tt in range(NTT):
        if lastq:
            p.wait(SY, "m_vector", lastq)
        batch_load(g, "s_xa",
                   [(h["xa0"][:, c * TT:(c + 1) * TT],
                     h["xn1"][c * 128:(c + 1) * 128, tt * TT:(tt + 1) * TT])
                    for c in range(NCH)], (V,))
        p.wait(V, "m_tensor", p.cnt["m_tensor"])
        for c in range(NCH):
            p.op(V, lambda e, c=c: e.tensor_scalar(
                h["w1t"][:, 0:TT], h["xa0"][:, c * TT:(c + 1) * TT],
                h["cols"][:, 0:1], 448.0, ALU.mult, ALU.min))
            p.fence(V)
            p.op(V, lambda e: e.tensor_scalar(
                h["w1t"][:, 0:TT], h["w1t"][:, 0:TT], -448.0, None, ALU.max))
            p.fence(V)
            p.op(V, lambda e: e.scalar_tensor_tensor(
                h["d_t"][:, 0:TT], h["w1t"][:, 0:TT], C1V, h["w1t"][:, 0:TT],
                ALU.mult, ALU.subtract))
            p.fence(V)
            p.op(V, lambda e: e.scalar_tensor_tensor(
                h["hi_t"][:, 0:TT], h["w1t"][:, 0:TT], C1V, h["d_t"][:, 0:TT],
                ALU.mult, ALU.subtract))
            p.fence(V)
            p.op(V, lambda e: e.tensor_scalar(
                h["ts_t"][:, 0:TT], h["w1t"][:, 0:TT], 512.0, MAGIC,
                ALU.mult, ALU.add))
            p.fence(V)
            p.op(V, lambda e: e.tensor_scalar(
                h["ts_t"][:, 0:TT], h["ts_t"][:, 0:TT], MAGIC, C2_9,
                ALU.subtract, ALU.mult))
            p.fence(V)
            p.op(V, lambda e: e.scalar_tensor_tensor(
                h["aw_t"][:, 0:TT], h["w1t"][:, 0:TT], -1.0, h["w1t"][:, 0:TT],
                ALU.mult, ALU.max))
            p.fence(V)
            p.op(V, lambda e: e.tensor_single_scalar(
                h["mk_t"][:, 0:TT], h["aw_t"][:, 0:TT], 0.015625, ALU.is_lt))
            p.fence(V)
            p.op(V, lambda e: e.tensor_sub(h["df8"][:, 0:TT], h["ts_t"][:, 0:TT],
                                           h["hi_t"][:, 0:TT]))
            p.fence(V)
            p.op(V, lambda e: e.tensor_mul(h["df8"][:, 0:TT], h["mk_t"][:, 0:TT],
                                           h["df8"][:, 0:TT]))
            p.fence(V)
            p.op(V, lambda e, c=c: e.tensor_add(
                h["x8b"][:, c * TT:(c + 1) * TT], h["hi_t"][:, 0:TT],
                h["df8"][:, 0:TT]))
            p.fence(V)
        lastq = p.cnt["m_vector"]
        x8tok = lastq
        p.wait(TE, "m_vector", x8tok)
        for ic in range(NCH):
            par = ic % 2
            off = 0 if par == 0 else TT
            psb = psP0 if par == 0 else psP1
            osem = "s_o0" if par == 0 else "s_o1"
            rsem = "s_r0" if par == 0 else "s_r1"
            if len(evs) >= 2:
                p.wait(TE, "m_scalar", evs[-2])
            for jc in range(NCH):
                p.op(TE, lambda e, ic=ic, jc=jc, psb=psb: e.matmul(
                    psb[:, 0:TT],
                    h["w8b"][:, jc * D + ic * 128: jc * D + (ic + 1) * 128],
                    h["x8b"][:, jc * TT:(jc + 1) * TT],
                    start=(jc == 0), stop=(jc == NCH - 1)),
                    inc=("m_tensor" if jc == NCH - 1 else None))
            mt = p.cnt["m_tensor"]
            p.wait(SC, "m_tensor", mt)
            if lastst[par]:
                p.wait(SC, osem, lastst[par])
            ev2 = p.op(SC, lambda e, ic=ic, psb=psb, off=off: e.activation(
                h["gfb4"][:, off:off + TT], psb[:, 0:TT], AF.Silu,
                bias=h["b1c"][:, ic:ic + 1], scale=h["cols"][:, 1:2]),
                inc="m_scalar")
            evs.append(ev2)
            if lastres[par]:
                p.wait(SY, "m_vector", lastres[par])
            batch_load(g, rsem,
                       [(h["xr"][:, off:off + TT],
                         h["x1"][ic * 128:(ic + 1) * 128, tt * TT:(tt + 1) * TT]),
                        (h["sr"][:, off:off + TT],
                         h["s1"][ic * 128:(ic + 1) * 128, tt * TT:(tt + 1) * TT])],
                       ())
            rld = p.cnt[rsem]
            p.wait(V, "m_scalar", ev2)
            p.wait(V, rsem, rld)
            p.op(V, lambda e, off=off: e.tensor_mul(
                h["orr"][:, off:off + TT], h["sr"][:, off:off + TT],
                h["gfb4"][:, off:off + TT]))
            p.fence(V)
            p.op(V, lambda e, off=off: e.tensor_add(
                h["orr"][:, off:off + TT], h["xr"][:, off:off + TT],
                h["orr"][:, off:off + TT]))
            otok = p.tok(V)
            lastres[par] = otok
            p.wait(GP, "m_vector", otok)
            p.op(GP, lambda e, ic=ic, tt=tt, off=off: e.dma_start(
                out=h["out"][ic * 128:(ic + 1) * 128, tt * TT:(tt + 1) * TT],
                in_=h["orr"][:, off:off + TT]), inc=osem, k=16)
            lastst[par] = p.cnt[osem]


def build_g4():
    g = G("g4")
    emit_g4(g)
    return g


def build_fused(rg=None):
    g = G("fused", fused=True, ar32=36960)
    emit_g1(g, bg_factory=emit_wquant_chunks)
    g.p.megabarrier()
    g.pool_reset()
    emit_g2(g, rg)
    g.p.megabarrier()
    g.pool_reset()
    emit_g3(g)
    g.p.megabarrier()
    g.pool_reset()
    emit_g4(g)
    return g


def _masks():
    import ml_dtypes
    i = np.arange(128)[:, None]
    j = np.arange(TT)[None, :]
    out = []
    for delta in (0, -128, -256, -384):
        out.append(np.where(i - j > delta, np.float32(-1e30), np.float32(0.0)))
    return np.ascontiguousarray(
        np.stack(out).reshape(4 * 128, TT).astype(ml_dtypes.bfloat16))



def _prep_host(inputs):
    t = lambda a: np.ascontiguousarray(np.asarray(a, np.float32).T)
    col = lambda b: np.ascontiguousarray(
        np.asarray(b, np.float32).reshape(NCH, 128).T)
    sh = {}
    sh["wqT"] = t(inputs["L0_attn_Wq"]); sh["wkT"] = t(inputs["L0_attn_Wk"])
    sh["wvT"] = t(inputs["L0_attn_Wv"]); sh["woT"] = t(inputs["L0_attn_Wo"])
    sh["bq"] = col(inputs["L0_attn_bq"]); sh["bk"] = col(inputs["L0_attn_bk"])
    sh["bv"] = col(inputs["L0_attn_bv"]); sh["bo"] = col(inputs["L0_attn_bo"])
    sh["gq"] = col(inputs["L0_attn_norm_q_w"])
    sh["gk"] = col(inputs["L0_attn_norm_k_w"])
    sh["identf32"] = np.eye(128, dtype=np.float32)
    sh["wg"] = np.asarray(inputs["L0_ffn_w_gate"], np.float32)
    sh["wu"] = np.asarray(inputs["L0_ffn_w_up"], np.float32)
    sh["wd"] = np.asarray(inputs["L0_ffn_w_down"], np.float32)
    sh["wgT"] = t(inputs["L0_ffn_w_gate"])
    sh["wuT"] = t(inputs["L0_ffn_w_up"])
    sh["wdT"] = t(inputs["L0_ffn_w_down"])
    sh["wdqT"] = t(inputs["L1_attn_W_dq"])
    sh["wuqT"] = t(inputs["L1_attn_W_uq"])
    sh["wdq_nat"] = np.asarray(inputs["L1_attn_W_dq"], np.float32)
    sh["wuk_nat"] = np.asarray(inputs["L1_attn_W_uk"], np.float32)
    sh["wuv_nat"] = np.asarray(inputs["L1_attn_W_uv"], np.float32)
    sh["woT1"] = t(inputs["L1_attn_W_o"])
    sh["wdkvT"] = t(inputs["L1_attn_W_dkv"])
    sh["masks"] = _masks()
    sh["w8T"] = t(inputs["L1_ffn_weight_fp8"])
    sh["b1"] = col(inputs["L1_ffn_bias"])
    sh["in_s"] = np.asarray(inputs["L1_ffn_input_scale"],
                            np.float32).reshape(1, 1)
    sh["w_s"] = np.asarray(inputs["L1_ffn_weight_scale"],
                           np.float32).reshape(1, 1)
    return sh


def kernel(**inputs):
    x = np.asarray(inputs["L0_attn_x"], np.float32)
    sh = _prep_host(inputs)
    xc = [np.ascontiguousarray(
        x[2 * c:2 * c + 2].reshape(T, D).T) for c in range(NCORES)]
    cores = list(range(NCORES))
    nc = build_fused().finish()
    in_maps = [{"x": xc[c], **sh} for c in cores]
    res = run_bass_kernel_spmd(nc, in_maps, core_ids=cores).results
    out = np.empty((2 * NCORES, S, D), np.float32)
    for c in cores:
        out[2 * c:2 * c + 2] = res[c]["out"].T.reshape(2, S, D)
    return out


def kernel_multi(**inputs):
    x = np.asarray(inputs["L0_attn_x"], np.float32)
    sh = _prep_host(inputs)
    xc = [np.ascontiguousarray(
        x[2 * c:2 * c + 2].reshape(T, D).T) for c in range(NCORES)]
    cores = list(range(NCORES))

    nc1 = build_g1().finish()
    m1 = [{"x": xc[c], **{k: sh[k] for k in
           ("wqT", "wkT", "wvT", "woT", "bq", "bk", "bv", "bo", "gq", "gk",
            "identf32")}} for c in cores]
    r1 = run_bass_kernel_spmd(nc1, m1, core_ids=cores).results

    nc2 = build_g2().finish()
    m2 = [{"x": xc[c], "xn0": r1[c]["xn0"], "s0": r1[c]["s0"],
           **{k: sh[k] for k in ("wg", "wu", "wd", "wgT", "wuT", "wdT",
                                 "identf32")}} for c in cores]
    r2 = run_bass_kernel_spmd(nc2, m2, core_ids=cores).results

    nc3 = build_g3().finish()
    m3 = [{"x1": r2[c]["x1"],
           **{k: sh[k] for k in ("wdqT", "wuqT", "wdq_nat", "wuk_nat",
                                 "wuv_nat", "woT1", "wdkvT", "masks",
                                 "identf32")}} for c in cores]
    r3 = run_bass_kernel_spmd(nc3, m3, core_ids=cores).results

    nc4 = build_g4().finish()
    m4 = [{"x1": r2[c]["x1"], "xn1": r3[c]["xn1"], "s1": r3[c]["s1"],
           **{k: sh[k] for k in ("w8T", "b1", "in_s", "w_s")}} for c in cores]
    r4 = run_bass_kernel_spmd(nc4, m4, core_ids=cores).results

    out = np.empty((2 * NCORES, S, D), np.float32)
    for c in cores:
        out[2 * c:2 * c + 2] = r4[c]["out"].T.reshape(2, S, D)
    return out


# version 10
# speedup vs baseline: 1.0241x; 1.0241x over previous
import numpy as np
from contextlib import ExitStack
import concourse.bass as bass
import concourse.mybir as mybir
from concourse.bass_utils import run_bass_kernel_spmd

F32 = mybir.dt.float32
BF16 = mybir.dt.bfloat16
F16 = mybir.dt.float16
ALU = mybir.AluOpType
AF = mybir.ActivationFunctionType
AX = mybir.AxisListType

NCORES = 8
D = 1024
NCH = 8
T = 4096
S = 2048
NSEQ = 2
NH = 16
HD = 64
I3 = 3072
ICH = 24
QL = 256
TT = 512
NTT = T // TT
NTC = S // TT
MAGIC = 12582912.0
C1V = float(2 ** 20 + 1)
LNEPS = 1e-5
RMSEPS = 1e-6
QMAX = 32767.0
SM_SLACK = 40.0
HALF_LN_D = 0.5 * float(np.log(1024.0))

ENGINES = ("sync", "vector", "scalar", "tensor", "gpsimd")
V, SC, TE, SY, GP = "vector", "scalar", "tensor", "sync", "gpsimd"


class Prog:
    def __init__(self, nc, ctx):
        self.nc = nc
        self.q = {e: [] for e in ENGINES}
        self.sem = {}
        self.cnt = {}
        self._ctx = ctx
        for e in ENGINES:
            self.new_sem("m_" + e)
        for s in ("ld", "ldb", "st", "st2", "cc", "s_prm", "s_cmn", "s_w0",
                  "s_w1", "s_xa", "s_att", "s_vst0", "s_vst1", "s_qn", "s_s0",
                  "s_xn"):
            self.new_sem(s)

    def new_sem(self, name):
        if name in self.sem:
            return name
        self.sem[name] = self._ctx.enter_context(self.nc.semaphore(name))
        self.cnt[name] = 0
        return name

    def megabarrier(self):
        for e in ENGINES:
            for sname in list(self.cnt):
                if not sname.startswith("m_") and self.cnt[sname] > 0:
                    self.wait(e, sname, self.cnt[sname])
        self.barrier()

    def wait(self, eng, sem, val):
        if val <= 0:
            return
        h = self.sem[sem]
        self.q[eng].append(lambda e, h=h, v=val: e.wait_ge(h, v))

    def op(self, eng, fn, inc=None, k=1):
        if inc is not None:
            h = self.sem[inc]
            self.cnt[inc] += k
            self.q[eng].append(lambda e, h=h, k=k, fn=fn: fn(e).then_inc(h, k))
            return self.cnt[inc]
        self.q[eng].append(lambda e, fn=fn: fn(e))
        return None

    def tok(self, eng):
        self.wait(eng, "m_" + eng, self.cnt["m_" + eng])
        if eng == V:
            return self.op(eng, lambda e: e.memset(self.scratch_v[0:1, 0:1], 0.0),
                           inc="m_" + eng)
        if eng == GP:
            return self.op(eng, lambda e: e.memset(self.scratch_g[0:1, 0:1], 0.0),
                           inc="m_" + eng)
        if eng == SC:
            return self.op(eng, lambda e: e.activation(
                self.scratch_s[0:1, 0:1], self.scratch_s[0:1, 1:2], AF.Copy,
                scale=0.0), inc="m_" + eng)
        if eng == TE and self.te_dummy is not None:
            td = self.te_dummy
            return self.op(eng, lambda e, td=td: e.matmul(
                td[0], td[1], td[1], start=True, stop=True), inc="m_" + eng)
        return self.op(eng, lambda e: e.nop(), inc="m_" + eng)

    def fence(self, eng):
        t = self.tok(eng)
        self.wait(eng, "m_" + eng, t)

    def barrier(self):
        toks = {e: self.tok(e) for e in ENGINES}
        for e in ENGINES:
            for o in ENGINES:
                if o != e:
                    self.wait(e, "m_" + o, toks[o])

    def replay(self, block):
        for e in ENGINES:
            def mk(e=e):
                def f(eng):
                    for fn in self.q[e]:
                        fn(eng)
                return f
            getattr(block, e)(mk())


class G:
    def __init__(self, name, fused=False, ar32=0, arbf=0, ar16=0):
        self.nc = bass.Bass("TRN2", target_bir_lowering=False)
        self.ctx = ExitStack()
        self.p = Prog(self.nc, self.ctx)
        self.p.scratch_v = self.ctx.enter_context(
            self.nc.sbuf_tensor("_scr_v", [1, 2], F32))
        self.p.scratch_g = self.ctx.enter_context(
            self.nc.sbuf_tensor("_scr_g", [1, 2], F32))
        self.p.scratch_s = self.ctx.enter_context(
            self.nc.sbuf_tensor("_scr_s", [1, 2], F32))
        self.p.te_dummy = None
        self.h = {}
        self.name = name
        self.fused = fused
        self.ar = None
        self.aroff = 0
        if fused:
            self.ar = self.ctx.enter_context(
                self.nc.sbuf_tensor("_arena", [128, ar32], F32))

    def pool_reset(self):
        self.aroff = 0

    def din(self, name, shape, dtype=F32):
        a = self.nc.dram_tensor(name, list(shape), dtype, kind="ExternalInput").ap()
        self.h[name] = a
        return a

    def dout(self, name, shape, dtype=F32):
        a = self.nc.dram_tensor(name, list(shape), dtype, kind="ExternalOutput").ap()
        self.h[name] = a
        return a

    def dint(self, name, shape, dtype=F32):
        a = self.nc.dram_tensor(name, list(shape), dtype, kind="Internal").ap()
        self.h[name] = a
        return a

    def sb(self, name, shape, dtype=F32, pers=False):
        if not self.fused or pers:
            t = self.ctx.enter_context(
                self.nc.sbuf_tensor(name + ("_p" if self.fused and pers else ""),
                                    list(shape), dtype))
            self.h[name] = t
            return t
        assert len(shape) == 2
        esz = {id(F32): 4, id(BF16): 2, id(F16): 2}[id(dtype)]
        n32 = (shape[1] * esz + 3) // 4
        off = self.aroff
        ncap = self.ar.shape[1]
        assert off + n32 <= ncap, (
            f"arena overflow: {name} needs {n32} f32 at {off}, cap {ncap}")
        if esz == 4:
            v = self.ar[0:shape[0], off:off + shape[1]]
        else:
            v = self.ar[:, off:off + n32].bitcast(dtype)[0:shape[0], 0:shape[1]]
            self.h[name + "_f32raw"] = self.ar[:, off:off + n32]
        self.aroff = (off + n32 + 7) // 8 * 8
        self.h[name] = v
        return v

    def din_once(self, name, shape, dtype=F32):
        if name in self.h:
            return self.h[name]
        return self.din(name, shape, dtype)

    def dio_out(self, name, shape, dtype=F32):
        return (self.dint if self.fused else self.dout)(name, shape, dtype)

    def dio_in(self, name, shape, dtype=F32):
        if self.fused:
            return self.h[name]
        return self.din(name, shape, dtype)

    def ps(self, name, shape, dtype=F32):
        t = self.ctx.enter_context(self.nc.psum_tensor(name, list(shape), dtype))
        self.h[name] = t
        return t

    def finish(self):
        p = self.p
        for sem in list(p.cnt):
            if not sem.startswith("m_") and p.cnt[sem] > 0:
                p.wait(GP, sem, p.cnt[sem])
        block = self.ctx.enter_context(self.nc.Block())
        p.replay(block)
        self.ctx.close()
        return self.nc



def load_wT_bf16(g, src, dst, jchunks, cols):
    p, h = g.p, g.h
    st = h["wstage"]
    half = st.shape[1] // 2
    assert cols <= half
    tok = 0
    toks = []
    prev = getattr(g, "_wstage_prev", {0: 0, 1: 0})
    for jc in range(jchunks):
        off = 0 if jc % 2 == 0 else half
        sem = "s_w0" if jc % 2 == 0 else "s_w1"
        if jc >= 2:
            p.wait(SY, "m_vector", toks[jc - 2])
        elif prev[jc % 2]:
            p.wait(SY, "m_vector", prev[jc % 2])
        ldt = p.op(SY, lambda e, jc=jc, off=off: e.dma_start(
            out=st[:, off:off + cols], in_=src[jc * 128:(jc + 1) * 128, :]),
            inc=sem, k=16)
        p.wait(V, sem, ldt)
        tok = p.op(V, lambda e, jc=jc, off=off: e.tensor_copy(
            dst[:, jc * cols:(jc + 1) * cols], st[:, off:off + cols]),
            inc="m_vector")
        toks.append(tok)
    g._wstage_prev = {0: toks[-2] if jchunks >= 2 else toks[-1],
                      1: toks[-1] if jchunks >= 2 else prev[1]}
    return tok


def layernorm_tile(g, xa, xn, xnb, psA, psB, psC=None):
    p, h = g.p, g.h
    sq = h["sq"]
    rows = h["rows"]
    r0 = rows[0:1, 0:TT]
    r1 = rows[0:1, TT:2 * TT]
    r2 = rows[0:1, 2 * TT:3 * TT]
    for c in range(NCH):
        p.op(SC, lambda e, c=c, sq=sq: e.activation(
            sq[:, c * TT:(c + 1) * TT], xa[:, c * TT:(c + 1) * TT], AF.Square))
    sqtok = p.tok(SC)
    p.wait(TE, "m_scalar", sqtok)
    for c in range(NCH):
        p.op(TE, lambda e, c=c: e.matmul(psA[0:1, 0:TT], h["ones_col"][:, 0:1],
                                         xa[:, c * TT:(c + 1) * TT],
                                         start=(c == 0), stop=(c == NCH - 1)))
    for c in range(NCH):
        p.op(TE, lambda e, c=c, sq=sq: e.matmul(
            psC[0:1, 0:TT], h["ones_col"][:, 0:1], sq[:, c * TT:(c + 1) * TT],
            start=(c == 0), stop=(c == NCH - 1)),
             inc=("m_tensor" if c == NCH - 1 else None))
    stat_tok = p.cnt["m_tensor"]
    p.wait(V, "m_tensor", stat_tok)
    p.op(V, lambda e: e.tensor_scalar(r0, psA[0:1, 0:TT], 1.0 / D, None, ALU.mult))
    p.op(V, lambda e: e.tensor_scalar(r1, psC[0:1, 0:TT], 1.0 / D, None, ALU.mult))
    p.fence(V)
    p.op(V, lambda e: e.tensor_mul(r2, r0, r0))
    p.fence(V)
    vtok = p.op(V, lambda e: e.tensor_sub(r1, r1, r2), inc="m_vector")
    p.wait(SC, "m_vector", vtok)
    p.op(SC, lambda e: e.activation(r2, r1, AF.Ln, bias=h["cst"][0:1, 0:1]))
    p.fence(SC)
    rtok = p.op(SC, lambda e: e.activation(r1, r2, AF.Exp, scale=-0.5),
                inc="m_scalar")
    p.wait(V, "m_scalar", rtok)
    mrtok = p.op(V, lambda e: e.tensor_mul(r2, r0, r1), inc="m_vector")
    p.wait(TE, "m_vector", mrtok)
    p.op(TE, lambda e: e.matmul(psB[:, 0:TT], h["ones_row"][0:1, :], r1,
                                start=True, stop=True))
    bctok = p.op(TE, lambda e: e.matmul(psB[:, TT:2 * TT], h["ones_row"][0:1, :],
                                        r2, start=True, stop=True),
                 inc="m_tensor")
    p.wait(V, "m_tensor", bctok)
    for c in range(NCH):
        p.op(V, lambda e, c=c: e.tensor_mul(xn[:, c * TT:(c + 1) * TT],
                                            xa[:, c * TT:(c + 1) * TT],
                                            psB[:, 0:TT]))
    p.fence(V)
    for c in range(NCH):
        p.op(V, lambda e, c=c: e.tensor_sub(xn[:, c * TT:(c + 1) * TT],
                                            xn[:, c * TT:(c + 1) * TT],
                                            psB[:, TT:2 * TT]))
    xn_tok = p.tok(V)
    p.wait(SC, "m_vector", xn_tok)
    for c in range(NCH):
        p.op(SC, lambda e, c=c: e.activation(xnb[:, c * TT:(c + 1) * TT],
                                             xn[:, c * TT:(c + 1) * TT], AF.Copy))
    xnb_tok = p.tok(SC)
    return xn_tok, xnb_tok


def bcast_scalar_col(g, table, row, col, dstcol, scale=1.0, psname="psBC"):
    p, h = g.p, g.h
    tk = p.op(TE, lambda e: e.matmul(h[psname][:, 0:1], h["ones_row"][0:1, :],
                                     table[row:row + 1, col:col + 1],
                                     start=True, stop=True), inc="m_tensor")
    p.wait(SC, "m_tensor", tk)
    return p.op(SC, lambda e: e.activation(dstcol[:, 0:1], h[psname][:, 0:1],
                                           AF.Copy, scale=scale), inc="m_scalar")



def emit_g1(g, bg_factory=None):
    p, h = g.p, g.h
    ps = common_setup(g)
    g.din_once("x", [D, T])
    for n in ("wqT", "wkT", "wvT", "woT"):
        g.din(n, [D, D])
    for n in ("bq", "bk", "bv", "bo", "gq", "gk"):
        g.din(n, [128, NCH])
    g.dio_out("xn0", [D, T])
    g.dio_out("s0", [D, T])
    q0 = g.dint("q0", [D, T], BF16)
    shdram = g.dint("shd", [2, 16])
    k0 = g.dint("k0", [D, T], BF16)
    v0 = g.dint("v0", [T, NH * 65], BF16)
    bvr = g.dint("bvr", [8, 128])

    g.sb("sq", [128, NCH * TT])
    for n in ("wqTb", "wkTb", "wvTb"):
        g.sb(n, [128, NCH * D], BF16)
    h["woTb"] = h["wvTb"]
    h["xa1"] = h["xa0"]
    g.sb("tmp8", [8, 128])
    g.sb("qf", [128, NCH * TT])
    g.sb("qnb", [128, NCH * TT], BF16)
    g.sb("ntabq", [2, 16]); g.sb("ntabk", [2, 16])
    g.sb("shift", [2, 16])
    g.sb("bvbc", [128, D])
    g.sb("vst0", [128, NH * 65], BF16); g.sb("vst1", [128, NH * 65], BF16)
    g.sb("qh", [64, S], BF16); g.sb("kh", [64, S], BF16)
    g.sb("vh", [128, 16 * 65], BF16)
    g.sb("et0", [128, TT], BF16); g.sb("et1", [128, TT], BF16)
    g.sb("et2", [128, TT], BF16)
    g.sb("ytm", [128, 64], BF16)
    g.sb("yfm", [128, NCH * S], BF16)
    g.sb("s0f", [128, TT])
    g.sb("shcol", [128, 1])
    g.sb("rcol", [128, 1])
    for n in ("bqc", "bkc", "bvc", "boc", "gqc", "gkc"):
        g.sb(n, [128, NCH])

    psQ0, psQ1, psA = ps[0], ps[1], ps[2]
    psS0, psS1, psO = ps[3], ps[4], ps[5]
    psB = ps["B"]
    psTb = psB[:, 0:64].bitcast(BF16)

    for n_in, n_c in (("bq", "bqc"), ("bk", "bkc"), ("bv", "bvc"), ("bo", "boc"),
                      ("gq", "gqc"), ("gk", "gkc")):
        p.op(SY, lambda e, a=h[n_in], b=h[n_c]: e.dma_start(out=b[:], in_=a),
             inc="s_prm", k=16)
    p.wait(V, "s_prm", p.cnt["s_prm"])
    p.op(V, lambda e: e.memset(h["ntabq"][:], 0.0))
    p.op(V, lambda e: e.memset(h["ntabk"][:], 0.0))
    p.op(V, lambda e: e.memset(h["mxc"][:], 0.0))
    p.op(V, lambda e: e.memset(h["vst0"][:], 0.0))
    p.op(V, lambda e: e.memset(h["vst1"][:], 0.0))
    p.fence(V)
    for b in ("vst0", "vst1"):
        p.op(V, lambda e, b=b: e.memset(
            h[b][:].rearrange("p (nh c) -> p nh c", c=65)[:, :, 0:1], 1.0))
    cst = p.tok(V)
    p.wait(TE, "m_vector", cst)
    p.wait(SC, "m_vector", cst)

    tk = p.op(TE, lambda e: e.transpose(psA[0:8, 0:128], h["bvc"][:],
                                        h["identf"][:]), inc="m_tensor")
    p.wait(V, "m_tensor", tk)
    vk = p.op(V, lambda e: e.tensor_copy(h["tmp8"][:], psA[0:8, 0:128]),
              inc="m_vector")
    p.wait(GP, "m_vector", vk)
    s1t = p.op(GP, lambda e: e.dma_start(out=bvr, in_=h["tmp8"][:]), inc="st2", k=16)
    p.wait(SY, "st2", s1t)
    bvt = p.op(SY, lambda e: e.dma_start(
        out=h["bvbc"][:],
        in_=bvr.rearrange("a b -> () (a b)").broadcast_to([128, D])), inc="ldb", k=16)
    p.wait(V, "ldb", bvt)

    for n in ("wqT", "wkT", "wvT"):
        load_wT_bf16(g, h[n], h[n + "b"], NCH, D)
    wtok = p.cnt["m_vector"]
    p.wait(TE, "m_vector", wtok)

    xn_toks = [0] * NTT
    sq_toks = [0] * NTT
    cons_toks = [0] * NTT
    qk_evtoks = []
    vst_last = {0: 0, 1: 0}
    prev_vev3 = 0
    prev_xn_store = 0
    prev_qn_tok = 0
    prev_qn_store = 0
    prev_sq_tok = 0

    for tt in range(NTT):
        xa = h["xa0"] if tt % 2 == 0 else h["xa1"]
        seq = tt // 4
        if tt >= 1:
            p.wait(SY, "m_vector", xn_toks[tt - 1])
            p.wait(SY, "m_scalar", sq_toks[tt - 1])
        for c in range(NCH):
            p.op(SY, lambda e, c=c, tt=tt, xa=xa: e.dma_start(
                out=xa[:, c * TT:(c + 1) * TT],
                in_=h["x"][c * 128:(c + 1) * 128, tt * TT:(tt + 1) * TT]),
                inc="s_xa", k=16)
        xld = p.cnt["s_xa"]
        p.wait(SC, "s_xa", xld)
        p.wait(V, "s_xa", xld)
        if tt >= 1:
            p.wait(V, "m_tensor", cons_toks[tt - 1])
            p.wait(SC, "m_tensor", cons_toks[tt - 1])
            p.wait(TE, "m_vector", prev_vev3)
            p.wait(V, "s_xn", prev_xn_store)
        xn_tok, xnb_tok = layernorm_tile(g, xa, h["xn"], h["xnb"], psA, psB,
                                         psC=psS0)
        xn_toks[tt] = xn_tok
        sq_toks[tt] = xnb_tok
        p.wait(GP, "m_vector", xn_tok)
        for c in range(NCH):
            p.op(GP, lambda e, c=c, tt=tt: e.dma_start(
                out=h["xn0"][c * 128:(c + 1) * 128, tt * TT:(tt + 1) * TT],
                in_=h["xn"][:, c * TT:(c + 1) * TT]), inc="s_xn", k=16)
        prev_xn_store = p.cnt["s_xn"]
        for c in range(NCH):
            p.op(V, lambda e, c=c: e.tensor_reduce(
                h["tmpcol"][:, 0:1], h["xn"][:, c * TT:(c + 1) * TT], AX.X,
                ALU.max, apply_absolute_value=True))
            p.fence(V)
            p.op(V, lambda e: e.tensor_max(h["mxc"][:, 0:1], h["mxc"][:, 0:1],
                                           h["tmpcol"][:, 0:1]))
            p.fence(V)

        p.wait(TE, "m_scalar", xnb_tok)
        evtoks = qk_evtoks
        for (wb, bc, gc, ntab, dram) in (("wqTb", "bqc", "gqc", "ntabq", q0),
                                         ("wkTb", "bkc", "gkc", "ntabk", k0)):
            for ic in range(NCH):
                psb = psQ0 if ic % 2 == 0 else psQ1
                if len(evtoks) >= 2:
                    p.wait(TE, "m_scalar", evtoks[-2])
                for jc in range(NCH):
                    p.op(TE, lambda e, ic=ic, jc=jc, wb=wb, psb=psb: e.matmul(
                        psb[:, 0:TT],
                        h[wb][:, jc * D + ic * 128: jc * D + (ic + 1) * 128],
                        h["xnb"][:, jc * TT:(jc + 1) * TT],
                        start=(jc == 0), stop=(jc == NCH - 1)),
                        inc=("m_tensor" if jc == NCH - 1 else None))
                mt = p.cnt["m_tensor"]
                p.wait(SC, "m_tensor", mt)
                if prev_qn_tok:
                    p.wait(SC, "m_vector", prev_qn_tok)
                ev = p.op(SC, lambda e, ic=ic, psb=psb, bc=bc: e.activation(
                    h["qf"][:, ic * TT:(ic + 1) * TT], psb[:, 0:TT], AF.Identity,
                    bias=h[bc][:, ic:ic + 1]), inc="m_scalar")
                evtoks.append(ev)
            qf_tok = evtoks[-1]
            p.wait(V, "m_scalar", qf_tok)
            for ic in range(NCH):
                p.op(V, lambda e, ic=ic, sq=h["sq"]: e.tensor_mul(
                    sq[:, ic * TT:(ic + 1) * TT],
                    h["qf"][:, ic * TT:(ic + 1) * TT],
                    h["qf"][:, ic * TT:(ic + 1) * TT]))
            sq_all = p.tok(V)
            p.wait(TE, "m_vector", sq_all)
            for ic in range(NCH):
                p.op(TE, lambda e, ic=ic, sq=h["sq"]: e.matmul(
                    psA[0:1, 0:TT], h["ones_col"][:, 0:1],
                    sq[:, ic * TT:(ic + 1) * TT],
                    start=(ic == 0), stop=(ic == NCH - 1)),
                    inc=("m_tensor" if ic == NCH - 1 else None))
            ssum = p.cnt["m_tensor"]
            p.wait(SC, "m_tensor", ssum)
            p.op(SC, lambda e: e.activation(h["rows"][0:1, 2 * TT:3 * TT],
                                            psA[0:1, 0:TT],
                                            AF.Ln, bias=h["cst"][0:1, 1:2]))
            p.fence(SC)
            rt = p.op(SC, lambda e: e.activation(h["rows"][0:1, TT:2 * TT],
                                                 h["rows"][0:1, 2 * TT:3 * TT],
                                                 AF.Exp,
                                                 scale=-0.5, bias=h["cst"][0:1, 2:3]),
                      inc="m_scalar")
            p.wait(TE, "m_scalar", rt)
            if prev_qn_tok:
                p.wait(TE, "m_vector", prev_qn_tok)
            bt = p.op(TE, lambda e: e.matmul(psB[:, 0:TT], h["ones_row"][0:1, :],
                                             h["rows"][0:1, TT:2 * TT], start=True,
                                             stop=True), inc="m_tensor")
            p.wait(V, "m_tensor", bt)
            if prev_qn_store:
                p.wait(V, "s_qn", prev_qn_store)
            if prev_sq_tok:
                p.wait(V, "m_scalar", prev_sq_tok)
            for ic in range(NCH):
                p.op(V, lambda e, ic=ic, gc=gc: e.scalar_tensor_tensor(
                    h["qnb"][:, ic * TT:(ic + 1) * TT],
                    h["qf"][:, ic * TT:(ic + 1) * TT], h[gc][:, ic:ic + 1],
                    psB[:, 0:TT], ALU.mult, ALU.mult))
            qn_tok = p.tok(V)
            prev_qn_tok = qn_tok
            p.wait(GP, "m_vector", qn_tok)
            for ic in range(NCH):
                p.op(GP, lambda e, ic=ic, tt=tt, dram=dram: e.dma_start(
                    out=dram[ic * 128:(ic + 1) * 128, tt * TT:(tt + 1) * TT],
                    in_=h["qnb"][:, ic * TT:(ic + 1) * TT]), inc="s_qn", k=16)
            prev_qn_store = p.cnt["s_qn"]
            p.wait(SC, "m_vector", qn_tok)
            for ic in range(NCH):
                sqt = p.op(SC, lambda e, ic=ic, sq=h["sq"]: e.activation(
                    sq[:, ic * TT:(ic + 1) * TT],
                    h["qnb"][:, ic * TT:(ic + 1) * TT], AF.Square),
                    inc="m_scalar")
                p.wait(TE, "m_scalar", sqt)
                pss = psS0 if ic % 2 == 0 else psS1
                hm = p.op(TE, lambda e, ic=ic, pss=pss, sq=h["sq"]: e.matmul(
                    pss[0:2, 0:TT], h["hmask"][:, 0:2],
                    sq[:, ic * TT:(ic + 1) * TT], start=True, stop=True),
                    inc="m_tensor")
                p.wait(V, "m_tensor", hm)
                p.op(V, lambda e, pss=pss: e.tensor_reduce(
                    h["tmp2"][0:2, 0:1], pss[0:2, 0:TT], AX.X, ALU.max))
                p.fence(V)
                vred = p.op(V, lambda e, ic=ic, ntab=ntab, seq=seq: e.tensor_max(
                    h[ntab][0:2, (seq * NCH + ic):(seq * NCH + ic) + 1],
                    h[ntab][0:2, (seq * NCH + ic):(seq * NCH + ic) + 1],
                    h["tmp2"][0:2, 0:1]), inc="m_vector")
                p.fence(V)
                p.wait(TE, "m_vector", vred)
            prev_sq_tok = p.cnt["m_scalar"]

        vstoks = [0] * 4
        vev_toks = [0] * 4
        for sub in range(4):
            vst = h["vst0"] if sub % 2 == 0 else h["vst1"]
            for ihalf in range(2):
                psb = psS0 if ihalf == 0 else psS1
                if sub >= 1:
                    p.wait(TE, "m_vector", vev_toks[sub - 1])
                for jc in range(NCH):
                    p.op(TE, lambda e, sub=sub, ihalf=ihalf, jc=jc, psb=psb:
                         e.matmul(psb[:, 0:TT],
                                  h["xnb"][:, jc * TT + sub * 128:
                                           jc * TT + (sub + 1) * 128],
                                  h["wvTb"][:, jc * D + ihalf * TT:
                                            jc * D + (ihalf + 1) * TT],
                                  start=(jc == 0), stop=(jc == NCH - 1)),
                         inc=("m_tensor" if jc == NCH - 1 else None))
            vmt = p.cnt["m_tensor"]
            p.wait(V, "m_tensor", vmt)
            if vst_last[sub % 2]:
                p.wait(V, "s_vst0" if sub % 2 == 0 else "s_vst1",
                       vst_last[sub % 2])
            for ihalf in range(2):
                psb = psS0 if ihalf == 0 else psS1
                p.op(V, lambda e, ihalf=ihalf, psb=psb, vst=vst:
                     e.scalar_tensor_tensor(
                         vst[:].rearrange("p (nh c) -> p nh c", c=65)
                         [:, ihalf * 8:(ihalf + 1) * 8, 1:65],
                         psb[:, 0:TT].rearrange("p (nh c) -> p nh c", c=64),
                         1.0,
                         h["bvbc"][:, ihalf * TT:(ihalf + 1) * TT].rearrange(
                             "p (nh c) -> p nh c", c=64),
                         ALU.mult, ALU.add))
            vtok = p.tok(V)
            vev_toks[sub] = vtok
            p.wait(GP, "m_vector", vtok)
            vstoks[sub] = p.op(GP, lambda e, tt=tt, sub=sub, vst=vst: e.dma_start(
                out=v0[tt * TT + sub * 128: tt * TT + (sub + 1) * 128, :],
                in_=vst[:]), inc="s_vst0" if sub % 2 == 0 else "s_vst1", k=16)
            vst_last[sub % 2] = vstoks[sub]
        cons_toks[tt] = p.cnt["m_tensor"]
        prev_vev3 = vev_toks[3]

    p.op(V, lambda e: e.tensor_mul(h["shift"][0:2, 0:16], h["ntabq"][0:2, 0:16],
                                   h["ntabk"][0:2, 0:16]))
    sht = p.tok(V)
    p.wait(SC, "m_vector", sht)
    p.op(SC, lambda e: e.activation(h["shift"][0:2, 0:16], h["shift"][0:2, 0:16],
                                    AF.Ln, bias=h["cst"][0:2, 3:4]))
    p.fence(SC)
    sh2 = p.op(SC, lambda e: e.activation(h["shift"][0:2, 0:16],
                                          h["shift"][0:2, 0:16], AF.Exp,
                                          scale=0.5), inc="m_scalar")
    p.wait(V, "m_scalar", sh2)
    p.op(V, lambda e: e.tensor_scalar(h["shift"][0:2, 0:16],
                                      h["shift"][0:2, 0:16],
                                      0.125, -SM_SLACK, ALU.mult, ALU.add))
    p.fence(V)
    p.op(V, lambda e: e.tensor_scalar(h["shift"][0:2, 0:16],
                                      h["shift"][0:2, 0:16], 0.0, None, ALU.max))
    p.fence(V)
    shneg = p.op(V, lambda e: e.tensor_scalar(h["shift"][0:2, 0:16],
                                              h["shift"][0:2, 0:16], -1.0, None,
                                              ALU.mult), inc="m_vector")
    p.wait(GP, "m_vector", shneg)
    p.op(GP, lambda e: e.dma_start(out=shdram, in_=h["shift"][0:2, 0:16]),
         inc="st2", k=16)
    p.barrier()
    for sem in ("st", "st2", "s_qn", "s_vst0", "s_vst1", "s_xn"):
        p.wait(SY, sem, p.cnt[sem])
    wo_tok = load_wT_bf16(g, h["woT"], h["woTb"], NCH, D)
    p.wait(TE, "m_vector", wo_tok)

    bg_pairs = list(bg_factory(g)) if bg_factory else []
    bg_loaded = []
    for _ in range(2):
        if bg_pairs:
            pr = bg_pairs.pop(0)
            pr[0]()
            bg_loaded.append(pr)

    out_banks = [psO, psA, psQ0, psQ1]
    psS2 = psB[:, 512:1024]
    sc_last = {0: 0, 1: 0, 2: 0}
    et_last = {0: 0, 1: 0, 2: 0}
    wo_last_ev = 0
    wo_evs = []
    wo_sts = 0
    for s in range(NSEQ):
        for hd in range(NH):
            ic, halfp = hd // 2, hd % 2
            if s > 0 and hd == 0:
                p.wait(TE, "m_scalar", wo_last_ev)
            if s + hd > 0:
                p.wait(SY, "m_tensor", p.cnt["m_tensor"])
            p.op(SY, lambda e, s=s, hd=hd: e.dma_start(
                out=h["qh"][:], in_=q0[hd * 64:(hd + 1) * 64, s * S:(s + 1) * S]),
                inc="s_att", k=16)
            p.op(SY, lambda e, s=s, hd=hd: e.dma_start(
                out=h["kh"][:], in_=k0[hd * 64:(hd + 1) * 64, s * S:(s + 1) * S]),
                inc="s_att", k=16)
            p.op(SY, lambda e, s=s, hd=hd: e.dma_start(
                out=h["vh"][:].rearrange("p (kt c) -> p kt c", c=65),
                in_=v0[s * S:(s + 1) * S, hd * 65:(hd + 1) * 65]
                .rearrange("(kt p) c -> p kt c", p=128)), inc="s_att", k=16)
            p.op(SY, lambda e, halfp=halfp, s=s, ic=ic: e.dma_start(
                out=h["shcol"][:, 0:1],
                in_=shdram[halfp:halfp + 1, s * NCH + ic: s * NCH + ic + 1]
                .broadcast_to([128, 1])), inc="s_att", k=16)
            hld = p.cnt["s_att"]
            p.wait(TE, "s_att", hld)
            p.wait(SC, "s_att", hld)
            for tc in range(NTC):
                ext_toks = {}
                out_toks = {}
                sc_banks = [psS0, psS1, psS2]
                et_bufs = [h["et0"], h["et1"], h["et2"]]

                def emit_outs(kt):
                    et = et_bufs[kt % 3]
                    p.wait(TE, "m_scalar", ext_toks[kt])
                    for qs in range(4):
                        ot = p.op(TE, lambda e, kt=kt, qs=qs, et=et,
                                  ob=out_banks[qs]: e.matmul(
                            ob[:, 0:65],
                            et[:, qs * 128:(qs + 1) * 128],
                            h["vh"][:, kt * 65:(kt + 1) * 65],
                            start=(kt == 0), stop=(kt == 15)),
                            inc=("m_tensor" if qs == 3 else None))
                    out_toks[kt] = ot

                for kt in range(16):
                    pss = sc_banks[kt % 3]
                    et = et_bufs[kt % 3]
                    if sc_last[kt % 3]:
                        p.wait(TE, "m_scalar", sc_last[kt % 3])
                    smt = p.op(TE, lambda e, kt=kt, tc=tc, pss=pss: e.matmul(
                        pss[:, 0:TT], h["kh"][:, kt * 128:(kt + 1) * 128],
                        h["qh"][:, tc * TT:(tc + 1) * TT], start=True, stop=True),
                        inc="m_tensor")
                    p.wait(SC, "m_tensor", smt)
                    if et_last[kt % 3]:
                        p.wait(SC, "m_tensor", et_last[kt % 3])
                    ext = p.op(SC, lambda e, pss=pss, et=et: e.activation(
                        et[:], pss[:, 0:TT], AF.Exp, bias=h["shcol"][:, 0:1],
                        scale=0.125), inc="m_scalar")
                    ext_toks[kt] = ext
                    sc_last[kt % 3] = ext
                    if kt >= 2:
                        emit_outs(kt - 2)
                        et_last[(kt - 2) % 3] = out_toks[kt - 2]
                emit_outs(14)
                et_last[14 % 3] = out_toks[14]
                emit_outs(15)
                et_last[15 % 3] = out_toks[15]
                p.wait(V, "m_tensor", out_toks[15])
                yts = []
                for qs in range(4):
                    p.op(V, lambda e, qs=qs, ob=out_banks[qs]: e.reciprocal(
                        h["rcol"][:, 0:1], ob[:, 0:1]))
                    p.fence(V)
                    yt = p.op(V, lambda e, qs=qs, ob=out_banks[qs]: e.tensor_scalar(
                        h["ytm"][:, 0:64], ob[:, 1:65],
                        h["rcol"][:, 0:1], None, ALU.mult), inc="m_vector")
                    yts.append(yt)
                    p.wait(TE, "m_vector", yt)
                    tp = p.op(TE, lambda e: e.transpose(
                        psTb[0:64, 0:128], h["ytm"][:, 0:64], h["identb"][:]),
                        inc="m_tensor")
                    p.wait(SC, "m_tensor", tp)
                    cp = p.op(SC, lambda e, tc=tc, qs=qs, ic=ic, halfp=halfp:
                              e.activation(
                                  h["yfm"][halfp * 64:(halfp + 1) * 64,
                                           ic * S + tc * TT + qs * 128:
                                           ic * S + tc * TT + (qs + 1) * 128],
                                  psTb[0:64, 0:128], AF.Copy), inc="m_scalar")
                    p.wait(TE, "m_scalar", cp)
                    p.wait(V, "m_scalar", cp)
                    p.wait(V, "m_vector", yt)
            prev_ld = bg_loaded
            bg_loaded = []
            for pr in prev_ld:
                pr[1]()
            for _ in range(2):
                if bg_pairs:
                    pr = bg_pairs.pop(0)
                    pr[0]()
                    bg_loaded.append(pr)
        yfm_tok = p.cnt["m_scalar"]
        p.wait(TE, "m_scalar", yfm_tok)
        for oc in range(NCH):
            for tc in range(NTC):
                it = oc * NTC + tc
                psb = psQ0 if it % 2 == 0 else psQ1
                if len(wo_evs) >= 2:
                    p.wait(TE, "m_scalar", wo_evs[-2])
                for jc in range(NCH):
                    p.op(TE, lambda e, oc=oc, tc=tc, jc=jc, psb=psb: e.matmul(
                        psb[:, 0:TT],
                        h["woTb"][:, jc * D + oc * 128: jc * D + (oc + 1) * 128],
                        h["yfm"][:, jc * S + tc * TT: jc * S + (tc + 1) * TT],
                        start=(jc == 0), stop=(jc == NCH - 1)),
                        inc=("m_tensor" if jc == NCH - 1 else None))
                wt_ = p.cnt["m_tensor"]
                p.wait(SC, "m_tensor", wt_)
                if wo_evs:
                    p.wait(SC, "s_s0", wo_sts)
                ev = p.op(SC, lambda e, oc=oc, psb=psb: e.activation(
                    h["s0f"][:, 0:TT], psb[:, 0:TT], AF.Identity,
                    bias=h["boc"][:, oc:oc + 1]), inc="m_scalar")
                wo_evs.append(ev)
                wo_last_ev = ev
                p.wait(GP, "m_scalar", ev)
                wo_sts = p.op(GP, lambda e, oc=oc, s=s, tc=tc: e.dma_start(
                    out=h["s0"][oc * 128:(oc + 1) * 128,
                                s * S + tc * TT: s * S + (tc + 1) * TT],
                    in_=h["s0f"][:, 0:TT]), inc="s_s0", k=16)
    for pr in bg_loaded:
        pr[1]()
    while bg_pairs:
        pr = bg_pairs.pop(0)
        pr[0]()
        pr[1]()


def build_g1():
    g = G("g1")
    emit_g1(g)
    return g



def batch_load(g, sem, pairs, waiters=()):
    p = g.p
    for dst, src in pairs:
        p.op(SY, lambda e, dst=dst, src=src: e.dma_start(out=dst, in_=src),
             inc=sem, k=16)
    tok = p.cnt[sem]
    for e in waiters:
        p.wait(e, sem, tok)
    return tok


def load_cast_bf16(g, src, dst, jchunks, cols):
    return load_wT_bf16(g, src, dst, jchunks, cols)


def colmax_to_scal(g, col, idx, psT):
    p, h = g.p, g.h
    t = p.tok(V)
    p.wait(TE, "m_vector", t)
    tk = p.op(TE, lambda e: e.transpose(psT[0:1, 0:128], col[:, 0:1],
                                        h["identf"][:]), inc="m_tensor")
    p.wait(V, "m_tensor", tk)
    p.op(V, lambda e, idx=idx: e.tensor_reduce(
        h["scal"][0:1, idx:idx + 1], psT[0:1, 0:128], AX.X, ALU.max))
    p.fence(V)


def scal_to_cols(g, idxs):
    p, h = g.p, g.h
    t = p.tok(V)
    p.wait(GP, "m_vector", t)
    st = p.op(GP, lambda e: e.dma_start(out=h["scald"], in_=h["scal"][0:1, 0:64]),
              inc="st2", k=16)
    p.wait(SY, "st2", st)
    for si, ci, sc in idxs:
        p.op(SY, lambda e, si=si, ci=ci: e.dma_start(
            out=h["colsr"][:, ci:ci + 1],
            in_=h["scald"][0:1, si:si + 1].broadcast_to([128, 1])),
            inc="s_col", k=16)
    tok = p.cnt["s_col"]
    p.wait(V, "s_col", tok)
    toks = []
    for si, ci, sc in idxs:
        if sc != 1.0:
            p.op(V, lambda e, ci=ci, sc=sc: e.tensor_scalar(
                h["cols"][:, ci:ci + 1], h["colsr"][:, ci:ci + 1], sc, None,
                ALU.mult))
        else:
            p.op(V, lambda e, ci=ci: e.tensor_copy(h["cols"][:, ci:ci + 1],
                                                   h["colsr"][:, ci:ci + 1]))
    return p.tok(V)


def exchange_max(g, n, rg):
    p, h = g.p, g.h
    t = p.tok(V)
    p.wait(GP, "m_vector", t)
    p.op(GP, lambda e, n=n: e.dma_start(out=h[f"cci{n}"], in_=h["ccst"][:]),
         inc="st2", k=16)
    p.wait(GP, "st2", p.cnt["st2"])
    p.op(GP, lambda e, n=n: e.collective_compute(
        "AllReduce", ALU.max, rg, [h[f"cci{n}"].opt()], [h[f"cco{n}"].opt()]),
        inc="cc", k=1)
    p.wait(GP, "cc", p.cnt["cc"])
    ld = p.op(GP, lambda e, n=n: e.dma_start(out=h["ccst"][:], in_=h[f"cco{n}"]),
              inc="s_cc", k=16)
    p.wait(V, "s_cc", ld)
    tok = p.op(V, lambda e: e.tensor_copy(h["scal"][0:1, 32:48],
                                          h["ccst"][0:1, 0:16]), inc="m_vector")
    p.wait(V, "m_vector", tok)
    return tok


def common_setup(g):
    if "xa0" in g.h:
        return g.h["_ps"]
    for n, shp, dt_ in (("xa0", [128, NCH * TT], F32), ("xn", [128, NCH * TT], F32),
                        ("xnb", [128, NCH * TT], BF16),
                        ("rows", [1, 3 * TT], F32), ("ones_col", [128, 1], F32),
                        ("ones_row", [1, 128], F32), ("cst", [128, 4], F32),
                        ("tmpcol", [128, 1], F32), ("tmp2", [2, 2], F32),
                        ("wstage", [128, 2048], F32), ("identf", [128, 128], F32),
                        ("identb", [128, 128], BF16), ("scal", [1, 64], F32),
                        ("cols", [128, 16], F32), ("colsr", [128, 16], F32),
                        ("hmask", [128, 2], F32), ("ccst", [16, 128], F32),
                        ("mxc", [128, 1], F32)):
        g.sb(n, shp, dt_, pers=True)
    psB = g.ctx.enter_context(g.nc.psum_tensor("PSB", [128, 1024], F32))
    PS = [g.ctx.enter_context(g.nc.psum_tensor(f"PS{i}", [128, 512], F32))
          for i in range(6)]
    g.h["PSB"] = psB
    for i, t in enumerate(PS):
        g.h[f"PS{i}"] = t
    ps = {"B": psB, **{i: PS[i] for i in range(6)}}
    g.h["_ps"] = ps
    g.din_once("identf32", [128, 128])
    p = g.p
    ln_consts(g)
    p.op(V, lambda e: e.memset(g.h["scal"][:], 0.0))
    p.op(V, lambda e: e.memset(g.h["cols"][:], 0.0))
    p.op(V, lambda e: e.memset(g.h["colsr"][:], 0.0))
    p.op(V, lambda e: e.memset(g.h["ccst"][:], 0.0))
    p.op(V, lambda e: e.memset(g.h["hmask"][:], 0.0))
    p.fence(V)
    p.op(V, lambda e: e.memset(g.h["hmask"][0:64, 0:1], 1.0))
    p.op(V, lambda e: e.memset(g.h["hmask"][64:128, 1:2], 1.0))
    c0 = p.tok(V)
    p.wait(SY, "m_vector", c0)
    batch_load(g, "s_cmn", [(g.h["identf"][:], g.h["identf32"])], (V,))
    p.op(V, lambda e: e.tensor_copy(g.h["identb"][:], g.h["identf"][:]))
    c1 = p.tok(V)
    p.wait(TE, "m_vector", c1)
    p.wait(SC, "m_vector", c1)
    p.wait(GP, "m_vector", c1)
    p.te_dummy = (PS[5][0:1, 500:501], g.h["ones_row"][0:1, 0:1])
    return ps


def ln_consts(g):
    p, h = g.p, g.h
    p.op(V, lambda e: e.memset(h["cst"][:, 0:1], LNEPS))
    p.op(V, lambda e: e.memset(h["cst"][:, 1:2], RMSEPS * D))
    p.op(V, lambda e: e.memset(h["cst"][:, 2:3], HALF_LN_D))
    p.op(V, lambda e: e.memset(h["cst"][:, 3:4], 1e-30))
    p.op(V, lambda e: e.memset(h["ones_col"][:], 1.0))
    p.op(V, lambda e: e.memset(h["ones_row"][:], 1.0))



def emit_wquant_chunks(g):
    p, h = g.p, g.h
    for sname in ("s_wq0", "s_ws0"):
        p.new_sem(sname)
    g.din("wg", [I3, D]); g.din("wu", [I3, D]); g.din("wd", [D, I3])
    g.din("wgT", [D, I3]); g.din("wuT", [D, I3]); g.din("wdT", [I3, D])
    g.dint("STu", [32, I3])
    g.dint("STg", [32, I3])
    g.dint("STd", [96, D])
    wbigs = [h["sq"][:, 0:I3], h["wqTb_f32raw"][:, 0:I3],
             h["wkTb_f32raw"][:, 0:I3]]
    stgS = h["sq"][:, I3:I3 + 128]
    am = h["qf"][:, I3:I3 + 96]
    sS = h["qf"][:, I3 + 96:I3 + 192]
    state = {"first": True, "v": {0: 0, 1: 0, 2: 0}}

    def mk(wn, stn, ti, cols_, nb, k):
        par = k % 3
        st_ = {}

        def a():
            if state["first"]:
                state["first"] = False
                for s_ in ("m_tensor", "m_scalar", "m_vector"):
                    p.wait(SY, s_, p.cnt[s_])
                p.wait(SY, "s_xn", p.cnt["s_xn"])
                p.wait(SY, "s_qn", p.cnt["s_qn"])
            elif state["v"][par]:
                p.wait(SY, "m_vector", state["v"][par])
            st_["ld"] = p.op(SY, lambda e: e.dma_start(
                out=wbigs[par][:, 0:cols_],
                in_=h[wn][ti * 128:(ti + 1) * 128, :]), inc="s_wq0", k=16)

        def b():
            wbig = wbigs[par]
            p.wait(V, "s_wq0", st_["ld"])
            p.wait(V, "s_ws0", p.cnt["s_ws0"])
            for c0 in range(0, cols_, 1024):
                p.op(V, lambda e, c0=c0: e.tensor_reduce(
                    am[:, c0 // 32:(c0 + 1024) // 32],
                    wbig[:, c0:c0 + 1024].rearrange("p (b c) -> p b c", c=32),
                    AX.X, ALU.max, apply_absolute_value=True))
            p.fence(V)
            p.op(V, lambda e: e.tensor_scalar(sS[:, 0:nb], am[:, 0:nb],
                                              1.0 / 7.0, 1e-12,
                                              ALU.mult, ALU.max))
            p.fence(V)
            for bi in range(4):
                for bj in range(nb // 32):
                    p.op(V, lambda e, bi=bi, bj=bj: e.transpose(
                        stgS[bj * 32:(bj + 1) * 32, bi * 32:(bi + 1) * 32],
                        sS[bi * 32:(bi + 1) * 32, bj * 32:(bj + 1) * 32]))
            vtok = p.tok(V)
            state["v"][par] = vtok
            p.wait(GP, "m_vector", vtok)
            p.op(GP, lambda e: e.dma_start(
                out=stn[0:nb, ti * 128:(ti + 1) * 128], in_=stgS[0:nb, 0:128]),
                inc="s_ws0", k=16)

        return a, b

    pairs = []
    k = 0
    for (wn, stn, ntile, cols_, nb) in (
            ("wu", h["STu"], ICH, D, 32), ("wg", h["STg"], ICH, D, 32),
            ("wd", h["STd"], NCH, I3, 96)):
        for ti in range(ntile):
            pairs.append(mk(wn, stn, ti, cols_, nb, k))
            k += 1
    return pairs


def quant_wload(g, wtn, stn, jchunks, cols_):
    p, h = g.p, g.h
    big0, big2 = h["big0"], h["big2"]
    prev = g.h.setdefault("_qwl_prev", {0: {"v": 0, "g": 0},
                                        1: {"v": 0, "g": 0}})
    mm_snap = p.cnt["m_tensor"]
    for jc in range(jchunks):
        par = jc % 2
        buf = big0 if par == 0 else big2
        wbig = buf[:, 0:cols_]
        sexp = buf[:, I3:I3 + cols_]
        if prev[par]["v"]:
            p.wait(SY, "m_vector", prev[par]["v"])
        if prev[par]["g"]:
            p.wait(SY, "m_gpsimd", prev[par]["g"])
        p.op(SY, lambda e, jc=jc, wbig=wbig: e.dma_start(
            out=wbig, in_=h[wtn][jc * 128:(jc + 1) * 128, :]),
            inc="s_wq0", k=16)
        for bb in range(4):
            p.op(SY, lambda e, jc=jc, bb=bb, sexp=sexp: e.dma_start(
                out=sexp[bb * 32:(bb + 1) * 32, :],
                in_=stn[jc * 4 + bb:jc * 4 + bb + 1, :]
                .broadcast_to([32, cols_])), inc="s_wq0", k=16)
        ld = p.cnt["s_wq0"]
        p.wait(V, "s_wq0", ld)
        p.op(V, lambda e, sexp=sexp: e.reciprocal(sexp, sexp))
        rtok = p.tok(V)
        p.wait(GP, "s_wq0", ld)
        p.wait(GP, "m_vector", rtok)
        p.op(GP, lambda e, wbig=wbig, sexp=sexp: e.tensor_mul(wbig, wbig, sexp))
        g1tok = p.tok(GP)
        p.wait(V, "m_gpsimd", g1tok)
        p.op(V, lambda e, wbig=wbig: e.tensor_scalar(
            wbig, wbig, MAGIC, MAGIC, ALU.add, ALU.subtract))
        p.op(V, lambda e, sexp=sexp: e.reciprocal(sexp, sexp))
        vtok = p.tok(V)
        prev[par]["v"] = vtok
        p.wait(GP, "m_vector", vtok)
        p.wait(GP, "m_tensor", mm_snap)
        p.op(GP, lambda e, jc=jc, wbig=wbig, sexp=sexp: e.tensor_mul(
            h["wffn"][:, jc * cols_:(jc + 1) * cols_], wbig, sexp))
        gtok = p.tok(GP)
        prev[par]["g"] = gtok
    return p.cnt["m_gpsimd"]


def emit_g2(g, rg=None):
    if rg is None:
        rg = [list(range(NCORES))]
    p, h = g.p, g.h
    ps = common_setup(g)
    for sname in ("s_col", "s_cc", "s_wq0", "s_wq1", "s_up0", "s_up1",
                  "s_st0", "s_st1", "s_x1", "s_x1b", "s_r0", "s_r1", "s_ws0", "s_ws1", "s_wf"):
        p.new_sem(sname)
    g.din_once("x", [D, T])
    g.dio_in("xn0", [D, T])
    g.dio_in("s0", [D, T])
    assert "STu" in h, "emit_wquant_chunks must run (in G1) before emit_g2"
    g.dio_out("x1", [D, T])
    up_raw = g.dint("up_raw", [I3, T])
    gate_raw = g.dint("gate_raw", [I3, T])
    t4d = g.dint("t4d", [I3, T])
    upi32 = g.dint("upi32", [I3, T])
    t5d = g.dint("t5d", [I3, T])
    g.dint("scald", [1, 64])
    for n in range(1, 5):
        g.dint(f"cci{n}", [16, 128]); g.dint(f"cco{n}", [16, 128])

    g.sb("tmpg", [128, 1])
    big0 = g.sb("big0", [128, 2 * I3])
    big1 = g.sb("big1", [128, 2 * I3])
    big2 = g.sb("big2", [128, 2 * I3])
    g.sb("wffn", [128, NCH * I3], BF16)
    g.sb("xib", [128, NCH * TT], BF16)
    g.sb("upf0", [128, TT]); g.sb("upf1", [128, TT])
    for n in ("mupc", "mgc", "m4c", "m5c"):
        g.sb(n, [128, 1])

    psT, psU0, psU1 = ps[0], ps[1], ps[2]

    SA = {"up": big1[:, 0:1024], "gt": big1[:, 1024:2048],
          "ui": big1[:, 2048:3072], "gi": big1[:, 3072:4096],
          "e": big1[:, 4096:5120], "d": big1[:, 5120:6144],
          "si": big2[:, 0:1024], "t4": big2[:, 1024:2048],
          "xr": big2[:, 2048:3072], "sr": big2[:, 3072:4096],
          "gr": big2[:, 4096:5120], "df": big2[:, 5120:6144]}
    oib = big0[:].bitcast(BF16)
    h["efb"] = SA["e"]
    h["gfb"] = SA["d"]

    for n in ("mupc", "mgc", "m4c", "m5c"):
        p.op(V, lambda e, n=n: e.memset(h[n][:], 0.0))
    cst0 = p.tok(V)
    p.wait(TE, "m_vector", cst0)
    p.wait(SC, "m_vector", cst0)

    colmax_to_scal(g, h["mxc"], 0, psT)
    p.op(V, lambda e: e.tensor_copy(h["ccst"][0:1, 0:1], h["scal"][0:1, 0:1]))
    exchange_max(g, 1, rg)
    p.op(V, lambda e: e.tensor_scalar(h["scal"][0:1, 1:2], h["scal"][0:1, 32:33],
                                      1.0 / QMAX, 1e-12, ALU.mult, ALU.max))
    p.fence(V)
    p.op(V, lambda e: e.reciprocal(h["scal"][0:1, 2:3], h["scal"][0:1, 1:2]))
    p.fence(V)
    rx_tok = scal_to_cols(g, [(2, 0, 1.0)])

    prev_xi = 0
    evs = []
    sts = {0: 0, 1: 0}
    p.wait(SY, "s_ws0", p.cnt["s_ws0"])
    for (wtn, stn, dstd, mcol) in (("wuT", "STu", up_raw, "mupc"),
                                   ("wgT", "STg", gate_raw, "mgc")):
        rdy = quant_wload(g, wtn, h[stn], NCH, I3)
        p.wait(TE, "m_gpsimd", rdy)
        for tt in range(NTT):
            if prev_xi:
                p.wait(SY, "m_vector", prev_xi)
            batch_load(g, "s_xa",
                       [(h["xa0"][:, c * TT:(c + 1) * TT],
                         h["xn0"][c * 128:(c + 1) * 128, tt * TT:(tt + 1) * TT])
                        for c in range(NCH)], (V,))
            p.wait(V, "m_vector", rx_tok)
            p.wait(V, "m_tensor", p.cnt["m_tensor"])
            for c in range(NCH):
                p.op(V, lambda e, c=c: e.tensor_scalar(
                    h["rows"][0:1, 0:1] if False else h["xa0"][:, c * TT:(c + 1) * TT],
                    h["xa0"][:, c * TT:(c + 1) * TT], h["cols"][:, 0:1], MAGIC,
                    ALU.mult, ALU.add))
            p.fence(V)
            for c in range(NCH):
                p.op(V, lambda e, c=c: e.tensor_scalar(
                    h["xib"][:, c * TT:(c + 1) * TT],
                    h["xa0"][:, c * TT:(c + 1) * TT], MAGIC, None, ALU.subtract))
            xi_tok = p.tok(V)
            prev_xi = xi_tok
            p.wait(TE, "m_vector", xi_tok)
            for uc in range(ICH):
                par = uc % 2
                psb = psU0 if par == 0 else psU1
                upf = h["upf0"] if par == 0 else h["upf1"]
                if len(evs) >= 2:
                    p.wait(TE, "m_scalar", evs[-2])
                for jc in range(NCH):
                    p.op(TE, lambda e, uc=uc, jc=jc, psb=psb: e.matmul(
                        psb[:, 0:TT],
                        h["wffn"][:, jc * I3 + uc * 128: jc * I3 + (uc + 1) * 128],
                        h["xib"][:, jc * TT:(jc + 1) * TT],
                        start=(jc == 0), stop=(jc == NCH - 1)),
                        inc=("m_tensor" if jc == NCH - 1 else None))
                mt = p.cnt["m_tensor"]
                p.wait(SC, "m_tensor", mt)
                if sts[par]:
                    p.wait(SC, "s_st0" if par == 0 else "s_st1", sts[par])
                ev = p.op(SC, lambda e, psb=psb, upf=upf: e.activation(
                    upf[:, 0:TT], psb[:, 0:TT], AF.Copy), inc="m_scalar")
                evs.append(ev)
                p.wait(V, "m_scalar", ev)
                p.op(V, lambda e, upf=upf: e.tensor_reduce(
                    h["tmpcol"][:, 0:1], upf[:, 0:TT], AX.X, ALU.max,
                    apply_absolute_value=True))
                p.fence(V)
                p.op(V, lambda e, mcol=mcol: e.tensor_max(
                    h[mcol][:, 0:1], h[mcol][:, 0:1], h["tmpcol"][:, 0:1]))
                rv = p.tok(V)
                p.wait(V, "m_vector", rv)
                p.wait(GP, "m_vector", rv)
                p.op(GP, lambda e, uc=uc, tt=tt, dstd=dstd, upf=upf: e.dma_start(
                    out=dstd[uc * 128:(uc + 1) * 128, tt * TT:(tt + 1) * TT],
                    in_=upf[:, 0:TT]),
                    inc="s_st0" if par == 0 else "s_st1", k=16)
                sts[par] = p.cnt["s_st0" if par == 0 else "s_st1"]
    colmax_to_scal(g, h["mupc"], 3, psT)
    colmax_to_scal(g, h["mgc"], 4, psT)
    p.op(V, lambda e: e.tensor_copy(h["ccst"][0:1, 0:2], h["scal"][0:1, 3:5]))
    exchange_max(g, 2, rg)
    p.op(V, lambda e: e.tensor_scalar(h["scal"][0:1, 5:6], h["scal"][0:1, 32:33],
                                      1.0 / QMAX, 1e-12, ALU.mult, ALU.max))
    p.op(V, lambda e: e.tensor_scalar(h["scal"][0:1, 7:8], h["scal"][0:1, 33:34],
                                      1.0 / QMAX, 1e-12, ALU.mult, ALU.max))
    p.fence(V)
    p.op(V, lambda e: e.reciprocal(h["scal"][0:1, 6:7], h["scal"][0:1, 5:6]))
    p.op(V, lambda e: e.reciprocal(h["scal"][0:1, 8:9], h["scal"][0:1, 7:8]))
    p.op(V, lambda e: e.tensor_mul(h["scal"][0:1, 9:10], h["scal"][0:1, 7:8],
                                   h["scal"][0:1, 1:2]))
    p.fence(V)
    sc2 = scal_to_cols(g, [(6, 1, 1.0), (8, 2, 1.0), (9, 3, 1.0)])
    p.wait(SC, "m_vector", sc2)
    p.wait(V, "m_vector", sc2)
    p.wait(GP, "m_vector", sc2)

    p.op(V, lambda e: e.memset(h["cols"][:, 14:15], MAGIC))
    p.op(V, lambda e: e.memset(h["cols"][:, 15:16], -MAGIC))
    mcst = p.tok(V)
    p.wait(SC, "m_vector", mcst)
    lastv = {0: 0, 1: 0}
    lastsc = {0: 0, 1: 0}
    lastst = {0: 0, 1: 0}
    lastg = {0: 0, 1: 0}
    for it in range(NTT * ICH):
        tt, uc = divmod(it, ICH)
        par = it % 2
        off = 0 if par == 0 else TT
        sem = "s_up0" if par == 0 else "s_up1"
        stsem = "s_st0" if par == 0 else "s_st1"
        if lastv[par]:
            p.wait(SY, "m_vector", lastv[par])
            p.wait(SY, "m_scalar", lastsc[par])
            p.wait(SY, "m_gpsimd", lastg[par])
        p.op(SY, lambda e, uc=uc, tt=tt, off=off: e.dma_start(
            out=SA["up"][:, off:off + TT],
            in_=up_raw[uc * 128:(uc + 1) * 128, tt * TT:(tt + 1) * TT]),
            inc=sem, k=16)
        p.op(SY, lambda e, uc=uc, tt=tt, off=off: e.dma_start(
            out=SA["gt"][:, off:off + TT],
            in_=gate_raw[uc * 128:(uc + 1) * 128, tt * TT:(tt + 1) * TT]),
            inc=sem, k=16)
        ldt = p.cnt[sem]
        p.wait(V, sem, ldt)
        if lastg[par]:
            p.wait(V, "m_gpsimd", lastg[par])
        if lastsc[par]:
            p.wait(V, "m_scalar", lastsc[par])
        p.op(V, lambda e, off=off: e.tensor_scalar(
            SA["sr"][:, off:off + TT], SA["up"][:, off:off + TT],
            h["cols"][:, 1:2], MAGIC, ALU.mult, ALU.add))
        p.op(V, lambda e, off=off: e.tensor_scalar(
            SA["gi"][:, off:off + TT], SA["gt"][:, off:off + TT],
            h["cols"][:, 2:3], MAGIC, ALU.mult, ALU.add))
        v1_tok = p.tok(V)
        p.wait(SC, "m_vector", v1_tok)
        if lastst[par]:
            p.wait(SC, stsem, lastst[par])
        p.op(SC, lambda e, off=off: e.activation(
            SA["ui"][:, off:off + TT], SA["sr"][:, off:off + TT], AF.Identity,
            scale=1.0, bias=h["cols"][:, 15:16]))
        p.op(SC, lambda e, off=off: e.activation(
            SA["gi"][:, off:off + TT], SA["gi"][:, off:off + TT], AF.Identity,
            scale=1.0, bias=h["cols"][:, 15:16]))
        gi_sc = p.tok(SC)
        p.wait(GP, "m_scalar", gi_sc)
        if lastst[par]:
            p.wait(GP, stsem, lastst[par])
        p.op(GP, lambda e, uc=uc, tt=tt, off=off: e.dma_start(
            out=upi32[uc * 128:(uc + 1) * 128, tt * TT:(tt + 1) * TT],
            in_=SA["ui"][:, off:off + TT]), inc=stsem, k=16)
        p.fence(SC)
        p.op(SC, lambda e, off=off: e.activation(
            SA["e"][:, off:off + TT], SA["gi"][:, off:off + TT], AF.Sigmoid,
            scale=h["cols"][:, 3:4]))
        si1_sc = p.tok(SC)
        p.wait(V, "m_scalar", si1_sc)
        p.op(V, lambda e, off=off: e.tensor_scalar(
            SA["si"][:, off:off + TT], SA["e"][:, off:off + TT], 65536.0,
            MAGIC, ALU.mult, ALU.add))
        p.fence(V)
        p.op(V, lambda e, off=off: e.tensor_scalar(
            SA["si"][:, off:off + TT], SA["si"][:, off:off + TT], MAGIC,
            65535.0, ALU.subtract, ALU.min))
        si_tok = p.tok(V)
        p.wait(GP, "m_vector", si_tok)
        p.op(GP, lambda e, off=off: e.tensor_mul(
            SA["t4"][:, off:off + TT], SA["gi"][:, off:off + TT],
            SA["si"][:, off:off + TT]))
        g2t = p.tok(GP)
        p.wait(GP, "m_gpsimd", g2t)
        p.op(GP, lambda e, uc=uc, tt=tt, off=off: e.dma_start(
            out=t4d[uc * 128:(uc + 1) * 128, tt * TT:(tt + 1) * TT],
            in_=SA["t4"][:, off:off + TT]), inc=stsem, k=16)
        g4t = p.tok(GP)
        p.wait(V, "m_gpsimd", g4t)
        p.op(V, lambda e, off=off: e.tensor_reduce(
            h["tmpg"][:, 0:1], SA["t4"][:, off:off + TT], AX.X, ALU.max,
            apply_absolute_value=True))
        p.fence(V)
        p.op(V, lambda e: e.tensor_max(h["m4c"][:, 0:1], h["m4c"][:, 0:1],
                                       h["tmpg"][:, 0:1]))
        m4tok = p.tok(V)
        lastst[par] = p.cnt[stsem]
        lastv[par] = m4tok
        lastsc[par] = si1_sc
        lastg[par] = g4t
    p.wait(V, "m_gpsimd", p.cnt["m_gpsimd"])
    colmax_to_scal(g, h["m4c"], 10, psT)
    p.op(V, lambda e: e.tensor_mul(h["scal"][0:1, 11:12], h["scal"][0:1, 10:11],
                                   h["scal"][0:1, 7:8]))
    p.fence(V)
    p.op(V, lambda e: e.tensor_scalar(h["scal"][0:1, 11:12],
                                      h["scal"][0:1, 11:12], 1.0 / 65536.0,
                                      None, ALU.mult))
    p.fence(V)
    p.op(V, lambda e: e.tensor_mul(h["scal"][0:1, 11:12], h["scal"][0:1, 11:12],
                                   h["scal"][0:1, 1:2]))
    p.fence(V)
    p.op(V, lambda e: e.tensor_copy(h["ccst"][0:1, 0:1], h["scal"][0:1, 11:12]))
    exchange_max(g, 3, rg)
    p.op(V, lambda e: e.tensor_scalar(h["scal"][0:1, 12:13],
                                      h["scal"][0:1, 32:33], 1.0 / QMAX, 1e-12,
                                      ALU.mult, ALU.max))
    p.fence(V)
    p.op(V, lambda e: e.reciprocal(h["scal"][0:1, 13:14], h["scal"][0:1, 12:13]))
    p.fence(V)
    p.op(V, lambda e: e.tensor_mul(h["scal"][0:1, 13:14], h["scal"][0:1, 13:14],
                                   h["scal"][0:1, 9:10]))
    p.fence(V)
    p.op(V, lambda e: e.tensor_scalar(h["scal"][0:1, 13:14],
                                      h["scal"][0:1, 13:14], 1.0 / 65536.0,
                                      None, ALU.mult))
    p.fence(V)
    c4tok = scal_to_cols(g, [(13, 4, 1.0)])
    p.wait(V, "m_vector", c4tok)

    p.wait(SY, "s_st0", p.cnt["s_st0"])
    p.wait(SY, "s_st1", p.cnt["s_st1"])
    lastv = {0: 0, 1: 0}
    lastsc = {0: 0, 1: 0}
    lastst = {0: 0, 1: 0}
    lastg = {0: 0, 1: 0}
    for it in range(NTT * ICH):
        tt, uc = divmod(it, ICH)
        par = it % 2
        off = 0 if par == 0 else TT
        sem = "s_up0" if par == 0 else "s_up1"
        stsem = "s_st0" if par == 0 else "s_st1"
        if lastsc[par]:
            p.wait(SY, "m_scalar", lastsc[par])
            p.wait(SY, "m_gpsimd", lastg[par])
            p.wait(SY, "m_vector", lastv[par])
        p.op(SY, lambda e, uc=uc, tt=tt, off=off: e.dma_start(
            out=SA["up"][:, off:off + TT],
            in_=t4d[uc * 128:(uc + 1) * 128, tt * TT:(tt + 1) * TT]),
            inc=sem, k=16)
        p.op(SY, lambda e, uc=uc, tt=tt, off=off: e.dma_start(
            out=SA["gt"][:, off:off + TT],
            in_=upi32[uc * 128:(uc + 1) * 128, tt * TT:(tt + 1) * TT]),
            inc=sem, k=16)
        ldt = p.cnt[sem]
        p.wait(V, sem, ldt)
        if lastg[par]:
            p.wait(V, "m_gpsimd", lastg[par])
        if lastsc[par]:
            p.wait(V, "m_scalar", lastsc[par])
        p.op(V, lambda e, off=off: e.tensor_scalar(
            SA["sr"][:, off:off + TT], SA["up"][:, off:off + TT],
            h["cols"][:, 4:5], MAGIC, ALU.mult, ALU.add))
        v1_tok = p.tok(V)
        p.wait(SC, "m_vector", v1_tok)
        if lastg[par]:
            p.wait(SC, "m_gpsimd", lastg[par])
        p.op(SC, lambda e, off=off: e.activation(
            SA["gi"][:, off:off + TT], SA["sr"][:, off:off + TT], AF.Identity,
            scale=1.0, bias=h["cols"][:, 15:16]))
        g2sc = p.tok(SC)
        p.wait(GP, "m_scalar", g2sc)
        p.wait(GP, sem, ldt)
        if lastv[par]:
            p.wait(GP, "m_vector", lastv[par])
        if lastst[par]:
            p.wait(GP, stsem, lastst[par])
        p.op(GP, lambda e, off=off: e.tensor_mul(
            SA["t4"][:, off:off + TT], SA["gi"][:, off:off + TT],
            SA["gt"][:, off:off + TT]))
        g5t = p.tok(GP)
        p.wait(GP, "m_gpsimd", g5t)
        p.op(GP, lambda e, uc=uc, tt=tt, off=off: e.dma_start(
            out=t5d[uc * 128:(uc + 1) * 128, tt * TT:(tt + 1) * TT],
            in_=SA["t4"][:, off:off + TT]), inc=stsem, k=16)
        p.wait(V, "m_gpsimd", g5t)
        p.op(V, lambda e, off=off: e.tensor_reduce(
            h["tmpcol"][:, 0:1], SA["t4"][:, off:off + TT], AX.X, ALU.max,
            apply_absolute_value=True))
        p.fence(V)
        p.op(V, lambda e: e.tensor_max(h["m5c"][:, 0:1], h["m5c"][:, 0:1],
                                       h["tmpcol"][:, 0:1]))
        t5tok = p.tok(V)
        lastst[par] = p.cnt[stsem]
        lastv[par] = t5tok
        lastsc[par] = g2sc
        lastg[par] = g5t
    colmax_to_scal(g, h["m5c"], 14, psT)
    p.op(V, lambda e: e.tensor_mul(h["scal"][0:1, 18:19], h["scal"][0:1, 5:6],
                                   h["scal"][0:1, 1:2]))
    p.fence(V)
    p.op(V, lambda e: e.tensor_mul(h["scal"][0:1, 15:16], h["scal"][0:1, 14:15],
                                   h["scal"][0:1, 12:13]))
    p.fence(V)
    p.op(V, lambda e: e.tensor_mul(h["scal"][0:1, 15:16], h["scal"][0:1, 15:16],
                                   h["scal"][0:1, 18:19]))
    p.fence(V)
    p.op(V, lambda e: e.tensor_copy(h["ccst"][0:1, 0:1], h["scal"][0:1, 15:16]))
    exchange_max(g, 4, rg)
    p.op(V, lambda e: e.tensor_scalar(h["scal"][0:1, 16:17],
                                      h["scal"][0:1, 32:33], 1.0 / QMAX, 1e-12,
                                      ALU.mult, ALU.max))
    p.fence(V)
    p.op(V, lambda e: e.reciprocal(h["scal"][0:1, 17:18], h["scal"][0:1, 16:17]))
    p.fence(V)
    p.op(V, lambda e: e.tensor_mul(h["scal"][0:1, 17:18], h["scal"][0:1, 17:18],
                                   h["scal"][0:1, 12:13]))
    p.fence(V)
    p.op(V, lambda e: e.tensor_mul(h["scal"][0:1, 17:18], h["scal"][0:1, 17:18],
                                   h["scal"][0:1, 18:19]))
    p.fence(V)
    c5tok = scal_to_cols(g, [(17, 5, 1.0), (16, 6, 1.0)])
    p.wait(V, "m_vector", c5tok)
    p.wait(SC, "m_vector", c5tok)

    p.wait(SY, "s_st0", p.cnt["s_st0"])
    p.wait(SY, "s_st1", p.cnt["s_st1"])
    p.wait(SY, "m_tensor", p.cnt["m_tensor"])
    for s_ in ("m_vector", "m_scalar", "m_gpsimd"):
        p.wait(SY, s_, p.cnt[s_])
    rdy4 = quant_wload(g, "wdT", h["STd"], ICH, D)
    p.wait(TE, "m_gpsimd", rdy4)
    p.wait(SY, "m_gpsimd", rdy4)
    lastv = {0: 0, 1: 0}
    lastsc4 = {0: 0, 1: 0}
    lastx1 = {0: 0, 1: 0}
    lastres = {0: 0, 1: 0}
    evs4 = []
    for tt in range(NTT):
        p.wait(SC, "m_tensor", p.cnt["m_tensor"])
        for uc in range(ICH):
            par = uc % 2
            off = 0 if par == 0 else TT
            sem = "s_up0" if par == 0 else "s_up1"
            if lastsc4[par]:
                p.wait(SY, "m_scalar", lastsc4[par])
                p.wait(SY, "m_vector", lastv[par])
            p.op(SY, lambda e, uc=uc, tt=tt, off=off: e.dma_start(
                out=SA["up"][:, off:off + TT],
                in_=t5d[uc * 128:(uc + 1) * 128, tt * TT:(tt + 1) * TT]),
                inc=sem, k=16)
            ldt = p.cnt[sem]
            p.wait(V, sem, ldt)
            if lastsc4[par]:
                p.wait(V, "m_scalar", lastsc4[par])
            p.op(V, lambda e, off=off: e.tensor_scalar(
                SA["ui"][:, off:off + TT], SA["up"][:, off:off + TT],
                h["cols"][:, 5:6], MAGIC, ALU.mult, ALU.add))
            vt = p.tok(V)
            lastv[par] = vt
            p.wait(SC, "m_vector", vt)
            p.op(SC, lambda e, uc=uc, off=off: e.activation(
                oib[:, uc * TT:(uc + 1) * TT], SA["ui"][:, off:off + TT],
                AF.Identity, scale=1.0, bias=h["cols"][:, 15:16]))
            o2sc = p.tok(SC)
            lastsc4[par] = o2sc
        oi_tok = p.cnt["m_scalar"]
        p.wait(TE, "m_scalar", oi_tok)
        for hc in range(NCH):
            par = hc % 2
            off = 0 if par == 0 else TT
            psb = psU0 if par == 0 else psU1
            if len(evs4) >= 2:
                p.wait(TE, "m_scalar", evs4[-2])
            for ic2 in range(ICH):
                p.op(TE, lambda e, hc=hc, ic2=ic2, psb=psb: e.matmul(
                    psb[:, 0:TT],
                    h["wffn"][:, ic2 * D + hc * 128: ic2 * D + (hc + 1) * 128],
                    oib[:, ic2 * TT:(ic2 + 1) * TT],
                    start=(ic2 == 0), stop=(ic2 == ICH - 1)),
                    inc=("m_tensor" if ic2 == ICH - 1 else None))
            mt = p.cnt["m_tensor"]
            p.wait(SC, "m_tensor", mt)
            if lastx1[par]:
                p.wait(SC, "s_x1" if par == 0 else "s_x1b", lastx1[par])
            ev2 = p.op(SC, lambda e, psb=psb, off=off: e.activation(
                h["gfb"][:, off:off + TT], psb[:, 0:TT], AF.Silu,
                scale=h["cols"][:, 6:7]), inc="m_scalar")
            evs4.append(ev2)
            rsem = "s_r0" if par == 0 else "s_r1"
            if lastres[par]:
                p.wait(SY, "m_vector", lastres[par])
            batch_load(g, rsem,
                       [(SA["xr"][:, off:off + TT],
                         h["x"][hc * 128:(hc + 1) * 128, tt * TT:(tt + 1) * TT]),
                        (SA["sr"][:, off:off + TT],
                         h["s0"][hc * 128:(hc + 1) * 128, tt * TT:(tt + 1) * TT])],
                       ())
            rld = p.cnt[rsem]
            p.wait(V, "m_scalar", ev2)
            p.wait(V, rsem, rld)
            p.op(V, lambda e, off=off: e.tensor_mul(
                SA["gr"][:, off:off + TT], SA["sr"][:, off:off + TT],
                h["gfb"][:, off:off + TT]))
            p.fence(V)
            p.op(V, lambda e, off=off: e.tensor_add(
                SA["gr"][:, off:off + TT], SA["xr"][:, off:off + TT],
                SA["gr"][:, off:off + TT]))
            x1tok = p.tok(V)
            lastres[par] = x1tok
            p.wait(GP, "m_vector", x1tok)
            p.op(GP, lambda e, hc=hc, tt=tt, off=off: e.dma_start(
                out=h["x1"][hc * 128:(hc + 1) * 128, tt * TT:(tt + 1) * TT],
                in_=SA["gr"][:, off:off + TT]),
                inc="s_x1" if par == 0 else "s_x1b", k=16)
            lastx1[par] = p.cnt["s_x1" if par == 0 else "s_x1b"]


def build_g2(rg=None):
    g = G("g2")
    emit_g2(g, rg)
    return g



def emit_g3(g):
    p, h = g.p, g.h
    ps = common_setup(g)
    for sname in ("s_q1a", "s_q1b", "s_s1a", "s_s1b", "s_sh"):
        p.new_sem(sname)
    g.dio_in("x1", [D, T])
    g.din("wdqT", [D, QL])
    g.din("wuqT", [QL, D])
    g.din("wdq_nat", [QL, D])
    g.din("wuk_nat", [D, QL])
    g.din("wuv_nat", [D, QL])
    g.din("woT1", [D, D])
    g.din("wdkvT", [D, QL])
    g.din("masks", [4 * 128, TT], BF16)
    g.dio_out("xn1", [D, T])
    g.dio_out("s1", [D, T])
    q1 = g.dint("q1", [D, T], BF16)
    shd3 = g.dint("shd3", [2, 16])

    g.sb("sq", [128, NCH * TT])
    g.sb("wuqTb", [128, 2 * D], BF16)
    g.sb("wdq_b", [128, 2 * D], BF16)
    g.sb("wdqT_b", [128, NCH * QL], BF16)
    g.sb("wdkvT_b", [128, NCH * QL], BF16)
    g.sb("wuk_b", [128, NCH * QL], BF16)
    g.sb("wuv_b", [128, NCH * QL], BF16)
    g.sb("wostg", [128, 2 * D], BF16)
    g.sb("ATb", [128, NCH * D], BF16)
    g.sb("keT", [128, 2 * D], BF16)
    g.sb("Mb", [128, 2 * D], BF16)
    g.sb("ntabq3", [2, 16])
    g.sb("shift3", [2, 16])
    g.sb("cs2", [2, 1])
    g.sb("qdqb", [128, 2 * TT], BF16)
    g.sb("qnb3", [128, NCH * TT], BF16)
    g.sb("ckvb", [128, 2 * T], BF16)
    g.sb("ckvtm", [128, 32 * 257], BF16)
    g.sb("sqk", [128, TT])
    g.sb("q1h", [64, S], BF16)
    g.sb("KCb", [64, S], BF16)
    g.sb("msk", [128, TT])
    g.sb("maskt", [128, 4 * TT], BF16)
    g.sb("e3t0", [128, TT], BF16); g.sb("e3t1", [128, TT], BF16)
    g.sb("e3t2", [128, TT], BF16)
    g.sb("ctxb", [128, 2 * TT], BF16)
    g.sb("rrow", [1, TT])
    g.sb("rbcs", [64, TT])
    g.sb("s1t", [64, 2 * TT])
    g.sb("shcol3", [128, 1])

    psP0, psP1, psA = ps[0], ps[1], ps[2]
    psC, psRS, psW = ps[3], ps[4], ps[5]
    psB = ps["B"]
    psTb = psRS[:, 64:128].bitcast(BF16)

    p.op(V, lambda e: e.memset(h["ntabq3"][:], 0.0))
    p.op(V, lambda e: e.memset(h["scal"][:], 0.0))
    p.op(V, lambda e: e.memset(
        h["ckvtm"][:].rearrange("p (t c) -> p t c", c=257)[:, :, 256:257], 1.0))
    cst0 = p.tok(V)
    p.wait(TE, "m_vector", cst0)
    p.wait(SC, "m_vector", cst0)
    p.wait(SY, "m_vector", cst0)
    batch_load(g, "s_prm",
               [(h["maskt"][:, mi * TT:(mi + 1) * TT],
                 h["masks"][mi * 128:(mi + 1) * 128, :]) for mi in range(4)],
               (V,))
    idb_tok = p.tok(V)
    p.wait(TE, "m_vector", idb_tok)

    load_wT_bf16(g, h["wuqT"], h["wuqTb"], 2, D)
    load_wT_bf16(g, h["wdq_nat"], h["wdq_b"], 2, D)
    load_wT_bf16(g, h["wdqT"], h["wdqT_b"], NCH, QL)
    load_wT_bf16(g, h["wdkvT"], h["wdkvT_b"], NCH, QL)
    load_wT_bf16(g, h["wuk_nat"], h["wuk_b"], NCH, QL)
    load_wT_bf16(g, h["wuv_nat"], h["wuv_b"], NCH, QL)
    wtok = p.cnt["m_vector"]
    p.wait(TE, "m_vector", wtok)

    evs = []
    for jc in range(NCH):
        for ih in range(2):
            psb = psP0 if (jc * 2 + ih) % 2 == 0 else psP1
            if len(evs) >= 2:
                p.wait(TE, "m_scalar", evs[-2])
            for lc in range(2):
                p.op(TE, lambda e, jc=jc, ih=ih, lc=lc, psb=psb: e.matmul(
                    psb[:, 0:TT],
                    h["wuqTb"][:, lc * D + jc * 128: lc * D + (jc + 1) * 128],
                    h["wdq_b"][:, lc * D + ih * TT: lc * D + (ih + 1) * TT],
                    start=(lc == 0), stop=(lc == 1)),
                    inc=("m_tensor" if lc == 1 else None))
            mt = p.cnt["m_tensor"]
            p.wait(SC, "m_tensor", mt)
            ev = p.op(SC, lambda e, jc=jc, ih=ih, psb=psb: e.activation(
                h["ATb"][:, jc * D + ih * TT: jc * D + (ih + 1) * TT],
                psb[:, 0:TT], AF.Copy), inc="m_scalar")
            evs.append(ev)
    p.wait(TE, "m_scalar", evs[-1])
    for lh in range(2):
        for ih in range(2):
            psb = psP0 if (lh * 2 + ih) % 2 == 0 else psP1
            if len(evs) >= 2:
                p.wait(TE, "m_scalar", evs[-2])
            for jc in range(NCH):
                p.op(TE, lambda e, lh=lh, ih=ih, jc=jc, psb=psb: e.matmul(
                    psb[:, 0:TT],
                    h["wuk_b"][:, jc * QL + lh * 128: jc * QL + (lh + 1) * 128],
                    h["ATb"][:, jc * D + ih * TT: jc * D + (ih + 1) * TT],
                    start=(jc == 0), stop=(jc == NCH - 1)),
                    inc=("m_tensor" if jc == NCH - 1 else None))
            mt = p.cnt["m_tensor"]
            p.wait(SC, "m_tensor", mt)
            ev = p.op(SC, lambda e, lh=lh, ih=ih, psb=psb: e.activation(
                h["keT"][:, lh * D + ih * TT: lh * D + (ih + 1) * TT],
                psb[:, 0:TT], AF.Copy), inc="m_scalar")
            evs.append(ev)
    p.wait(TE, "m_scalar", evs[-1])
    mbanks = [psP0, psP1, psA, psC]
    wo_toks = []
    p.wait(SY, "m_vector", p.cnt["m_vector"])
    for jc in range(NCH):
        par = jc % 2
        off = 0 if par == 0 else D
        sem = "s_w0" if par == 0 else "s_w1"
        if jc >= 2:
            p.wait(SY, "m_tensor", wo_toks[jc - 2])
        ldt = p.op(SY, lambda e, jc=jc, off=off: e.dma_start(
            out=h["wstage"][:, 0:D] if False else h["wstage"][:, off:off + D],
            in_=h["woT1"][jc * 128:(jc + 1) * 128, :]), inc=sem, k=16)
        p.wait(V, sem, ldt)
        wct = p.op(V, lambda e, off=off: e.tensor_copy(
            h["wostg"][:, off:off + D], h["wstage"][:, off:off + D]),
            inc="m_vector")
        p.wait(TE, "m_vector", wct)
        for q in range(4):
            lh, ih = divmod(q, 2)
            tk = p.op(TE, lambda e, jc=jc, lh=lh, ih=ih, off=off, q=q: e.matmul(
                mbanks[q][:, 0:TT],
                h["wuv_b"][:, jc * QL + lh * 128: jc * QL + (lh + 1) * 128],
                h["wostg"][:, off + ih * TT: off + (ih + 1) * TT],
                start=(jc == 0), stop=(jc == NCH - 1)),
                inc=("m_tensor" if q == 3 else None))
        wo_toks.append(p.cnt["m_tensor"])
    p.wait(SC, "m_tensor", wo_toks[-1])
    for q in range(4):
        lh, ih = divmod(q, 2)
        ev = p.op(SC, lambda e, lh=lh, ih=ih, q=q: e.activation(
            h["Mb"][:, lh * D + ih * TT: lh * D + (ih + 1) * TT],
            mbanks[q][:, 0:TT], AF.Copy), inc="m_scalar")
        evs.append(ev)
    ke_tok = evs[-1]
    p.wait(V, "m_scalar", ke_tok)
    for it in range(4):
        lh, ih = divmod(it, 2)
        p.op(V, lambda e, lh=lh, ih=ih, sq=h["sq"]: e.tensor_mul(
            sq[:, 0:TT],
            h["keT"][:, lh * D + ih * TT: lh * D + (ih + 1) * TT],
            h["keT"][:, lh * D + ih * TT: lh * D + (ih + 1) * TT]))
        sqt = p.tok(V)
        p.wait(TE, "m_vector", sqt)
        tk = p.op(TE, lambda e, it=it, sq=h["sq"]: e.matmul(
            psA[0:1, 0:TT], h["ones_col"][:, 0:1], sq[:, 0:TT],
            start=(it == 0), stop=(it == 3)), inc="m_tensor")
        p.wait(V, "m_tensor", tk)
    p.op(V, lambda e: e.tensor_reduce(h["scal"][0:1, 0:1], psA[0:1, 0:TT],
                                      AX.X, ALU.add))
    p.fence(V)

    p.wait(SC, "m_vector", p.cnt["m_vector"])
    xn_toks = [0] * NTT
    sq_toks = [0] * NTT
    cons_toks = [0] * NTT
    prev_xn_store = 0
    prev_qn_store = {0: 0, 1: 0}
    evq = []
    prev_qn_tok = 0
    prev_sq_tok = 0
    for tt in range(NTT):
        xa = h["xa0"]
        seq = tt // 4
        if tt >= 1:
            p.wait(SY, "m_vector", xn_toks[tt - 1])
            p.wait(SY, "m_scalar", sq_toks[tt - 1])
        batch_load(g, "s_xa",
                   [(xa[:, c * TT:(c + 1) * TT],
                     h["x1"][c * 128:(c + 1) * 128, tt * TT:(tt + 1) * TT])
                    for c in range(NCH)], (V, SC))
        if tt >= 1:
            p.wait(V, "m_tensor", cons_toks[tt - 1])
            p.wait(SC, "m_tensor", cons_toks[tt - 1])
            p.wait(V, "s_xn", prev_xn_store)
            p.wait(TE, "m_vector", p.cnt["m_vector"])
        xn_tok, xnb_tok = layernorm_tile(g, xa, h["xn"], h["xnb"], psA, psB,
                                         psC=psC)
        xn_toks[tt] = xn_tok
        sq_toks[tt] = xnb_tok
        p.wait(GP, "m_vector", xn_tok)
        for c in range(NCH):
            p.op(GP, lambda e, c=c, tt=tt: e.dma_start(
                out=h["xn1"][c * 128:(c + 1) * 128, tt * TT:(tt + 1) * TT],
                in_=h["xn"][:, c * TT:(c + 1) * TT]), inc="s_xn", k=16)
        prev_xn_store = p.cnt["s_xn"]
        p.wait(TE, "m_scalar", xnb_tok)
        for qlc in range(2):
            psb = psP0 if qlc % 2 == 0 else psP1
            if len(evq) >= 2:
                p.wait(TE, "m_scalar", evq[-2])
            for jc in range(NCH):
                p.op(TE, lambda e, qlc=qlc, jc=jc, psb=psb: e.matmul(
                    psb[:, 0:TT],
                    h["wdqT_b"][:, jc * QL + qlc * 128: jc * QL + (qlc + 1) * 128],
                    h["xnb"][:, jc * TT:(jc + 1) * TT],
                    start=(jc == 0), stop=(jc == NCH - 1)),
                    inc=("m_tensor" if jc == NCH - 1 else None))
            mt = p.cnt["m_tensor"]
            p.wait(SC, "m_tensor", mt)
            ev = p.op(SC, lambda e, qlc=qlc, psb=psb: e.activation(
                h["qdqb"][:, qlc * TT:(qlc + 1) * TT], psb[:, 0:TT], AF.Copy),
                inc="m_scalar")
            evq.append(ev)
        qdq_tok = evq[-1]
        p.wait(TE, "m_scalar", qdq_tok)
        if prev_qn_tok:
            p.wait(V, "m_tensor", 0)
        for ic in range(NCH):
            psb = psP0 if ic % 2 == 0 else psP1
            if len(evq) >= 2:
                p.wait(TE, "m_scalar", evq[-2])
            for qlc in range(2):
                p.op(TE, lambda e, ic=ic, qlc=qlc, psb=psb: e.matmul(
                    psb[:, 0:TT],
                    h["wuqTb"][:, qlc * D + ic * 128: qlc * D + (ic + 1) * 128],
                    h["qdqb"][:, qlc * TT:(qlc + 1) * TT],
                    start=(qlc == 0), stop=(qlc == 1)),
                    inc=("m_tensor" if qlc == 1 else None))
            mt = p.cnt["m_tensor"]
            p.wait(SC, "m_tensor", mt)
            par = ic % 2
            if prev_qn_store[par]:
                p.wait(SC, "s_q1a" if par == 0 else "s_q1b",
                       prev_qn_store[par])
            ev = p.op(SC, lambda e, ic=ic, psb=psb: e.activation(
                h["qnb3"][:, ic * TT:(ic + 1) * TT], psb[:, 0:TT], AF.Copy),
                inc="m_scalar")
            evq.append(ev)
            p.wait(GP, "m_scalar", ev)
            p.op(GP, lambda e, ic=ic, tt=tt: e.dma_start(
                out=q1[ic * 128:(ic + 1) * 128, tt * TT:(tt + 1) * TT],
                in_=h["qnb3"][:, ic * TT:(ic + 1) * TT]),
                inc="s_q1a" if par == 0 else "s_q1b", k=16)
            prev_qn_store[par] = p.cnt["s_q1a" if par == 0 else "s_q1b"]
            sqt = p.op(SC, lambda e, ic=ic, sq=h["sq"]: e.activation(
                sq[:, 0:TT], h["qnb3"][:, ic * TT:(ic + 1) * TT],
                AF.Square), inc="m_scalar")
            evq.append(sqt)
            p.wait(TE, "m_scalar", sqt)
            hm = p.op(TE, lambda e, sq=h["sq"]: e.matmul(
                psRS[0:2, 0:TT], h["hmask"][:, 0:2], sq[:, 0:TT],
                start=True, stop=True), inc="m_tensor")
            p.wait(V, "m_tensor", hm)
            p.op(V, lambda e: e.tensor_reduce(h["tmp2"][0:2, 0:1],
                                              psRS[0:2, 0:TT], AX.X, ALU.max))
            p.fence(V)
            p.op(V, lambda e, ic=ic, seq=seq: e.tensor_max(
                h["ntabq3"][0:2, (seq * NCH + ic):(seq * NCH + ic) + 1],
                h["ntabq3"][0:2, (seq * NCH + ic):(seq * NCH + ic) + 1],
                h["tmp2"][0:2, 0:1]))
            vred = p.tok(V)
            p.wait(V, "m_vector", vred)
            p.wait(TE, "m_vector", vred)
        for kc in range(2):
            psb = psP0 if kc % 2 == 0 else psP1
            if len(evq) >= 2:
                p.wait(TE, "m_scalar", evq[-2])
            for jc in range(NCH):
                p.op(TE, lambda e, kc=kc, jc=jc, psb=psb: e.matmul(
                    psb[:, 0:TT],
                    h["wdkvT_b"][:, jc * QL + kc * 128: jc * QL + (kc + 1) * 128],
                    h["xnb"][:, jc * TT:(jc + 1) * TT],
                    start=(jc == 0), stop=(jc == NCH - 1)),
                    inc=("m_tensor" if jc == NCH - 1 else None))
            mt = p.cnt["m_tensor"]
            p.wait(SC, "m_tensor", mt)
            ev = p.op(SC, lambda e, kc=kc, tt=tt, psb=psb: e.activation(
                h["ckvb"][:, kc * T + tt * TT: kc * T + (tt + 1) * TT],
                psb[:, 0:TT], AF.Copy), inc="m_scalar")
            evq.append(ev)
            p.wait(V, "m_scalar", ev)
            sqv = p.op(V, lambda e, kc=kc, tt=tt: e.tensor_mul(
                h["sqk"][:, 0:TT],
                h["ckvb"][:, kc * T + tt * TT: kc * T + (tt + 1) * TT],
                h["ckvb"][:, kc * T + tt * TT: kc * T + (tt + 1) * TT]),
                inc="m_vector")
            p.wait(TE, "m_vector", sqv)
            ck = p.op(TE, lambda e, kc=kc: e.matmul(
                psC[0:1, 0:TT], h["ones_col"][:, 0:1], h["sqk"][:, 0:TT],
                start=(kc == 0), stop=(kc == 1)),
                inc=("m_tensor" if kc == 1 else None))
        cknorm = p.cnt["m_tensor"]
        p.wait(V, "m_tensor", cknorm)
        p.op(V, lambda e: e.tensor_reduce(h["tmp2"][0:1, 1:2], psC[0:1, 0:TT],
                                          AX.X, ALU.max))
        p.fence(V)
        p.op(V, lambda e, seq=seq: e.tensor_max(
            h["scal"][0:1, 20 + seq:21 + seq], h["scal"][0:1, 20 + seq:21 + seq],
            h["tmp2"][0:1, 1:2]))
        p.fence(V)
        ck_ev = evq[-1]
        p.wait(TE, "m_scalar", ck_ev)
        for sub in range(4):
            ti = tt * 4 + sub
            for kc in range(2):
                tp = p.op(TE, lambda e, kc=kc, tt=tt, sub=sub: e.transpose(
                    psTb[:, 0:128],
                    h["ckvb"][:, kc * T + tt * TT + sub * 128:
                              kc * T + tt * TT + (sub + 1) * 128],
                    h["identb"][:]), inc="m_tensor")
                p.wait(SC, "m_tensor", tp)
                cp = p.op(SC, lambda e, ti=ti, kc=kc: e.activation(
                    h["ckvtm"][:, ti * 257 + kc * 128: ti * 257 + (kc + 1) * 128],
                    psTb[:, 0:128], AF.Copy), inc="m_scalar")
                p.wait(TE, "m_scalar", cp)
        cons_toks[tt] = p.cnt["m_tensor"]
        prev_qn_tok = 1

    for seq in range(NSEQ):
        p.op(V, lambda e, seq=seq: e.tensor_mul(
            h["scal"][0:1, 24 + seq:25 + seq], h["scal"][0:1, 0:1],
            h["scal"][0:1, 20 + seq:21 + seq]))
    sct0 = p.tok(V)
    p.wait(TE, "m_vector", sct0)
    for seq in range(NSEQ):
        if seq > 0:
            p.wait(TE, "m_vector", p.cnt["m_vector"])
        tk = p.op(TE, lambda e, seq=seq: e.matmul(
            psW[0:2, 0:1], h["ones_row"][0:1, 0:2],
            h["scal"][0:1, 24 + seq:25 + seq], start=True, stop=True),
            inc="m_tensor")
        p.wait(V, "m_tensor", tk)
        p.op(V, lambda e: e.tensor_copy(h["cs2"][0:2, 0:1], psW[0:2, 0:1]))
        p.fence(V)
        p.op(V, lambda e, seq=seq: e.tensor_scalar(
            h["shift3"][0:2, seq * NCH:(seq + 1) * NCH],
            h["ntabq3"][0:2, seq * NCH:(seq + 1) * NCH],
            h["cs2"][0:2, 0:1], None, ALU.mult))
        p.fence(V)
    sht = p.tok(V)
    p.wait(SC, "m_vector", sht)
    p.op(SC, lambda e: e.activation(h["shift3"][0:2, 0:16], h["shift3"][0:2, 0:16],
                                    AF.Ln, bias=h["cst"][0:2, 3:4]))
    p.fence(SC)
    sh2 = p.op(SC, lambda e: e.activation(h["shift3"][0:2, 0:16],
                                          h["shift3"][0:2, 0:16], AF.Exp,
                                          scale=0.5), inc="m_scalar")
    p.wait(V, "m_scalar", sh2)
    p.op(V, lambda e: e.tensor_scalar(h["shift3"][0:2, 0:16],
                                      h["shift3"][0:2, 0:16],
                                      0.125, -SM_SLACK, ALU.mult, ALU.add))
    p.fence(V)
    p.op(V, lambda e: e.tensor_scalar(h["shift3"][0:2, 0:16],
                                      h["shift3"][0:2, 0:16], 0.0, None, ALU.max))
    p.fence(V)
    shneg = p.op(V, lambda e: e.tensor_scalar(h["shift3"][0:2, 0:16],
                                              h["shift3"][0:2, 0:16], -1.0, None,
                                              ALU.mult), inc="m_vector")
    p.wait(GP, "m_vector", shneg)
    p.op(GP, lambda e: e.dma_start(out=shd3, in_=h["shift3"][0:2, 0:16]),
         inc="st2", k=16)
    p.barrier()
    for sem in ("st2", "s_q1a", "s_q1b", "s_xn"):
        p.wait(SY, sem, p.cnt[sem])

    lasts1 = {0: 0, 1: 0}
    sc3_last = {0: 0, 1: 0, 2: 0}
    sc3_lastv = {0: 0, 1: 0, 2: 0}
    et3_last = {0: 0, 1: 0, 2: 0}
    mk_last = [0]
    rec_prev = [0]
    s1v_prev = [0]
    for s in range(NSEQ):
        for hd in range(NH):
            ic, halfp = hd // 2, hd % 2
            if s + hd > 0:
                p.wait(SY, "m_tensor", p.cnt["m_tensor"])
            p.op(SY, lambda e, s=s, hd=hd: e.dma_start(
                out=h["q1h"][:], in_=q1[hd * 64:(hd + 1) * 64, s * S:(s + 1) * S]),
                inc="s_att", k=16)
            p.op(SY, lambda e, halfp=halfp, s=s, ic=ic: e.dma_start(
                out=h["shcol3"][:, 0:1],
                in_=shd3[halfp:halfp + 1, s * NCH + ic: s * NCH + ic + 1]
                .broadcast_to([128, 1])), inc="s_att", k=16)
            hld = p.cnt["s_att"]
            p.wait(TE, "s_att", hld)
            if sc3_last[2]:
                p.wait(TE, "m_scalar", sc3_last[2])
                p.wait(TE, "m_vector", sc3_lastv[2])
            kevs = []
            for sc4 in range(NTC):
                for lc in range(2):
                    p.op(TE, lambda e, sc4=sc4, lc=lc, hd=hd, s=s: e.matmul(
                        psW[0:64, 0:TT],
                        h["keT"][:, lc * D + hd * 64: lc * D + (hd + 1) * 64],
                        h["ckvb"][:, lc * T + s * S + sc4 * TT:
                                  lc * T + s * S + (sc4 + 1) * TT],
                        start=(lc == 0), stop=(lc == 1)),
                        inc=("m_tensor" if lc == 1 else None))
                mt = p.cnt["m_tensor"]
                p.wait(SC, "m_tensor", mt)
                kev = p.op(SC, lambda e, sc4=sc4: e.activation(
                    h["KCb"][0:64, sc4 * TT:(sc4 + 1) * TT], psW[0:64, 0:TT],
                    AF.Copy), inc="m_scalar")
                kevs.append(kev)
                p.wait(TE, "m_scalar", kev)
            p.wait(SC, "s_att", hld)
            for tc in range(NTC):
                n_st = tc * 4 + 4
                ext_toks = {}
                ctx_toks = {}
                sc3 = [psP0, psP1, psW]
                et3 = [h["e3t0"], h["e3t1"], h["e3t2"]]

                def emit_ctx(st):
                    et = et3[st % 3]
                    ti = s * 16 + st
                    p.wait(TE, "m_scalar", ext_toks[st])
                    if st == 0 and rec_prev[0]:
                        p.wait(TE, "m_vector", rec_prev[0])
                    for lc in range(2):
                        p.op(TE, lambda e, st=st, lc=lc, et=et, ti=ti,
                             n_st=n_st: e.matmul(
                            psA[:, 0:TT] if lc == 0 else psC[:, 0:TT],
                            h["ckvtm"][:, ti * 257 + lc * 128:
                                       ti * 257 + (lc + 1) * 128],
                            et[:], start=(st == 0), stop=(st == n_st - 1)),
                            inc=None)
                    ct = p.op(TE, lambda e, st=st, et=et, ti=ti, n_st=n_st:
                              e.matmul(
                        psRS[0:1, 0:TT],
                        h["ckvtm"][:, ti * 257 + 256: ti * 257 + 257],
                        et[:], start=(st == 0), stop=(st == n_st - 1)),
                        inc="m_tensor")
                    ctx_toks[st] = ct

                for st in range(n_st):
                    pss = sc3[st % 3]
                    et = et3[st % 3]
                    if sc3_last[st % 3]:
                        p.wait(TE, "m_scalar", sc3_last[st % 3])
                        p.wait(TE, "m_vector", sc3_lastv[st % 3])
                    smt = p.op(TE, lambda e, st=st, tc=tc, pss=pss: e.matmul(
                        pss[:, 0:TT], h["KCb"][0:64, st * 128:(st + 1) * 128],
                        h["q1h"][:, tc * TT:(tc + 1) * TT],
                        start=True, stop=True), inc="m_tensor")
                    partial = st >= tc * 4
                    if partial:
                        mi = st - tc * 4
                        p.wait(V, "m_tensor", smt)
                        if mk_last[0]:
                            p.wait(V, "m_scalar", mk_last[0])
                        mk = p.op(V, lambda e, pss=pss, mi=mi: e.tensor_add(
                            h["msk"][:, 0:TT], pss[:, 0:TT],
                            h["maskt"][:, mi * TT:(mi + 1) * TT]),
                            inc="m_vector")
                        sc3_lastv[st % 3] = mk
                        p.wait(SC, "m_vector", mk)
                        if et3_last[st % 3]:
                            p.wait(SC, "m_tensor", et3_last[st % 3])
                        ext = p.op(SC, lambda e, et=et: e.activation(
                            et[:], h["msk"][:, 0:TT], AF.Exp,
                            bias=h["shcol3"][:, 0:1], scale=0.125),
                            inc="m_scalar")
                        mk_last[0] = ext
                    else:
                        p.wait(SC, "m_tensor", smt)
                        if et3_last[st % 3]:
                            p.wait(SC, "m_tensor", et3_last[st % 3])
                        ext = p.op(SC, lambda e, pss=pss, et=et: e.activation(
                            et[:], pss[:, 0:TT], AF.Exp,
                            bias=h["shcol3"][:, 0:1], scale=0.125),
                            inc="m_scalar")
                    ext_toks[st] = ext
                    sc3_last[st % 3] = ext
                    if st >= 2:
                        emit_ctx(st - 2)
                        et3_last[(st - 2) % 3] = ctx_toks[st - 2]
                if n_st >= 2:
                    emit_ctx(n_st - 2)
                    et3_last[(n_st - 2) % 3] = ctx_toks[n_st - 2]
                emit_ctx(n_st - 1)
                et3_last[(n_st - 1) % 3] = ctx_toks[n_st - 1]
                ctx_done = ctx_toks[n_st - 1]
                p.wait(SC, "m_tensor", ctx_done)
                cv0 = p.op(SC, lambda e: e.activation(
                    h["ctxb"][:, 0:TT], psA[:, 0:TT], AF.Copy), inc="m_scalar")
                cv1 = p.op(SC, lambda e: e.activation(
                    h["ctxb"][:, TT:2 * TT], psC[:, 0:TT], AF.Copy),
                    inc="m_scalar")
                p.wait(V, "m_tensor", ctx_done)
                p.op(V, lambda e: e.reciprocal(h["rrow"][0:1, 0:TT],
                                               psRS[0:1, 0:TT]))
                rec_tok = p.tok(V)
                rec_prev[0] = rec_tok
                p.wait(TE, "m_vector", rec_tok)
                p.wait(TE, "m_scalar", cv1)
                bt = p.op(TE, lambda e: e.matmul(
                    psB[0:64, 0:TT], h["ones_row"][0:1, 0:64],
                    h["rrow"][0:1, 0:TT], start=True, stop=True),
                    inc="m_tensor")
                p.wait(SC, "m_tensor", bt)
                if s1v_prev[0]:
                    p.wait(SC, "m_vector", s1v_prev[0])
                rb = p.op(SC, lambda e: e.activation(
                    h["rbcs"][0:64, 0:TT], psB[0:64, 0:TT], AF.Copy),
                    inc="m_scalar")
                p.wait(TE, "m_scalar", rb)
                if s1v_prev[0]:
                    p.wait(TE, "m_vector", s1v_prev[0])
                for lc in range(2):
                    p.op(TE, lambda e, lc=lc, hd=hd: e.matmul(
                        psB[0:64, TT:2 * TT],
                        h["Mb"][:, lc * D + hd * 64: lc * D + (hd + 1) * 64],
                        h["ctxb"][:, lc * TT:(lc + 1) * TT],
                        start=(lc == 0), stop=(lc == 1)),
                        inc=("m_tensor" if lc == 1 else None))
                yt = p.cnt["m_tensor"]
                p.wait(V, "m_tensor", yt)
                par = tc % 2
                off = 0 if par == 0 else TT
                if lasts1[par]:
                    p.wait(V, "s_s1a" if par == 0 else "s_s1b", lasts1[par])
                s1v = p.op(V, lambda e, off=off: e.tensor_mul(
                    h["s1t"][0:64, off:off + TT], psB[0:64, TT:2 * TT],
                    h["rbcs"][0:64, 0:TT]), inc="m_vector")
                s1v_prev[0] = s1v
                p.wait(GP, "m_vector", s1v)
                p.op(GP, lambda e, hd=hd, s=s, tc=tc, off=off: e.dma_start(
                    out=h["s1"][hd * 64:(hd + 1) * 64,
                                s * S + tc * TT: s * S + (tc + 1) * TT],
                    in_=h["s1t"][0:64, off:off + TT]),
                    inc="s_s1a" if par == 0 else "s_s1b", k=16)
                lasts1[par] = p.cnt["s_s1a" if par == 0 else "s_s1b"]


def build_g3():
    g = G("g3")
    emit_g3(g)
    return g



def emit_g4(g):
    p, h = g.p, g.h
    ps = common_setup(g)
    for sname in ("s_col", "s_r0", "s_r1", "s_o0", "s_o1", "s_wf"):
        p.new_sem(sname)
    g.dio_in("x1", [D, T])
    g.dio_in("xn1", [D, T])
    g.dio_in("s1", [D, T])
    g.din("w8T", [D, D])
    g.din("b1", [128, NCH])
    g.din("in_s", [1, 1])
    g.din("w_s", [1, 1])
    g.dout("out", [D, T])
    if "scald" not in g.h:
        g.dint("scald", [1, 64])

    g.sb("w8b", [128, NCH * D], BF16)
    g.sb("b1c", [128, NCH])
    g.sb("x8b", [128, NCH * TT], BF16)
    g.sb("w1t", [128, TT]); g.sb("d_t", [128, TT]); g.sb("hi_t", [128, TT])
    g.sb("ts_t", [128, TT]); g.sb("aw_t", [128, TT]); g.sb("mk_t", [128, TT])
    g.sb("df8", [128, TT])
    g.sb("dfb", [128, 2 * TT]); g.sb("efb4", [128, 2 * TT]); g.sb("gfb4", [128, 2 * TT])
    g.sb("xr", [128, 2 * TT]); g.sb("sr", [128, 2 * TT]); g.sb("orr", [128, 2 * TT])

    psP0, psP1, psW = ps[0], ps[1], ps[5]

    p.op(V, lambda e: e.memset(h["scal"][:], 0.0))
    p.op(V, lambda e: e.memset(h["cols"][:], 0.0))
    p.op(V, lambda e: e.memset(h["colsr"][:], 0.0))
    cst0 = p.tok(V)
    p.wait(TE, "m_vector", cst0)
    p.wait(SC, "m_vector", cst0)
    p.wait(SY, "m_vector", cst0)
    batch_load(g, "s_prm", [(h["b1c"][:], h["b1"]),
                            (h["scal"][0:1, 0:1], h["in_s"]),
                            (h["scal"][0:1, 1:2], h["w_s"])], (V,))
    p.op(V, lambda e: e.reciprocal(h["scal"][0:1, 2:3], h["scal"][0:1, 0:1]))
    p.fence(V)
    p.op(V, lambda e: e.tensor_mul(h["scal"][0:1, 3:4], h["scal"][0:1, 0:1],
                                   h["scal"][0:1, 1:2]))
    p.fence(V)
    ctok = scal_to_cols(g, [(2, 0, 1.0), (3, 1, 1.0)])
    p.wait(SC, "m_vector", ctok)
    load_wT_bf16(g, h["w8T"], h["w8b"], NCH, D)
    wtok = p.cnt["m_vector"]
    p.wait(TE, "m_vector", wtok)

    C2_9 = float(2.0 ** -9)
    lastq = 0
    evs = []
    lastres = {0: 0, 1: 0}
    lastst = {0: 0, 1: 0}
    for tt in range(NTT):
        if lastq:
            p.wait(SY, "m_vector", lastq)
        batch_load(g, "s_xa",
                   [(h["xa0"][:, c * TT:(c + 1) * TT],
                     h["xn1"][c * 128:(c + 1) * 128, tt * TT:(tt + 1) * TT])
                    for c in range(NCH)], (V,))
        p.wait(V, "m_tensor", p.cnt["m_tensor"])
        for c in range(NCH):
            p.op(V, lambda e, c=c: e.tensor_scalar(
                h["w1t"][:, 0:TT], h["xa0"][:, c * TT:(c + 1) * TT],
                h["cols"][:, 0:1], 448.0, ALU.mult, ALU.min))
            p.fence(V)
            p.op(V, lambda e: e.tensor_scalar(
                h["w1t"][:, 0:TT], h["w1t"][:, 0:TT], -448.0, None, ALU.max))
            p.fence(V)
            p.op(V, lambda e: e.scalar_tensor_tensor(
                h["d_t"][:, 0:TT], h["w1t"][:, 0:TT], C1V, h["w1t"][:, 0:TT],
                ALU.mult, ALU.subtract))
            p.fence(V)
            p.op(V, lambda e: e.scalar_tensor_tensor(
                h["hi_t"][:, 0:TT], h["w1t"][:, 0:TT], C1V, h["d_t"][:, 0:TT],
                ALU.mult, ALU.subtract))
            p.fence(V)
            p.op(V, lambda e: e.tensor_scalar(
                h["ts_t"][:, 0:TT], h["w1t"][:, 0:TT], 512.0, MAGIC,
                ALU.mult, ALU.add))
            p.fence(V)
            p.op(V, lambda e: e.tensor_scalar(
                h["ts_t"][:, 0:TT], h["ts_t"][:, 0:TT], MAGIC, C2_9,
                ALU.subtract, ALU.mult))
            p.fence(V)
            p.op(V, lambda e: e.scalar_tensor_tensor(
                h["aw_t"][:, 0:TT], h["w1t"][:, 0:TT], -1.0, h["w1t"][:, 0:TT],
                ALU.mult, ALU.max))
            p.fence(V)
            p.op(V, lambda e: e.tensor_single_scalar(
                h["mk_t"][:, 0:TT], h["aw_t"][:, 0:TT], 0.015625, ALU.is_lt))
            p.fence(V)
            p.op(V, lambda e: e.tensor_sub(h["df8"][:, 0:TT], h["ts_t"][:, 0:TT],
                                           h["hi_t"][:, 0:TT]))
            p.fence(V)
            p.op(V, lambda e: e.tensor_mul(h["df8"][:, 0:TT], h["mk_t"][:, 0:TT],
                                           h["df8"][:, 0:TT]))
            p.fence(V)
            p.op(V, lambda e, c=c: e.tensor_add(
                h["x8b"][:, c * TT:(c + 1) * TT], h["hi_t"][:, 0:TT],
                h["df8"][:, 0:TT]))
            p.fence(V)
        lastq = p.cnt["m_vector"]
        x8tok = lastq
        p.wait(TE, "m_vector", x8tok)
        for ic in range(NCH):
            par = ic % 2
            off = 0 if par == 0 else TT
            psb = psP0 if par == 0 else psP1
            osem = "s_o0" if par == 0 else "s_o1"
            rsem = "s_r0" if par == 0 else "s_r1"
            if len(evs) >= 2:
                p.wait(TE, "m_scalar", evs[-2])
            for jc in range(NCH):
                p.op(TE, lambda e, ic=ic, jc=jc, psb=psb: e.matmul(
                    psb[:, 0:TT],
                    h["w8b"][:, jc * D + ic * 128: jc * D + (ic + 1) * 128],
                    h["x8b"][:, jc * TT:(jc + 1) * TT],
                    start=(jc == 0), stop=(jc == NCH - 1)),
                    inc=("m_tensor" if jc == NCH - 1 else None))
            mt = p.cnt["m_tensor"]
            p.wait(SC, "m_tensor", mt)
            if lastst[par]:
                p.wait(SC, osem, lastst[par])
            ev2 = p.op(SC, lambda e, ic=ic, psb=psb, off=off: e.activation(
                h["gfb4"][:, off:off + TT], psb[:, 0:TT], AF.Silu,
                bias=h["b1c"][:, ic:ic + 1], scale=h["cols"][:, 1:2]),
                inc="m_scalar")
            evs.append(ev2)
            if lastres[par]:
                p.wait(SY, "m_vector", lastres[par])
            batch_load(g, rsem,
                       [(h["xr"][:, off:off + TT],
                         h["x1"][ic * 128:(ic + 1) * 128, tt * TT:(tt + 1) * TT]),
                        (h["sr"][:, off:off + TT],
                         h["s1"][ic * 128:(ic + 1) * 128, tt * TT:(tt + 1) * TT])],
                       ())
            rld = p.cnt[rsem]
            p.wait(V, "m_scalar", ev2)
            p.wait(V, rsem, rld)
            p.op(V, lambda e, off=off: e.tensor_mul(
                h["orr"][:, off:off + TT], h["sr"][:, off:off + TT],
                h["gfb4"][:, off:off + TT]))
            p.fence(V)
            p.op(V, lambda e, off=off: e.tensor_add(
                h["orr"][:, off:off + TT], h["xr"][:, off:off + TT],
                h["orr"][:, off:off + TT]))
            otok = p.tok(V)
            lastres[par] = otok
            p.wait(GP, "m_vector", otok)
            p.op(GP, lambda e, ic=ic, tt=tt, off=off: e.dma_start(
                out=h["out"][ic * 128:(ic + 1) * 128, tt * TT:(tt + 1) * TT],
                in_=h["orr"][:, off:off + TT]), inc=osem, k=16)
            lastst[par] = p.cnt[osem]


def build_g4():
    g = G("g4")
    emit_g4(g)
    return g


def build_fused(rg=None):
    g = G("fused", fused=True, ar32=36960)
    emit_g1(g, bg_factory=emit_wquant_chunks)
    g.p.megabarrier()
    g.pool_reset()
    emit_g2(g, rg)
    g.p.megabarrier()
    g.pool_reset()
    emit_g3(g)
    g.p.megabarrier()
    g.pool_reset()
    emit_g4(g)
    return g


def _masks():
    import ml_dtypes
    i = np.arange(128)[:, None]
    j = np.arange(TT)[None, :]
    out = []
    for delta in (0, -128, -256, -384):
        out.append(np.where(i - j > delta, np.float32(-1e30), np.float32(0.0)))
    return np.ascontiguousarray(
        np.stack(out).reshape(4 * 128, TT).astype(ml_dtypes.bfloat16))



def _prep_host(inputs):
    t = lambda a: np.ascontiguousarray(np.asarray(a, np.float32).T)
    col = lambda b: np.ascontiguousarray(
        np.asarray(b, np.float32).reshape(NCH, 128).T)
    sh = {}
    sh["wqT"] = t(inputs["L0_attn_Wq"]); sh["wkT"] = t(inputs["L0_attn_Wk"])
    sh["wvT"] = t(inputs["L0_attn_Wv"]); sh["woT"] = t(inputs["L0_attn_Wo"])
    sh["bq"] = col(inputs["L0_attn_bq"]); sh["bk"] = col(inputs["L0_attn_bk"])
    sh["bv"] = col(inputs["L0_attn_bv"]); sh["bo"] = col(inputs["L0_attn_bo"])
    sh["gq"] = col(inputs["L0_attn_norm_q_w"])
    sh["gk"] = col(inputs["L0_attn_norm_k_w"])
    sh["identf32"] = np.eye(128, dtype=np.float32)
    sh["wg"] = np.asarray(inputs["L0_ffn_w_gate"], np.float32)
    sh["wu"] = np.asarray(inputs["L0_ffn_w_up"], np.float32)
    sh["wd"] = np.asarray(inputs["L0_ffn_w_down"], np.float32)
    sh["wgT"] = t(inputs["L0_ffn_w_gate"])
    sh["wuT"] = t(inputs["L0_ffn_w_up"])
    sh["wdT"] = t(inputs["L0_ffn_w_down"])
    sh["wdqT"] = t(inputs["L1_attn_W_dq"])
    sh["wuqT"] = t(inputs["L1_attn_W_uq"])
    sh["wdq_nat"] = np.asarray(inputs["L1_attn_W_dq"], np.float32)
    sh["wuk_nat"] = np.asarray(inputs["L1_attn_W_uk"], np.float32)
    sh["wuv_nat"] = np.asarray(inputs["L1_attn_W_uv"], np.float32)
    sh["woT1"] = t(inputs["L1_attn_W_o"])
    sh["wdkvT"] = t(inputs["L1_attn_W_dkv"])
    sh["masks"] = _masks()
    sh["w8T"] = t(inputs["L1_ffn_weight_fp8"])
    sh["b1"] = col(inputs["L1_ffn_bias"])
    sh["in_s"] = np.asarray(inputs["L1_ffn_input_scale"],
                            np.float32).reshape(1, 1)
    sh["w_s"] = np.asarray(inputs["L1_ffn_weight_scale"],
                           np.float32).reshape(1, 1)
    return sh


def kernel(**inputs):
    x = np.asarray(inputs["L0_attn_x"], np.float32)
    sh = _prep_host(inputs)
    xc = [np.ascontiguousarray(
        x[2 * c:2 * c + 2].reshape(T, D).T) for c in range(NCORES)]
    cores = list(range(NCORES))
    nc = build_fused().finish()
    in_maps = [{"x": xc[c], **sh} for c in cores]
    res = run_bass_kernel_spmd(nc, in_maps, core_ids=cores).results
    out = np.empty((2 * NCORES, S, D), np.float32)
    for c in cores:
        out[2 * c:2 * c + 2] = res[c]["out"].T.reshape(2, S, D)
    return out


def kernel_multi(**inputs):
    x = np.asarray(inputs["L0_attn_x"], np.float32)
    sh = _prep_host(inputs)
    xc = [np.ascontiguousarray(
        x[2 * c:2 * c + 2].reshape(T, D).T) for c in range(NCORES)]
    cores = list(range(NCORES))

    nc1 = build_g1().finish()
    m1 = [{"x": xc[c], **{k: sh[k] for k in
           ("wqT", "wkT", "wvT", "woT", "bq", "bk", "bv", "bo", "gq", "gk",
            "identf32")}} for c in cores]
    r1 = run_bass_kernel_spmd(nc1, m1, core_ids=cores).results

    nc2 = build_g2().finish()
    m2 = [{"x": xc[c], "xn0": r1[c]["xn0"], "s0": r1[c]["s0"],
           **{k: sh[k] for k in ("wg", "wu", "wd", "wgT", "wuT", "wdT",
                                 "identf32")}} for c in cores]
    r2 = run_bass_kernel_spmd(nc2, m2, core_ids=cores).results

    nc3 = build_g3().finish()
    m3 = [{"x1": r2[c]["x1"],
           **{k: sh[k] for k in ("wdqT", "wuqT", "wdq_nat", "wuk_nat",
                                 "wuv_nat", "woT1", "wdkvT", "masks",
                                 "identf32")}} for c in cores]
    r3 = run_bass_kernel_spmd(nc3, m3, core_ids=cores).results

    nc4 = build_g4().finish()
    m4 = [{"x1": r2[c]["x1"], "xn1": r3[c]["xn1"], "s1": r3[c]["s1"],
           **{k: sh[k] for k in ("w8T", "b1", "in_s", "w_s")}} for c in cores]
    r4 = run_bass_kernel_spmd(nc4, m4, core_ids=cores).results

    out = np.empty((2 * NCORES, S, D), np.float32)
    for c in cores:
        out[2 * c:2 * c + 2] = r4[c]["out"].T.reshape(2, S, D)
    return out
